# version 07 (final)
# speedup vs baseline: 1.1544x; 1.0402x over previous
import os
import numpy as np
from contextlib import ExitStack
import concourse.bass as bass
import concourse.mybir as mybir
from concourse.bass_utils import run_bass_kernel_spmd

F32 = mybir.dt.float32
BF16 = mybir.dt.bfloat16
I32 = mybir.dt.int32
AF = mybir.ActivationFunctionType
ALU = mybir.AluOpType
AX = mybir.AxisListType

D = 1024
T = 2048
DI = 2048
DS = 16
DTR = 64
NH = 16
HD = 64
TS = 256
NSL = T // TS
NTT = T // 128

ENGS = ("sync", "tensor", "vector", "scalar", "gpsimd")
POOLN = [4]


class Builder:
    def __init__(self, nc, es):
        self.nc = nc
        self.prog = {e: [] for e in ENGS}
        self.semh = {}
        self.cnt = {}
        for s in ("bar", "T", "V", "S", "G", "WD", "XD", "ST",
                  "D0", "D1", "D2", "D3", "D4", "D5", "D6", "D7"):
            self.semh[s] = es.enter_context(nc.semaphore(s))
            self.cnt[s] = 0

    ESEM = {"tensor": "T", "vector": "V", "scalar": "S", "gpsimd": "G"}

    def emit(self, eng, fn, inc=None, n=1):
        esem = self.ESEM.get(eng)
        chain = esem is not None and inc in (None, esem)
        if chain:
            prev = self.cnt[esem]
            eh = self.semh[esem]
            if prev > 0:
                self.prog[eng].append(
                    lambda e, h=eh, v=prev: e.wait_ge(h, v))
            self.cnt[esem] += 1
            tgt = self.cnt[esem]

            def run(e, fn=fn, h=eh):
                fn(e).then_inc(h, 1)
            self.prog[eng].append(run)
            return tgt
        h = self.semh.get(inc)
        if inc is not None:
            self.cnt[inc] += n
        tgt = self.cnt.get(inc)

        def run(e, fn=fn, h=h, n=n, inc=inc):
            r = fn(e)
            if inc is not None:
                r.then_inc(h, n)
        self.prog[eng].append(run)
        return tgt

    def wait(self, eng, sem, val):
        if val <= 0:
            return
        h = self.semh[sem]
        self.prog[eng].append(lambda e, h=h, v=val: e.wait_ge(h, v))

    def barrier(self):
        self.cnt["bar"] += 5
        tgt = self.cnt["bar"]
        h = self.semh["bar"]
        for eng in ENGS:
            es_ = self.ESEM.get(eng)
            if es_ is not None and self.cnt[es_] > 0:
                eh, ev = self.semh[es_], self.cnt[es_]
                self.prog[eng].append(lambda e, h2=eh, v=ev: e.wait_ge(h2, v))
            self.prog[eng].append(lambda e, h=h: e.sem_inc(h, 1))
            self.prog[eng].append(lambda e, h=h, v=tgt: e.wait_ge(h, v))


class ColArena:
    def __init__(self, tens, cols, elem=2):
        self.t = tens
        self.cols = cols
        self.off = 0
        self.elem = elem

    def alloc(self, cols, pattern=None, **kw):
        o = self.off
        self.off += (cols + 15) // 16 * 16
        assert self.off <= self.cols, f"arena overflow {self.off}>{self.cols}"
        v = self.t[:, o:o + cols]
        if pattern:
            v = v.rearrange(pattern, **kw)
        return v

    def alloc_f32(self, cols, pattern=None, **kw):
        o = self.off
        self.off += (2 * cols + 15) // 16 * 16
        assert self.off <= self.cols, f"arena overflow {self.off}>{self.cols}"
        v = self.t[:, o:o + 2 * cols].bitcast(F32)
        if pattern:
            v = v.rearrange(pattern, **kw)
        return v

    def reset(self):
        self.off = 0


def build_program(active=(0, 1, 2, 3)):
    nc = bass.Bass("TRN2", target_bir_lowering=False)
    es = ExitStack()

    def din(name, shape, dtype=F32):
        return nc.dram_tensor(name, shape, dtype, kind="ExternalInput").ap()

    x_in = din("x", [T, D])
    W = {}
    W["l0_win"] = din("L0_attn_W_in", [D, 2 * DI])
    W["l0_convw"] = din("L0_attn_conv_w", [DI, 1, 4])
    W["l0_convb"] = din("L0_attn_conv_b", [DI])
    W["l0_wx"] = din("L0_attn_W_x", [DI, DTR + 2 * DS])
    W["l0_wdt"] = din("L0_attn_W_dt", [DTR, DI])
    W["l0_bdt"] = din("L0_attn_b_dt", [DI])
    W["l0_alog"] = din("L0_attn_A_log", [DI, DS])
    W["l0_d"] = din("L0_attn_D", [DI])
    W["l0_wout"] = din("L0_attn_W_out", [DI, D])
    W["l0_u1"] = din("L0_ffn_U1", [D, 256])
    W["l0_v1"] = din("L0_ffn_V1", [256, 4096])
    W["l0_b1"] = din("L0_ffn_b1", [4096])
    W["l0_u2"] = din("L0_ffn_U2", [4096, 256])
    W["l0_v2"] = din("L0_ffn_V2", [256, D])
    W["l0_b2"] = din("L0_ffn_b2", [D])
    for nm in ("q_l", "k_l", "v_l", "o_l", "q_g", "k_g", "v_g", "o_g"):
        W["l1_w" + nm] = din("L1_attn_W" + nm, [D, D])
    W["l1_wr"] = din("L1_ffn_Wr", [8, D])
    W["l1_we"] = din("L1_ffn_We", [8, D, D])
    W["l2_wqkv"] = din("L2_attn_w_qkv", [3 * D, D])
    W["l2_wout"] = din("L2_attn_w_out", [D, D])
    W["l2_gw"] = din("L2_ffn_gate_w", [8, D])
    W["l2_w1"] = din("L2_ffn_w1", [8, D, 4096])
    W["l2_w2"] = din("L2_ffn_w2", [8, 4096, D])
    for nm in ("q", "k", "v", "o"):
        W["l3_w" + nm] = din("L3_attn_w" + nm, [D, D])
        W["l3_b" + nm] = din("L3_attn_b" + nm, [D])
    W["l3_wr"] = din("L3_ffn_W_r", [D, D])
    W["l3_wk"] = din("L3_ffn_W_k", [D, D])
    W["l3_wv"] = din("L3_ffn_W_v", [D, D])
    W["l3_mur"] = din("L3_ffn_mu_r", [D])
    W["l3_muk"] = din("L3_ffn_mu_k", [D])
    consts = din("consts", [128, 640])
    consts_i = din("consts_i", [8, 32], I32)
    y_out = nc.dram_tensor("y", [T, D], F32, kind="ExternalOutput").ap()

    sbt = lambda name, shape, dt: es.enter_context(nc.sbuf_tensor(name, shape, dt))
    xres = sbt("xres", [128, NTT, D], F32)
    arena_t = sbt("arena", [128, 40960], BF16)
    stat_t = sbt("stat", [128, 3584], F32)
    cst = sbt("cst", [128, 640], F32)
    cstb = sbt("cstb", [128, 512], BF16)
    wbuf_t = sbt("wbuf", [128, 10240], BF16)
    pers_t = sbt("pers", [128, 8704], BF16)

    pb = [es.enter_context(nc.psum_tensor(f"pb{i}", [128, 512], F32))
          for i in range(5)]
    ptp2 = [es.enter_context(nc.psum_tensor(f"ptp{i}", [128, 512], BF16))
            for i in range(2)]
    pmisc = es.enter_context(nc.psum_tensor("pmisc", [128, 512], F32))

    b = Builder(nc, es)
    AR = ColArena(arena_t, 40960)
    STA = ColArena(stat_t, 3584 * 2)

    class F32Arena:
        def __init__(self):
            self.off = 0

        def alloc(self, cols, pattern=None, **kw):
            o = self.off
            self.off += (cols + 7) // 8 * 8
            assert self.off <= 3584, f"stat overflow {self.off}"
            v = stat_t[:, o:o + cols]
            if pattern:
                v = v.rearrange(pattern, **kw)
            return v

        def reset(self):
            self.off = 0

    STA = F32Arena()
    WB = ColArena(wbuf_t, 10240)
    PSW = ColArena(pers_t, 8704)

    def dma(queue, out_ap, in_ap, sem="WD", slow_ok=False):
        return b.emit(queue, lambda e, o=out_ap, i=in_ap, sk=slow_ok: e.dma_start(
            out=o, in_=i, allow_slow_non_contiguous=sk), inc=sem, n=16)

    ident = cstb[:, 0:128]
    ones_row = cstb[:, 128:256]
    iota_f = cst[:, 0:512]

    nc.const_aps.register(F32, 1.0, cst[:, 513:514])
    nc.const_aps.register(F32, 1e-5, cst[:, 514:515])
    nc.const_aps.register(F32, 0.0, cst[:, 515:516])
    nc.const_aps.register(F32, -1.0, cst[:, 516:517])
    nc.const_aps.register(F32, 1e-12, cst[:, 517:518])
    nc.const_aps.register(F32, 0.125, cst[:, 518:519])
    nc.const_aps.register(F32, -2.0, cst[:, 519:520])
    nc.const_aps.register(F32, -1e30, cst[:, 520:521])
    csti = sbt("csti", [8, 32], I32)
    dma("sync", csti[:, :], consts_i, "XD")
    dma("sync", cst[:, :], consts, "XD")
    dma("sync", xres[:, :, :], x_in.rearrange("(tt p) d -> p tt d", p=128), "XD")
    XD0 = b.cnt["XD"]

    ustrict = cstb[:, 256:384]
    ones_col = cstb[:, 384:385]
    col10 = cstb[:, 385:387]
    col01 = cstb[:, 387:389]
    b.emit("vector", lambda e: e.memset(cstb[:, :], 0.0), inc="V")
    b.emit("vector", lambda e: e.memset(ident, 1.0), inc="V")
    b.emit("vector", lambda e: e.memset(ones_row[0:1, :], 1.0), inc="V")
    b.emit("vector", lambda e: e.memset(ustrict, 1.0), inc="V")
    b.emit("vector", lambda e: e.memset(ones_col, 1.0), inc="V")
    b.emit("vector", lambda e: e.memset(col10[:, 0:1], 1.0), inc="V")
    b.emit("vector", lambda e: e.memset(col01[:, 1:2], 1.0), inc="V")
    vsetup = b.cnt["V"]
    b.wait("gpsimd", "V", vsetup)
    b.emit("gpsimd", lambda e: e.affine_select(
        out=ident, in_=ident, pattern=[[-1, 128]], compare_op=ALU.is_equal,
        fill=0.0, channel_multiplier=1), inc="G")
    b.emit("gpsimd", lambda e: e.affine_select(
        out=ustrict, in_=ustrict, pattern=[[1, 128]], compare_op=ALU.is_gt,
        fill=0.0, channel_multiplier=-1), inc="G")
    gsetup = b.cnt["G"]
    for eng in ("vector", "scalar", "tensor"):
        b.wait(eng, "XD", XD0)
        b.wait(eng, "G", gsetup)

    pbi = [0]
    evac_after_bank = {}

    def mm_group(kparts, out_rows, out_cols, evac_fn, evac_eng,
                 bank=None, start=True, stop=True, inc_last=True):
        if bank is None:
            bi = pbi[0] % POOLN[0]
            pbi[0] += 1
            bank = pb[bi]
            prev = evac_after_bank.get(bi)
            if prev is not None:
                b.wait("tensor", prev[0], prev[1])
        else:
            bi = None
        n = len(kparts)
        for i, (lh, rh) in enumerate(kparts):
            st_ = start and (i == 0)
            sp_ = stop and (i == n - 1)
            do_inc = inc_last and (i == n - 1)
            b.emit("tensor",
                   lambda e, lh=lh, rh=rh, st_=st_, sp_=sp_, p=bank,
                   r=out_rows, c=out_cols: e.matmul(
                       p[0:r, 0:c], lh, rh, start=st_, stop=sp_),
                   inc=("T" if do_inc else None))
        if evac_fn is None:
            return None
        tT = b.cnt["T"]
        b.wait(evac_eng, "T", tT)
        sem = {"vector": "V", "scalar": "S"}[evac_eng]
        tgt = b.emit(evac_eng,
                     lambda e, p=bank, r=out_rows, c=out_cols: evac_fn(
                         e, p[0:r, 0:c]), inc=sem)
        if bi is not None:
            evac_after_bank[bi] = (sem, tgt)
        return tgt

    tp_state = {"slot": 0, "evacs": []}

    def transpose_tiles(pairs, out_op="copy", idn=128):
        last = None
        for (srcp, dstp) in pairs:
            s = tp_state["slot"] % 2
            tp_state["slot"] += 1
            if len(tp_state["evacs"]) >= 2:
                b.wait("tensor", "V", tp_state["evacs"][-2])
            tT = b.emit("tensor", lambda e, s=s, a=srcp, idn=idn: e.transpose(
                ptp2[s][:, 0:idn],
                a, ident[0:idn, 0:idn]), inc="T")
            b.wait("vector", "T", tT)
            if out_op == "copy":
                vv = b.emit("vector", lambda e, s=s, d=dstp, idn=idn:
                            e.tensor_copy(out=d, in_=ptp2[s][:, 0:idn]),
                            inc="V")
            else:
                vv = b.emit("vector", lambda e, s=s, d=dstp, idn=idn:
                            e.tensor_add(out=d, in0=d,
                                         in1=ptp2[s][:, 0:idn]), inc="V")
            tp_state["evacs"].append(vv)
            last = vv
        return last

    def layernorm_stats(sq_scr):
        sums = STA.alloc(NTT)
        sumsq = STA.alloc(NTT)
        mcol = STA.alloc(NTT)
        rstd = STA.alloc(NTT)
        for tt in range(NTT):
            b.emit("vector", lambda e, tt=tt: e.reduce_sum(
                out=sums[:, tt:tt+1], in_=xres[:, tt, :], axis=AX.X), inc="V")
            b.emit("scalar", lambda e, tt=tt: e.activation(
                out=sq_scr, in_=xres[:, tt, :], func=AF.Square,
                accum_out=sumsq[:, tt:tt+1]), inc="S")
        b.wait("vector", "S", b.cnt["S"])
        b.emit("vector", lambda e: e.tensor_scalar(
            out=mcol, in0=sums, scalar1=1.0 / D, scalar2=None, op0=ALU.mult))
        b.emit("vector", lambda e: e.tensor_mul(out=rstd, in0=mcol, in1=mcol))
        b.emit("vector", lambda e: e.tensor_scalar(
            out=sumsq, in0=sumsq, scalar1=1.0 / D, scalar2=None, op0=ALU.mult))
        b.emit("vector", lambda e: e.tensor_sub(out=rstd, in0=sumsq, in1=rstd),
               inc="V")
        b.wait("scalar", "V", b.cnt["V"])
        b.emit("scalar", lambda e: e.activation(out=rstd, in_=rstd, func=AF.Sqrt,
                                                bias=1e-5), inc="S")
        b.wait("vector", "S", b.cnt["S"])
        b.emit("vector", lambda e: e.reciprocal(out=rstd, in_=rstd), inc="V")
        return mcol, rstd

    def ln_apply(mcol, rstd, tt, dst):
        return b.emit("vector", lambda e, tt=tt, d=dst: e.tensor_scalar(
            out=d, in0=xres[:, tt, :], scalar1=mcol[:, tt:tt+1],
            scalar2=rstd[:, tt:tt+1], op0=ALU.subtract, op1=ALU.mult), inc="V")

    stream_sem_pool = ["D0", "D1", "D2", "D3", "D4", "D5", "D6", "D7"]

    def make_stream(buf2):
        sems = (stream_sem_pool.pop(0), stream_sem_pool.pop(0))
        state = {"last": [0, 0]}

        def load(idx, dram_ap, reshape=None, rows=128):
            bufv = buf2[0:rows, idx % 2, :]
            if reshape:
                bufv = bufv.rearrange(reshape[0], **reshape[1])
            b.wait("gpsimd", "T", state["last"][idx % 2])
            return bufv, (sems[idx % 2],
                          dma("gpsimd", bufv, dram_ap, sems[idx % 2]))

        def consumed(idx, ttarget):
            state["last"][idx % 2] = ttarget
        return load, consumed

    def layer0():
        POOLN[0] = 4
        PRO = int(os.environ.get("L0_PRO", "9"))
        AR.reset(); STA.reset(); WB.reset()
        wx = PSW.alloc(16 * 96, "p (k m) -> p k m", k=16)
        wdt = PSW.alloc(16 * 128, "p (m q) -> p m q", m=16)
        u1 = PSW.alloc(8 * 256, "p (k m) -> p k m", k=8)
        v2 = PSW.alloc(2 * 1024, "p (k m) -> p k m", k=2)
        convw = PSW.alloc_f32(16 * 4, "p (k j) -> p k j", k=16)
        convb = PSW.alloc_f32(16)
        bdt = PSW.alloc_f32(16)
        a_sb = PSW.alloc_f32(16 * 16, "p (k s) -> p k s", k=16)
        d_sb = PSW.alloc_f32(16)
        b1c = PSW.alloc_f32(32)
        b2c = PSW.alloc_f32(8)

        if PRO < 1:
            return
        dma("gpsimd", wx, W["l0_wx"].rearrange("(k p) m -> p k m", p=128))
        dma("gpsimd", wdt[0:64, :, :],
            W["l0_wdt"].rearrange("c (m q) -> c m q", q=128))
        dma("gpsimd", u1, W["l0_u1"].rearrange("(k p) m -> p k m", p=128))
        dma("gpsimd", v2, W["l0_v2"].rearrange("(k p) m -> p k m", p=128))
        dma("gpsimd", convw,
            W["l0_convw"].rearrange("(k p) one j -> p (k one) j", p=128))
        dma("gpsimd", convb, W["l0_convb"].rearrange("(k p) -> p k", p=128), slow_ok=True)
        dma("gpsimd", bdt, W["l0_bdt"].rearrange("(k p) -> p k", p=128), slow_ok=True)
        dma("gpsimd", a_sb, W["l0_alog"].rearrange("(k p) s -> p k s", p=128))
        dma("gpsimd", d_sb, W["l0_d"].rearrange("(k p) -> p k", p=128), slow_ok=True)
        dma("gpsimd", b1c, W["l0_b1"].rearrange("(m p) -> p m", p=128), slow_ok=True)
        wWD = dma("gpsimd", b2c, W["l0_b2"].rearrange("(m p) -> p m", p=128), slow_ok=True)

        if PRO < 2:
            return
        b.wait("scalar", "WD", wWD)
        b.emit("scalar", lambda e: e.activation(out=a_sb[:, :, :],
                                                in_=a_sb[:, :, :], func=AF.Exp),
               inc="S")
        b.wait("vector", "S", b.cnt["S"])
        b.emit("vector", lambda e: e.tensor_scalar(
            out=a_sb[:, :, :], in0=a_sb[:, :, :], scalar1=-1.0, scalar2=None,
            op0=ALU.mult), inc="V")

        win_buf = WB.alloc(2 * 2048, "p (b2 m) -> p b2 m", b2=2)
        wout_buf = WB.alloc(2 * 2048, "p (b2 m) -> p b2 m", b2=2)
        v1_buf = WB.alloc(2 * 256, "p (b2 m) -> p b2 m", b2=2)
        u2_buf = WB.alloc(2 * 256, "p (b2 m) -> p b2 m", b2=2)
        win_ld, win_done = make_stream(win_buf)
        wout_ld, wout_done = make_stream(wout_buf)
        v1_ld, v1_done = make_stream(v1_buf)
        u2_ld, u2_done = make_stream(u2_buf)

        xn = AR.alloc(2 * 1024, "p (tt d) -> p tt d", tt=2)
        sq_scr = xn[:, 0, :]
        xnT = AR.alloc(8 * TS, "p (k t) -> p k t", k=8)
        xi = AR.alloc(16 * (TS + 3), "p (c t) -> p c t", c=16)
        gz = AR.alloc(16 * TS, "p (c t) -> p c t", c=16)
        brep = AR.alloc(16 * TS, "p (s t) -> p s t", s=16)
        crep = AR.alloc(TS * 16, "p (t s) -> p t s", t=TS)
        Hbuf = AR.alloc(TS * 16, "p (t s) -> p t s", t=TS)
        ybuf = AR.alloc(16 * TS, "p (c t) -> p c t", c=16)
        m8 = AR.alloc(TS * 8, "p (t s) -> p t s", t=TS)
        sT = AR.alloc(8 * TS, "p (k t) -> p k t", k=8)
        gT = AR.alloc(8 * TS, "p (k t) -> p k t", k=8)
        pT = AR.alloc(2 * TS, "p (k t) -> p k t", k=2)
        qbf = pT
        hT = AR.alloc(TS)
        hstate = AR.alloc(16 * 16, "p (c s) -> p c s", c=16)
        dtx = AR.alloc(TS)
        xp_bf = AR.alloc(TS)
        bcflat = AR.alloc(16 * TS)
        prod = ybuf[:, 0:8, :]
        xp_f = STA.alloc(TS)
        dt_f = STA.alloc(TS)
        da2 = [STA.alloc(TS), STA.alloc(TS)]
        cacc2 = [STA.alloc(TS), STA.alloc(TS)]

        b.emit("gpsimd", lambda e: e.memset(hstate[:, :, :], 0.0))
        b.emit("gpsimd", lambda e: e.memset(xi[:, :, 0:3], 0.0), inc="G")
        b.wait("vector", "G", b.cnt["G"])
        b.wait("tensor", "G", b.cnt["G"])

        if PRO < 3:
            return
        mcol, rstd = layernorm_stats(sq_scr)

        if PRO < 4:
            return
        PH = int(os.environ.get("L0_PHASE", "9"))
        NS2 = int(os.environ.get("L0_NSL", str(NSL)))
        for sl in range(NS2):
            tt0 = sl * TS // 128
            ntt = TS // 128
            for j in range(ntt):
                ln_apply(mcol, rstd, tt0 + j, xn[:, j, :])
            lnV = b.cnt["V"]
            b.wait("tensor", "V", lnV)
            if not os.environ.get("L0_NOTP"):
                transpose_tiles([(xn[:, j, dc*128:(dc+1)*128],
                                  xnT[:, dc, j*128:(j+1)*128])
                                 for j in range(ntt) for dc in range(8)])
            xnT_V = b.cnt["V"]

            if PH < 1:
                continue
            for mt2 in range(16):
                bufv, wtgt = win_ld(mt2, W["l0_win"][:, mt2*256:(mt2+1)*256]
                                    .rearrange("(k p) q -> p k q", p=128),
                                    ("p (k q) -> p k q", dict(k=8)))
                b.wait("tensor", wtgt[0], wtgt[1])
                if mt2 == 0:
                    b.wait("tensor", "V", xnT_V)
                for half2 in range(2):
                    mt = mt2 * 2 + half2
                    if mt < 16:
                        def ev(e, p, ct=mt):
                            return e.activation(out=xi[:, ct, 3:3+TS], in_=p,
                                                func=AF.Copy)
                    else:
                        def ev(e, p, ct=mt-16):
                            return e.activation(out=gz[:, ct, :], in_=p,
                                                func=AF.Silu)
                    mm_group([(bufv[:, kt, half2*128:(half2+1)*128],
                               xnT[:, kt, :]) for kt in range(8)],
                             128, TS, ev, "scalar")
                win_done(mt2, b.cnt["T"])
            xzS = b.cnt["S"]

            if PH < 2:
                continue
            b.wait("vector", "S", xzS)
            silu_tgts = []
            for ct in range(16):
                cacc = cacc2[ct % 2]
                if len(silu_tgts) >= 2:
                    b.wait("vector", "S", silu_tgts[-2])
                b.emit("vector", lambda e, ct=ct, cacc=cacc: e.tensor_scalar(
                    out=cacc, in0=xi[:, ct, 0:TS], scalar1=convw[:, ct, 0:1],
                    scalar2=None, op0=ALU.mult))
                for j in (1, 2, 3):
                    b.emit("vector", lambda e, ct=ct, j=j, cacc=cacc:
                           e.scalar_tensor_tensor(
                               out=cacc, in0=xi[:, ct, j:j+TS],
                               scalar=convw[:, ct, j:j+1], in1=cacc,
                               op0=ALU.mult, op1=ALU.add))
                b.emit("vector", lambda e, ct=ct: e.tensor_copy(
                    out=xi[:, ct, 0:3], in_=xi[:, ct, TS:TS+3]), inc="V")
                b.wait("scalar", "V", b.cnt["V"])
                stg = b.emit("scalar", lambda e, ct=ct, cacc=cacc: e.activation(
                    out=xi[:, ct, 3:3+TS], in_=cacc, func=AF.Silu,
                    bias=convb[:, ct:ct+1]), inc="S")
                silu_tgts.append(stg)
            convS = b.cnt["S"]

            if PH < 3:
                continue
            b.wait("tensor", "S", convS)
            b.wait("vector", "WD", b.cnt["WD"])

            def ev_xp(e, p):
                return e.tensor_copy(out=xp_bf[0:96, :], in_=p)
            xpV = mm_group([(wx[:, kt, :][:, 0:96], xi[:, kt, 3:3+TS])
                            for kt in range(16)], 96, TS, ev_xp, "vector")
            b.wait("gpsimd", "V", xpV)
            b.wait("gpsimd", "T", b.cnt["T"])
            bct = dma("gpsimd",
                      bcflat[0:1, :].rearrange("q (r t) -> q r t", r=16),
                      xp_bf[64:80, :].rearrange("r (q t) -> r q t", q=1), "WD")

            if PH < 4:
                continue
            for mt in range(2):
                def ev_pm(e, p, mt=mt):
                    return e.tensor_copy(out=pT[:, mt, :], in_=p)
                mm_group([(u1[:, kt, mt*128:(mt+1)*128], xnT[:, kt, :])
                          for kt in range(8)], 128, TS, ev_pm, "vector")
            pV = b.cnt["V"]
            for mt in range(32):
                v1b, vtgt = v1_ld(mt, W["l0_v1"][:, mt*128:(mt+1)*128]
                                  .rearrange("(k p) q -> p k q", p=128),
                                  ("p (k q) -> p k q", dict(k=2)))
                u2b, utgt = u2_ld(mt, W["l0_u2"][mt*128:(mt+1)*128, :])
                b.wait("tensor", vtgt[0], vtgt[1])
                b.wait("tensor", utgt[0], utgt[1])
                if mt == 0:
                    b.wait("tensor", "V", pV)

                def ev_h(e, p, mt=mt):
                    return e.activation(out=hT, in_=p, func=AF.Gelu_apprx_tanh,
                                        bias=b1c[:, mt:mt+1])
                mm_group([(v1b[:, kt, :], pT[:, kt, :]) for kt in range(2)],
                         128, TS, ev_h, "scalar")
                v1_done(mt, b.cnt["T"])
                hS = b.cnt["S"]
                b.wait("tensor", "S", hS)
                b.emit("tensor", lambda e, mt=mt, u2b=u2b: e.matmul(
                    pb[4][:, 0:TS], u2b[:, 0:128], hT,
                    start=(mt == 0), stop=(mt == 31)))
                b.emit("tensor", lambda e, mt=mt, u2b=u2b: e.matmul(
                    pmisc[:, 0:TS], u2b[:, 128:256], hT,
                    start=(mt == 0), stop=(mt == 31)),
                    inc=("T" if mt == 31 else None))
                u2_done(mt, b.cnt["T"])
            qT = b.cnt["T"]
            b.wait("vector", "T", qT)
            b.emit("vector", lambda e: e.tensor_copy(out=qbf[:, 0, :],
                                                     in_=pb[4][:, 0:TS]))
            b.emit("vector", lambda e: e.tensor_copy(out=qbf[:, 1, :],
                                                     in_=pmisc[:, 0:TS]), inc="V")
            qV = b.cnt["V"]
            b.wait("tensor", "V", qV)
            for mt in range(8):
                def ev_g(e, p, mt=mt):
                    return e.activation(out=gT[:, mt, :], in_=p, func=AF.Silu,
                                        bias=b2c[:, mt:mt+1])
                mm_group([(v2[:, kt, mt*128:(mt+1)*128], qbf[:, kt, :])
                          for kt in range(2)], 128, TS, ev_g, "scalar")
            gS = b.cnt["S"]

            if PH < 5:
                continue
            b.wait("tensor", "WD", b.cnt["WD"])
            for g2 in range(8):
                bi = pbi[0] % POOLN[0]
                pbi[0] += 1
                bank = pb[bi]
                prev = evac_after_bank.get(bi)
                if prev is not None:
                    b.wait("tensor", prev[0], prev[1])
                b.emit("tensor", lambda e, g2=g2, p=bank: e.matmul(
                    p[:, 0:TS], ones_row[0:1, :], bcflat[0:1, (2*g2)*TS:(2*g2+1)*TS],
                    start=True, stop=True))
                b.emit("tensor", lambda e, g2=g2, p=bank: e.matmul(
                    p[:, TS:2*TS], ones_row[0:1, :],
                    bcflat[0:1, (2*g2+1)*TS:(2*g2+2)*TS],
                    start=True, stop=True), inc="T")
                b.wait("vector", "T", b.cnt["T"])
                tgt = b.emit("vector", lambda e, g2=g2, p=bank: e.tensor_copy(
                    out=brep[:, 2*g2:2*g2+2, :].rearrange("p s t -> p (s t)"),
                    in_=p[:, 0:2*TS]), inc="V")
                evac_after_bank[bi] = ("V", tgt)
            lastBmm = b.cnt["T"]
            b.wait("gpsimd", "T", lastBmm)
            cflat_tgt = dma("gpsimd",
                            bcflat[0:1, :].rearrange("q (r t) -> q r t", r=16),
                            xp_bf[80:96, :].rearrange("r (q t) -> r q t", q=1),
                            "WD")
            b.wait("tensor", "WD", b.cnt["WD"])
            for s in range(16):
                bi = pbi[0] % POOLN[0]
                pbi[0] += 1
                bank = pb[bi]
                prev = evac_after_bank.get(bi)
                if prev is not None:
                    b.wait("tensor", prev[0], prev[1])
                b.emit("tensor", lambda e, s=s, p=bank: e.matmul(
                    p[:, 0:TS], ones_row[0:1, :], bcflat[0:1, s*TS:(s+1)*TS],
                    start=True, stop=True), inc="T")
                b.wait("vector", "T", b.cnt["T"])
                tgt = b.emit("vector", lambda e, s=s, p=bank: e.tensor_copy(
                    out=crep[:, :, s], in_=p[:, 0:TS]), inc="V")
                evac_after_bank[bi] = ("V", tgt)
            repV = b.cnt["V"]
            b.wait("scalar", "V", repV)

            if PH < 6:
                continue
            scan_tgts = []
            dtx_tgt = 0
            for ct in range(16):
                bi = pbi[0] % POOLN[0]
                pbi[0] += 1
                bank = pb[bi]
                prev = evac_after_bank.get(bi)
                if prev is not None:
                    b.wait("tensor", prev[0], prev[1])
                b.emit("tensor", lambda e, ct=ct, p=bank: e.matmul(
                    p[:, 0:TS], wdt[0:64, ct, :], xp_bf[0:64, :],
                    start=True, stop=True), inc="T")
                b.wait("scalar", "T", b.cnt["T"])
                b.wait("scalar", "V", dtx_tgt)
                b.emit("scalar", lambda e, ct=ct, p=bank: e.activation(
                    out=dt_f, in_=p[:, 0:TS], func=AF.Exp, bias=bdt[:, ct:ct+1]))
                stgt = b.emit("scalar", lambda e: e.activation(
                    out=dt_f, in_=dt_f, func=AF.Ln, bias=1.0), inc="S")
                evac_after_bank[bi] = ("S", stgt)
                b.wait("vector", "S", stgt)
                dtx_tgt = b.emit("vector", lambda e, ct=ct: e.tensor_mul(
                    out=dtx, in0=dt_f, in1=xi[:, ct, 3:3+TS]), inc="V")
                ub2 = [m8[:, :, 7], m8[:, :, 6]]
                b.wait("gpsimd", "V", dtx_tgt)
                for s in range(DS):
                    if len(scan_tgts) >= 2:
                        b.wait("scalar", "V", scan_tgts[-2])
                        b.wait("gpsimd", "V", scan_tgts[-2])
                    b.emit("scalar", lambda e, ct=ct, s=s, d=da2[s % 2]:
                           e.activation(out=d, in_=dt_f, func=AF.Exp,
                                        scale=a_sb[:, ct, s:s+1]), inc="S")
                    sS = b.cnt["S"]
                    gU = b.emit("gpsimd", lambda e, s=s, u=ub2[s % 2]:
                                e.tensor_mul(out=u, in0=dtx,
                                             in1=brep[:, s, :]), inc="G")
                    b.wait("vector", "S", sS)
                    b.wait("vector", "G", gU)
                    stg = b.emit("vector", lambda e, ct=ct, s=s, d=da2[s % 2],
                                 u=ub2[s % 2]: e.tensor_tensor_scan(
                                     out=Hbuf[:, :, s], data0=d,
                                     data1=u,
                                     initial=hstate[:, ct, s:s+1],
                                     op0=ALU.mult, op1=ALU.add), inc="V")
                    scan_tgts.append(stg)
                b.emit("vector", lambda e, ct=ct: e.tensor_copy(
                    out=hstate[:, ct, :], in_=Hbuf[:, TS-1, :]))
                b.emit("vector", lambda e: e.tensor_mul(
                    out=Hbuf[:, :, :], in0=Hbuf[:, :, :], in1=crep[:, :, :]))
                b.emit("vector", lambda e: e.tensor_add(
                    out=m8[:, :, 0:8],
                    in0=Hbuf[:, :, 0:8], in1=Hbuf[:, :, 8:16]))
                b.emit("vector", lambda e: e.tensor_add(
                    out=m8[:, :, 0:4], in0=m8[:, :, 0:4], in1=m8[:, :, 4:8]))
                b.emit("vector", lambda e: e.tensor_add(
                    out=m8[:, :, 0:2], in0=m8[:, :, 0:2], in1=m8[:, :, 2:4]))
                b.emit("vector", lambda e: e.tensor_add(
                    out=m8[:, :, 0], in0=m8[:, :, 0], in1=m8[:, :, 1]))
                b.emit("vector", lambda e, ct=ct: e.scalar_tensor_tensor(
                    out=m8[:, :, 1], in0=xi[:, ct, 3:3+TS],
                    scalar=d_sb[:, ct:ct+1], in1=m8[:, :, 0],
                    op0=ALU.mult, op1=ALU.add))
                b.emit("vector", lambda e, ct=ct: e.tensor_mul(
                    out=ybuf[:, ct, :], in0=m8[:, :, 1], in1=gz[:, ct, :]),
                    inc="V")
            scanV = b.cnt["V"]

            if PH < 7:
                continue
            b.wait("tensor", "V", scanV)
            for mt in range(8):
                wb16, wtgt = wout_ld(mt, W["l0_wout"][:, mt*128:(mt+1)*128]
                                     .rearrange("(k p) q -> p k q", p=128),
                                     ("p (k q) -> p k q", dict(k=16)))
                b.wait("tensor", wtgt[0], wtgt[1])

                def ev_s(e, p, mt=mt):
                    return e.tensor_copy(out=sT[:, mt, :], in_=p)
                mm_group([(wb16[:, kt, :], ybuf[:, kt, :]) for kt in range(16)],
                         128, TS, ev_s, "vector")
                wout_done(mt, b.cnt["T"])

            if PH < 8:
                continue
            b.wait("vector", "S", gS)
            b.emit("vector", lambda e: e.tensor_mul(
                out=prod[:, :, :], in0=sT[:, :, :], in1=gT[:, :, :]), inc="V")
            b.wait("tensor", "V", b.cnt["V"])
            transpose_tiles([(prod[:, dc, j*128:(j+1)*128],
                              xres[:, tt0 + j, dc*128:(dc+1)*128])
                             for j in range(ntt) for dc in range(8)],
                            out_op="add")
        b.barrier()


    def layer1():
        POOLN[0] = 5
        AR.reset(); STA.reset(); WB.reset()
        stream_sem_pool[:] = ["D0", "D1", "D2", "D3", "D4", "D5", "D6", "D7"]

        sq_buf = WB.alloc(2 * 1024, "p (b2 m) -> p b2 m", b2=2)
        sk_buf = WB.alloc(2 * 1024, "p (b2 m) -> p b2 m", b2=2)
        sv_buf = WB.alloc(2 * 2048, "p (b2 m) -> p b2 m", b2=2)
        so_buf = WB.alloc(2 * 1024, "p (b2 m) -> p b2 m", b2=2)
        sq_ld, sq_done = make_stream(sq_buf)
        sk_ld, sk_done = make_stream(sk_buf)
        sv_ld, sv_done = make_stream(sv_buf)
        so_ld, so_done = make_stream(so_buf)
        pass
        wWD_all = b.cnt["WD"]

        big = AR.alloc(16384)
        chunksT = AR.alloc(16384, "p (k t) -> p k t", k=8)
        Vp = AR.alloc(2 * 16 * 65, "p (tm h e) -> p tm h e", tm=2, h=16)
        ctxn = AR.alloc(2 * 1024, "p (tm d) -> p tm d", tm=2)
        at = AR.alloc(2 * 256, "p (kt t) -> p kt t", kt=2)
        summT = AR.alloc(8 * 8, "p (k c) -> p k c", k=8)
        gbuf = AR.alloc(144)
        ce_all = STA.alloc(2 * 8 * 8)
        offtot = gbuf[:, 128:144]
        wrT = AR.alloc(8 * 8, "p (k e) -> p k e", k=8)
        xn2 = big[:, 0:2048].rearrange("p (tt d) -> p tt d", tt=2)
        xnT_c = big[:, 2048:4096].rearrange("p (k t) -> p k t", k=8)
        QT = big[:, 4096:6144].rearrange("p (k t) -> p k t", k=8)
        KT = big[:, 6144:8192].rearrange("p (k t) -> p k t", k=8)
        ctxT = QT
        macc = big[:, 0:8192].rearrange("p (tt d) -> p tt d", tt=8)
        xnTh = big[:, 8192:16384].rearrange("p (k t) -> p k t", k=8)

        wrsb = ctxn[:, 1, :]
        dma("gpsimd", wrsb[0:8, :], W["l1_wr"])
        wWD_all = b.cnt["WD"]
        mcol, rstd = layernorm_stats(xn2[:, 0, :])
        recip_col = STA.alloc(1)
        lgbuf = STA.alloc(8)
        m1c = STA.alloc(1); m2c = STA.alloc(1)
        e1f = STA.alloc(8); e2f = STA.alloc(8)
        cums = STA.alloc(8); cums2 = STA.alloc(8)
        posc = STA.alloc(8); w1c = STA.alloc(1); w2c = STA.alloc(1)
        k1c = STA.alloc(1); k2c = STA.alloc(1)

        b.wait("tensor", "WD", wWD_all)
        for dk in range(8):
            sslot = tp_state["slot"] % 2
            tp_state["slot"] += 1
            if len(tp_state["evacs"]) >= 2:
                b.wait("tensor", "V", tp_state["evacs"][-2])
            tT = b.emit("tensor", lambda e, sslot=sslot, dk=dk: e.transpose(
                ptp2[sslot][0:128, 0:8], wrsb[0:8, dk*128:(dk+1)*128],
                ident[0:8, 0:8]), inc="T")
            b.wait("vector", "T", tT)
            vv = b.emit("vector", lambda e, sslot=sslot, dk=dk: e.tensor_copy(
                out=wrT[:, dk, :], in_=ptp2[sslot][0:128, 0:8]), inc="V")
            tp_state["evacs"].append(vv)
        b.emit("vector", lambda e: e.memset(offtot[0:1, :], 0.0), inc="V")
        b.emit("vector", lambda e: e.memset(
            Vp[:, :, :, 64:65].rearrange("p a b c -> p (a b c)"), 0.0), inc="V")
        b.emit("vector", lambda e: e.memset(
            Vp[:, :, :, 64:65].rearrange("p a b c -> p (a b c)"), 1.0), inc="V")

        for c in range(8):
            for j in range(2):
                ln_apply(mcol, rstd, 2*c + j, xn2[:, j, :])
            lnV = b.cnt["V"]
            b.wait("tensor", "V", lnV)
            transpose_tiles([(xn2[:, j, dc*128:(dc+1)*128],
                              xnT_c[:, dc, j*128:(j+1)*128])
                             for j in range(2) for dc in range(8)])
            xnT_V = b.cnt["V"]
            b.wait("tensor", "V", xnT_V)
            for mt in range(8):
                wqb, qt_ = sq_ld(c*8 + mt, W["l1_wq_l"][:, mt*128:(mt+1)*128]
                                 .rearrange("(k p) q -> p k q", p=128),
                                 ("p (k q) -> p k q", dict(k=8)))
                b.wait("tensor", qt_[0], qt_[1])
                def evq(e, p, mt=mt):
                    return e.tensor_copy(out=QT[:, mt, :], in_=p)
                mm_group([(wqb[:, kt, :], xnT_c[:, kt, :])
                          for kt in range(8)], 128, 256, evq, "vector")
                sq_done(c*8 + mt, b.cnt["T"])
            for mt in range(8):
                wkb, kt_ = sk_ld(c*8 + mt, W["l1_wk_l"][:, mt*128:(mt+1)*128]
                                 .rearrange("(k p) q -> p k q", p=128),
                                 ("p (k q) -> p k q", dict(k=8)))
                b.wait("tensor", kt_[0], kt_[1])
                def evk(e, p, mt=mt):
                    return e.tensor_copy(out=KT[:, mt, :], in_=p)
                mm_group([(wkb[:, kt, :], xnT_c[:, kt, :])
                          for kt in range(8)], 128, 256, evk, "vector")
                sk_done(c*8 + mt, b.cnt["T"])
            for nb in range(4):
                wvb, vt_ = sv_ld(c*4 + nb, W["l1_wv_l"][:, nb*256:(nb+1)*256]
                                 .rearrange("(k p) q -> p k q", p=128),
                                 ("p (k q) -> p k q", dict(k=8)))
                b.wait("tensor", vt_[0], vt_[1])
                for tm in range(2):
                    def evv(e, p, tm=tm, nb=nb):
                        return e.tensor_copy(
                            out=Vp[:, tm, nb*4:(nb+1)*4, 0:64],
                            in_=p.rearrange("p (h x) -> p h x", h=4))
                    mm_group([(xnT_c[:, kt, tm*128:(tm+1)*128],
                               wvb[:, kt, :])
                              for kt in range(8)], 128, 256, evv, "vector")
                sv_done(c*4 + nb, b.cnt["T"])
            projV = b.cnt["V"]
            b.wait("tensor", "V", projV)
            for h in range(16):
                r0 = (h % 2) * 64
                mt = h // 2
                for kt in range(2):
                    def eva(e, p, kt=kt):
                        return e.activation(out=at[:, kt, :], in_=p,
                                            func=AF.Exp, scale=0.125)
                    mm_group([(KT[r0:r0+64, mt, kt*128:(kt+1)*128],
                               QT[r0:r0+64, mt, 0:256])],
                             128, 256, eva, "scalar")
                aS = b.cnt["S"]
                b.wait("tensor", "S", aS)
                for qm in range(2):
                    bi = pbi[0] % POOLN[0]
                    pbi[0] += 1
                    bank = pb[bi]
                    prev = evac_after_bank.get(bi)
                    if prev is not None:
                        b.wait("tensor", prev[0], prev[1])
                    for kt in range(2):
                        b.emit("tensor", lambda e, kt=kt, qm=qm, h=h, p=bank:
                               e.matmul(p[:, 0:65], at[:, kt, qm*128:(qm+1)*128],
                                        Vp[:, kt, h, :], start=(kt == 0),
                                        stop=(kt == 1)),
                               inc=("T" if kt == 1 else None))
                    tT = b.cnt["T"]
                    b.wait("vector", "T", tT)
                    b.emit("vector", lambda e, p=bank: e.reciprocal(
                        out=recip_col, in_=p[:, 64:65]), inc="V")
                    tgt = b.emit("vector", lambda e, p=bank, qm=qm, h=h:
                                 e.tensor_scalar(
                                     out=ctxn[:, qm, 64*h:64*h+64],
                                     in0=p[:, 0:64], scalar1=recip_col,
                                     scalar2=None, op0=ALU.mult), inc="V")
                    evac_after_bank[bi] = ("V", tgt)
            ctxV = b.cnt["V"]
            b.wait("tensor", "V", ctxV)
            transpose_tiles([(ctxn[:, j, dc*128:(dc+1)*128],
                              ctxT[:, dc, j*128:(j+1)*128])
                             for j in range(2) for dc in range(8)])
            b.wait("tensor", "V", b.cnt["V"])
            for mt in range(8):
                wob, ot_ = so_ld(c*8 + mt, W["l1_wo_l"][:, mt*128:(mt+1)*128]
                                 .rearrange("(k p) q -> p k q", p=128),
                                 ("p (k q) -> p k q", dict(k=8)))
                b.wait("tensor", ot_[0], ot_[1])
                def evc(e, p, mt=mt, c=c):
                    return e.tensor_add(out=chunksT[:, mt, c*256:(c+1)*256],
                                        in0=xnT_c[:, mt, :], in1=p)
                mm_group([(wob[:, kt, :], ctxT[:, kt, :])
                          for kt in range(8)], 128, 256, evc, "vector")
                so_done(c*8 + mt, b.cnt["T"])
            b.wait("vector", "V", b.cnt["V"])
            b.emit("vector", lambda e, c=c: e.tensor_reduce(
                out=summT[:, :, c], in_=chunksT[:, :, c*256:(c+1)*256]
                .rearrange("p k (z t) -> p (k z) t", z=1),
                axis=AX.X, op=ALU.add), inc="V")
        b.emit("vector", lambda e: e.tensor_scalar(
            out=summT[:, :, :], in0=summT[:, :, :], scalar1=1.0/256,
            scalar2=None, op0=ALU.mult), inc="V")
        summV = b.cnt["V"]

        b.wait("tensor", "V", summV)
        gQ = gbuf[:, 0:64].rearrange("p (k t) -> p k t", k=8)
        gK = gbuf[:, 64:128].rearrange("p (k t) -> p k t", k=8)
        gVp = gbuf[:, 128:1168 if False else 128][:, 0:0]
        gVp = AR.alloc(16 * 65, "p (h e) -> p h e", h=16)
        gat = AR.alloc(8)
        gctxn = ctxn[:, 0, :]
        gctxT = AR.alloc(8 * 8, "p (k t) -> p k t", k=8)
        gfin = AR.alloc(8 * 8, "p (k c) -> p k c", k=8)
        b.emit("vector", lambda e: e.memset(
            gVp[:, :, 64:65].rearrange("p a c -> p (a c)"), 1.0), inc="V")
        for mt in range(8):
            wqb, qt_ = sq_ld(64 + mt, W["l1_wq_g"][:, mt*128:(mt+1)*128]
                             .rearrange("(k p) q -> p k q", p=128),
                             ("p (k q) -> p k q", dict(k=8)))
            b.wait("tensor", qt_[0], qt_[1])
            def evgq(e, p, mt=mt):
                return e.tensor_copy(out=gQ[:, mt, :], in_=p)
            mm_group([(wqb[:, kt, :], summT[:, kt, :])
                      for kt in range(8)], 128, 8, evgq, "vector")
            sq_done(64 + mt, b.cnt["T"])
        for mt in range(8):
            wkb, kt_ = sk_ld(64 + mt, W["l1_wk_g"][:, mt*128:(mt+1)*128]
                             .rearrange("(k p) q -> p k q", p=128),
                             ("p (k q) -> p k q", dict(k=8)))
            b.wait("tensor", kt_[0], kt_[1])
            def evgk(e, p, mt=mt):
                return e.tensor_copy(out=gK[:, mt, :], in_=p)
            mm_group([(wkb[:, kt, :], summT[:, kt, :])
                      for kt in range(8)], 128, 8, evgk, "vector")
            sk_done(64 + mt, b.cnt["T"])
        for nb in range(4):
            wvb, vt_ = sv_ld(32 + nb, W["l1_wv_g"][:, nb*256:(nb+1)*256]
                             .rearrange("(k p) q -> p k q", p=128),
                             ("p (k q) -> p k q", dict(k=8)))
            b.wait("tensor", vt_[0], vt_[1])
            def evgv(e, p, nb=nb):
                return e.tensor_copy(out=gVp[0:8, nb*4:(nb+1)*4, 0:64],
                                     in_=p.rearrange("p (h x) -> p h x", h=4))
            mm_group([(summT[:, kt, :], wvb[:, kt, :])
                      for kt in range(8)], 8, 256, evgv, "vector")
            sv_done(32 + nb, b.cnt["T"])
        gprojV = b.cnt["V"]
        b.wait("tensor", "V", gprojV)
        for h in range(16):
            r0 = (h % 2) * 64
            mt = h // 2
            def evga(e, p, h=h):
                return e.activation(out=gat[0:8, :], in_=p, func=AF.Exp,
                                    scale=0.125)
            mm_group([(gK[r0:r0+64, mt, :], gQ[r0:r0+64, mt, :])],
                     8, 8, evga, "scalar")
            gaS = b.cnt["S"]
            b.wait("tensor", "S", gaS)
            bi = pbi[0] % POOLN[0]
            pbi[0] += 1
            bank = pb[bi]
            prev = evac_after_bank.get(bi)
            if prev is not None:
                b.wait("tensor", prev[0], prev[1])
            b.emit("tensor", lambda e, h=h, p=bank: e.matmul(
                p[0:8, 0:65], gat[0:8, :], gVp[0:8, h, :],
                start=True, stop=True), inc="T")
            b.wait("vector", "T", b.cnt["T"])
            b.emit("vector", lambda e, p=bank: e.reciprocal(
                out=recip_col[0:8, :], in_=p[0:8, 64:65]), inc="V")
            tgt = b.emit("vector", lambda e, p=bank, h=h: e.tensor_scalar(
                out=gctxn[0:8, 64*h:64*h+64], in0=p[0:8, 0:64],
                scalar1=recip_col[0:8, :], scalar2=None, op0=ALU.mult), inc="V")
            evac_after_bank[bi] = ("V", tgt)
        b.wait("tensor", "V", b.cnt["V"])
        for dk in range(8):
            sslot = tp_state["slot"] % 2
            tp_state["slot"] += 1
            if len(tp_state["evacs"]) >= 2:
                b.wait("tensor", "V", tp_state["evacs"][-2])
            tT = b.emit("tensor", lambda e, sslot=sslot, dk=dk: e.transpose(
                ptp2[sslot][0:128, 0:8], gctxn[0:8, dk*128:(dk+1)*128],
                ident[0:8, 0:8]), inc="T")
            b.wait("vector", "T", tT)
            vv = b.emit("vector", lambda e, sslot=sslot, dk=dk: e.tensor_copy(
                out=gctxT[:, dk, :], in_=ptp2[sslot][0:128, 0:8]), inc="V")
            tp_state["evacs"].append(vv)
        b.wait("tensor", "V", b.cnt["V"])
        for mt in range(8):
            wob, ot_ = so_ld(64 + mt, W["l1_wo_g"][:, mt*128:(mt+1)*128]
                             .rearrange("(k p) q -> p k q", p=128),
                             ("p (k q) -> p k q", dict(k=8)))
            b.wait("tensor", ot_[0], ot_[1])
            def evgf(e, p, mt=mt):
                return e.tensor_add(out=gfin[:, mt, :], in0=summT[:, mt, :],
                                    in1=p)
            mm_group([(wob[:, kt, :], gctxT[:, kt, :])
                      for kt in range(8)], 128, 8, evgf, "vector")
            so_done(64 + mt, b.cnt["T"])
        b.wait("vector", "V", b.cnt["V"])
        for c in range(8):
            for mt in range(8):
                b.emit("vector", lambda e, c=c, mt=mt: e.tensor_add(
                    out=chunksT[:, mt, c*256:(c+1)*256],
                    in0=chunksT[:, mt, c*256:(c+1)*256],
                    in1=gfin[:, mt, c:c+1].to_broadcast([128, 256])), inc="V")
        s1V = b.cnt["V"]

        prodscr = ctxn
        tp_done = {0: 0, 1: 0}
        for hf in range(2):
            for tt in range(8):
                gt = hf * 8 + tt
                b.wait("vector", "T", tp_done[tt % 2])
                ln_apply(mcol, rstd, gt, ctxn[:, tt % 2, :])
                b.wait("tensor", "V", b.cnt["V"])
                transpose_tiles([(ctxn[:, tt % 2, dc*128:(dc+1)*128],
                                  xnTh[:, dc, tt*128:(tt+1)*128])
                                 for dc in range(8)])
                tp_done[tt % 2] = b.cnt["T"]
            xnhV = b.cnt["V"]
            b.wait("tensor", "V", xnhV)
            for tt in range(8):
                def evlg(e, p, tt=tt):
                    return e.tensor_copy(out=lgbuf, in_=p)
                mm_group([(xnTh[:, kt, tt*128:(tt+1)*128], wrT[:, kt, :])
                          for kt in range(8)], 128, 8, evlg, "vector")
                b.emit("vector", lambda e: e.reduce_max(
                    out=m1c, in_=lgbuf, axis=AX.X))
                b.emit("vector", lambda e: e.tensor_scalar(
                    out=e1f, in0=lgbuf, scalar1=m1c, scalar2=None,
                    op0=ALU.is_ge))
                b.emit("vector", lambda e: e.memset(cums, 0.0))
                b.emit("vector", lambda e: e.tensor_copy(out=cums[:, 1:8],
                                                         in_=e1f[:, 0:7]))
                b.emit("vector", lambda e: e.tensor_add(
                    out=cums[:, 2:8], in0=cums[:, 2:8], in1=e1f[:, 0:6]))
                b.emit("vector", lambda e: e.tensor_add(
                    out=cums[:, 3:8], in0=cums[:, 3:8], in1=e1f[:, 0:5]))
                b.emit("vector", lambda e: e.tensor_add(
                    out=cums[:, 4:8], in0=cums[:, 4:8], in1=e1f[:, 0:4]))
                b.emit("vector", lambda e: e.tensor_add(
                    out=cums[:, 5:8], in0=cums[:, 5:8], in1=e1f[:, 0:3]))
                b.emit("vector", lambda e: e.tensor_add(
                    out=cums[:, 6:8], in0=cums[:, 6:8], in1=e1f[:, 0:2]))
                b.emit("vector", lambda e: e.tensor_add(
                    out=cums[:, 7:8], in0=cums[:, 7:8], in1=e1f[:, 0:1]))
                b.emit("vector", lambda e: e.tensor_scalar(
                    out=cums, in0=cums, scalar1=0.5, scalar2=None,
                    op0=ALU.is_lt))
                b.emit("vector", lambda e: e.tensor_mul(
                    out=e1f, in0=e1f, in1=cums))
                b.emit("vector", lambda e: e.tensor_scalar(
                    out=cums2, in0=e1f, scalar1=-1e30, scalar2=None,
                    op0=ALU.mult))
                b.emit("vector", lambda e: e.tensor_add(
                    out=cums2, in0=cums2, in1=lgbuf))
                b.emit("vector", lambda e: e.reduce_max(
                    out=m2c, in_=cums2, axis=AX.X))
                b.emit("vector", lambda e: e.tensor_scalar(
                    out=e2f, in0=cums2, scalar1=m2c, scalar2=None,
                    op0=ALU.is_ge))
                b.emit("vector", lambda e: e.memset(cums, 0.0))
                b.emit("vector", lambda e: e.tensor_copy(out=cums[:, 1:8],
                                                         in_=e2f[:, 0:7]))
                b.emit("vector", lambda e: e.tensor_add(
                    out=cums[:, 2:8], in0=cums[:, 2:8], in1=e2f[:, 0:6]))
                b.emit("vector", lambda e: e.tensor_add(
                    out=cums[:, 3:8], in0=cums[:, 3:8], in1=e2f[:, 0:5]))
                b.emit("vector", lambda e: e.tensor_add(
                    out=cums[:, 4:8], in0=cums[:, 4:8], in1=e2f[:, 0:4]))
                b.emit("vector", lambda e: e.tensor_add(
                    out=cums[:, 5:8], in0=cums[:, 5:8], in1=e2f[:, 0:3]))
                b.emit("vector", lambda e: e.tensor_add(
                    out=cums[:, 6:8], in0=cums[:, 6:8], in1=e2f[:, 0:2]))
                b.emit("vector", lambda e: e.tensor_add(
                    out=cums[:, 7:8], in0=cums[:, 7:8], in1=e2f[:, 0:1]))
                b.emit("vector", lambda e: e.tensor_scalar(
                    out=cums, in0=cums, scalar1=0.5, scalar2=None,
                    op0=ALU.is_lt))
                b.emit("vector", lambda e: e.tensor_mul(
                    out=e2f, in0=e2f, in1=cums))
                b.emit("vector", lambda e: e.tensor_sub(
                    out=w1c, in0=m1c, in1=m2c), inc="V")
                b.wait("scalar", "V", b.cnt["V"])
                b.emit("scalar", lambda e: e.activation(
                    out=w1c, in_=w1c, func=AF.Sigmoid), inc="S")
                b.wait("vector", "S", b.cnt["S"])
                b.emit("vector", lambda e: e.tensor_scalar(
                    out=w2c, in0=w1c, scalar1=-1.0, scalar2=1.0,
                    op0=ALU.mult, op1=ALU.add))
                bi = pbi[0] % POOLN[0]
                pbi[0] += 1
                bank = pb[bi]
                prev = evac_after_bank.get(bi)
                if prev is not None:
                    b.wait("tensor", prev[0], prev[1])
                b.emit("vector", lambda e: e.tensor_copy(
                    out=at[:, 0, 0:8], in_=e1f))
                b.emit("vector", lambda e: e.tensor_copy(
                    out=at[:, 0, 8:16], in_=e2f), inc="V")
                b.wait("tensor", "V", b.cnt["V"])
                b.emit("tensor", lambda e, p=bank: e.matmul(
                    p[:, 0:16], ustrict, at[:, 0, 0:16],
                    start=True, stop=False))
                b.emit("tensor", lambda e, p=bank: e.matmul(
                    p[:, 0:16], ones_row[0:1, :], offtot[0:1, 0:16],
                    start=False, stop=True))
                b.emit("tensor", lambda e, p=bank: e.matmul(
                    p[0:1, 16:32], ones_col, at[:, 0, 0:16],
                    start=True, stop=True), inc="T")
                tT = b.cnt["T"]
                b.wait("vector", "T", tT)
                b.emit("vector", lambda e, p=bank: e.tensor_add(
                    out=offtot[0:1, :], in0=offtot[0:1, :], in1=p[0:1, 16:32]))
                b.emit("vector", lambda e, p=bank: e.tensor_copy(
                    out=cums2, in_=p[:, 8:16]))
                b.emit("vector", lambda e, p=bank: e.tensor_add(
                    out=posc, in0=cums2, in1=p[:, 0:8]), inc="V")
                vv = b.cnt["V"]
                evac_after_bank[bi] = ("V", vv)
                b.emit("vector", lambda e: e.tensor_mul(
                    out=cums, in0=posc, in1=e1f))
                b.emit("vector", lambda e: e.tensor_reduce(
                    out=k1c, in_=cums, axis=AX.X, op=ALU.add))
                b.emit("vector", lambda e: e.tensor_scalar(
                    out=k1c, in0=k1c, scalar1=768.5, scalar2=None,
                    op0=ALU.is_lt))
                b.emit("vector", lambda e: e.tensor_add(
                    out=cums2, in0=posc, in1=e1f))
                b.emit("vector", lambda e: e.tensor_mul(
                    out=cums2, in0=cums2, in1=e2f))
                b.emit("vector", lambda e: e.tensor_reduce(
                    out=k2c, in_=cums2, axis=AX.X, op=ALU.add))
                b.emit("vector", lambda e: e.tensor_scalar(
                    out=k2c, in0=k2c, scalar1=768.5, scalar2=None,
                    op0=ALU.is_lt))
                b.emit("vector", lambda e: e.tensor_mul(
                    out=k1c, in0=k1c, in1=w1c))
                b.emit("vector", lambda e: e.tensor_mul(
                    out=k2c, in0=k2c, in1=w2c))
                b.emit("vector", lambda e: e.tensor_scalar(
                    out=e1f, in0=e1f, scalar1=k1c, scalar2=None, op0=ALU.mult))
                b.emit("vector", lambda e: e.tensor_scalar(
                    out=e2f, in0=e2f, scalar1=k2c, scalar2=None, op0=ALU.mult))
                b.emit("vector", lambda e, hf=hf, tt=tt: e.tensor_add(
                    out=ce_all[:, (hf*8+tt)*8:(hf*8+tt)*8+8], in0=e1f,
                    in1=e2f), inc="V")
            ceV = b.cnt["V"]
            b.wait("tensor", "V", ceV)
            for ex in range(8):
                for nb4 in range(4):
                    web, wtgt = sv_ld(40 + hf*32 + ex*4 + nb4,
                                      W["l1_we"][ex][:, nb4*256:(nb4+1)*256]
                                      .rearrange("(k p) m -> p k m", p=128),
                                      ("p (k m) -> p k m", dict(k=8)))
                    b.wait("tensor", wtgt[0], wtgt[1])
                    for tt in range(8):
                        def evme(e, p, ex=ex, tt=tt, nb4=nb4, hf=hf):
                            gt = hf*8 + tt
                            dst = macc[:, tt, nb4*256:(nb4+1)*256]
                            if ex == 0:
                                return e.tensor_scalar(
                                    out=dst, in0=p,
                                    scalar1=ce_all[:, gt*8:gt*8+1],
                                    scalar2=None, op0=ALU.mult)
                            return e.scalar_tensor_tensor(
                                out=dst, in0=p,
                                scalar=ce_all[:, gt*8+ex:gt*8+ex+1],
                                in1=dst, op0=ALU.mult, op1=ALU.add)
                        mm_group([(xnTh[:, kt, tt*128:(tt+1)*128],
                                   web[:, kt, :])
                                  for kt in range(8)], 128, 256, evme,
                                 "vector")
                    sv_done(40 + hf*32 + ex*4 + nb4, b.cnt["T"])
            maccV = b.cnt["V"]
            b.wait("scalar", "V", maccV)
            for tt in range(8):
                b.emit("scalar", lambda e, tt=tt: e.activation(
                    out=macc[:, tt, :], in_=macc[:, tt, :], func=AF.Silu),
                    inc="S")
            gsS = b.cnt["S"]
            b.wait("vector", "S", gsS)
            b.wait("vector", "V", s1V)
            for tt in range(8):
                gt = hf*8 + tt
                for dc in range(8):
                    sslot = tp_state["slot"] % 2
                    tp_state["slot"] += 1
                    if len(tp_state["evacs"]) >= 2:
                        b.wait("tensor", "V", tp_state["evacs"][-2])
                    tT = b.emit("tensor", lambda e, sslot=sslot, dc=dc, gt=gt:
                                e.transpose(ptp2[sslot][:, 0:128],
                                            chunksT[:, dc, gt*128:(gt+1)*128],
                                            ident), inc="T")
                    b.wait("vector", "T", tT)
                    b.emit("vector", lambda e, sslot=sslot, tt=tt, dc=dc:
                           e.tensor_mul(out=prodscr[:, 0, 0:128],
                                        in0=ptp2[sslot][:, 0:128],
                                        in1=macc[:, tt, dc*128:(dc+1)*128]),
                           inc="V")
                    vv = b.emit("vector", lambda e, gt=gt, dc=dc: e.tensor_add(
                        out=xres[:, gt, dc*128:(dc+1)*128],
                        in0=xres[:, gt, dc*128:(dc+1)*128],
                        in1=prodscr[:, 0, 0:128]), inc="V")
                    tp_state["evacs"].append(vv)
        b.barrier()


    def layer2():
        POOLN[0] = 5
        AR.reset(); STA.reset(); WB.reset()
        stream_sem_pool[:] = ["D0", "D1", "D2", "D3", "D4", "D5", "D6", "D7"]

        identf = STA.alloc(128)
        lg = STA.alloc(8)
        negm = STA.alloc(1)
        ssum = STA.alloc(1)
        probs_t = STA.alloc(16 * 8, "p (tt e) -> p tt e", tt=16)
        rankm_t = STA.alloc(16 * 8, "p (tt e) -> p tt e", tt=16)
        cnt8 = STA.alloc(1)
        cgt8 = STA.alloc(1)
        need8 = STA.alloc(1)
        sums2 = STA.alloc(2)

        xn = AR.alloc(16 * 1024, "p (tt d) -> p tt d", tt=16)
        g2T = AR.alloc(8 * 2048, "p (k t) -> p k t", k=8)
        P_g = AR.alloc(4 * 2048, "p (i t) -> p i t", i=4)
        selA = P_g[:, 0, 0:2048]
        selB = P_g[:, 1, 0:2048]
        ones8 = P_g[:, 2, 0:2048]
        xnT_tt = P_g[:, 3, 0:1024].rearrange("p (k q) -> p k q", k=8)
        probs_e = pers_t[:, 0:4096].bitcast(F32)
        cumf = pers_t[:, 4096:8192].bitcast(F32)
        PT = pers_t[:, 0:8192].rearrange("p (tt i) -> p tt i", tt=16)
        cmask = pers_t[:, 8192:8320]
        w1_buf = WB.alloc(2 * 1024, "p (b2 m) -> p b2 m", b2=2)
        w2_buf = WB.alloc(2 * 1024, "p (b2 m) -> p b2 m", b2=2)
        XgT = WB.alloc(8 * 512, "p (k i) -> p k i", k=8)
        hT2 = [WB.alloc(256), WB.alloc(256)]
        pmisc2 = [pmisc[:, 0:256], ptp2[0].bitcast(F32)[:, 0:256]]
        gwsb = WB.alloc(1024)
        gwT = WB.alloc(8 * 8, "p (k e) -> p k e", k=8)
        w1_ld, w1_done = make_stream(w1_buf)
        w2_ld, w2_done = make_stream(w2_buf)

        mcol, rstd = layernorm_stats(xn[:, 0, :])
        for tt in range(NTT):
            ln_apply(mcol, rstd, tt, xn[:, tt, :])
        xnV = b.cnt["V"]
        b.emit("vector", lambda e: e.memset(identf, 1.0), inc="V")
        b.wait("gpsimd", "V", b.cnt["V"])
        b.emit("gpsimd", lambda e: e.affine_select(
            out=identf, in_=identf, pattern=[[-1, 128]],
            compare_op=ALU.is_equal, fill=0.0, channel_multiplier=1), inc="G")
        b.emit("vector", lambda e: e.memset(ones8[0:8, :], 1.0), inc="V")
        b.wait("vector", "G", b.cnt["G"])

        dma("gpsimd", gwsb[0:8, :], W["l2_gw"])
        b.wait("tensor", "WD", b.cnt["WD"])
        for dk in range(8):
            sslot = tp_state["slot"] % 2
            tp_state["slot"] += 1
            if len(tp_state["evacs"]) >= 2:
                b.wait("tensor", "V", tp_state["evacs"][-2])
            tT = b.emit("tensor", lambda e, sslot=sslot, dk=dk: e.transpose(
                ptp2[sslot][0:128, 0:8], gwsb[0:8, dk*128:(dk+1)*128],
                ident[0:8, 0:8]), inc="T")
            b.wait("vector", "T", tT)
            vv = b.emit("vector", lambda e, sslot=sslot, dk=dk: e.tensor_copy(
                out=gwT[:, dk, :], in_=ptp2[sslot][0:128, 0:8]), inc="V")
            tp_state["evacs"].append(vv)

        b.wait("tensor", "V", xnV)
        for tt in range(NTT):
            transpose_tiles([(xn[:, tt, dc*128:(dc+1)*128], xnT_tt[:, dc, :])
                             for dc in range(8)])
            b.wait("tensor", "V", b.cnt["V"])

            def evlg(e, p):
                return e.tensor_copy(out=lg, in_=p)
            mm_group([(xnT_tt[:, kt, :], gwT[:, kt, :]) for kt in range(8)],
                     128, 8, evlg, "vector")
            b.emit("vector", lambda e: e.reduce_max(out=negm, in_=lg,
                                                    axis=AX.X))
            b.emit("vector", lambda e: e.tensor_scalar(
                out=negm, in0=negm, scalar1=-1.0, scalar2=None, op0=ALU.mult),
                inc="V")
            b.wait("scalar", "V", b.cnt["V"])
            b.emit("scalar", lambda e: e.activation(
                out=lg, in_=lg, func=AF.Exp, bias=negm), inc="S")
            b.wait("vector", "S", b.cnt["S"])
            b.emit("vector", lambda e: e.reduce_sum(out=ssum, in_=lg,
                                                    axis=AX.X))
            b.emit("vector", lambda e: e.reciprocal(out=ssum, in_=ssum))
            b.emit("vector", lambda e, tt=tt: e.tensor_scalar(
                out=probs_t[:, tt, :], in0=lg, scalar1=ssum, scalar2=None,
                op0=ALU.mult), inc="V")
            b.wait("tensor", "V", b.cnt["V"])
            bi = pbi[0] % POOLN[0]
            pbi[0] += 1
            bank = pb[bi]
            prev = evac_after_bank.get(bi)
            if prev is not None:
                b.wait("tensor", prev[0], prev[1])
            b.emit("tensor", lambda e, tt=tt, p=bank: e.transpose(
                p[0:8, 0:128], probs_t[:, tt, :], identf), inc="T")
            b.wait("vector", "T", b.cnt["T"])
            tgt = b.emit("vector", lambda e, tt=tt, p=bank: e.tensor_copy(
                out=probs_e[0:8, tt*128:(tt+1)*128], in_=p[0:8, 0:128]),
                inc="V")
            evac_after_bank[bi] = ("V", tgt)

        candi = cnt8.bitcast(I32)
        tmpi = cgt8.bitcast(I32)
        keepf = need8
        b.emit("vector", lambda e: e.memset(candi[0:8, :], 0), inc="V")
        for bit in range(30, -1, -1):
            b.emit("vector", lambda e, bit=bit: e.tensor_tensor(
                out=tmpi[0:8, :], in0=candi[0:8, :],
                in1=csti[0:8, bit:bit+1], op=ALU.bitwise_or))
            b.emit("vector", lambda e: e.tensor_scalar(
                out=selA[0:8, :], in0=probs_e[0:8, :],
                scalar1=cgt8[0:8, :], scalar2=None, op0=ALU.is_ge))
            b.emit("vector", lambda e: e.reduce_sum(
                out=keepf[0:8, :], in_=selA[0:8, :], axis=AX.X))
            b.emit("vector", lambda e: e.tensor_scalar(
                out=keepf[0:8, :], in0=keepf[0:8, :], scalar1=511.5,
                scalar2=None, op0=ALU.is_ge))
            b.emit("vector", lambda e: e.tensor_copy(
                out=tmpi[0:8, :], in_=keepf[0:8, :]))
            b.emit("vector", lambda e, bit=bit: e.tensor_tensor(
                out=tmpi[0:8, :], in0=tmpi[0:8, :],
                in1=csti[0:8, bit:bit+1], op=ALU.mult))
            b.emit("vector", lambda e: e.tensor_tensor(
                out=candi[0:8, :], in0=candi[0:8, :], in1=tmpi[0:8, :],
                op=ALU.bitwise_or), inc="V")
        thr = cnt8
        b.emit("vector", lambda e: e.tensor_scalar(
            out=selA[0:8, :], in0=probs_e[0:8, :], scalar1=thr[0:8, :],
            scalar2=None, op0=ALU.is_gt))
        b.emit("vector", lambda e: e.reduce_sum(
            out=cgt8[0:8, :], in_=selA[0:8, :], axis=AX.X))
        b.emit("vector", lambda e: e.tensor_scalar(
            out=need8[0:8, :], in0=cgt8[0:8, :], scalar1=-1.0, scalar2=511.5,
            op0=ALU.mult, op1=ALU.add))
        b.emit("vector", lambda e: e.tensor_scalar(
            out=selB[0:8, :], in0=probs_e[0:8, :], scalar1=thr[0:8, :],
            scalar2=None, op0=ALU.is_equal))
        b.emit("vector", lambda e: e.tensor_tensor_scan(
            out=cumf[0:8, :], data0=ones8[0:8, :], data1=selB[0:8, :],
            initial=0.0, op0=ALU.mult, op1=ALU.add))
        b.emit("vector", lambda e: e.tensor_sub(
            out=cumf[0:8, :], in0=cumf[0:8, :], in1=selB[0:8, :]))
        b.emit("vector", lambda e: e.tensor_scalar(
            out=cumf[0:8, :], in0=cumf[0:8, :], scalar1=need8[0:8, :],
            scalar2=None, op0=ALU.is_lt))
        b.emit("vector", lambda e: e.tensor_mul(
            out=selB[0:8, :], in0=selB[0:8, :], in1=cumf[0:8, :]))
        b.emit("vector", lambda e: e.tensor_add(
            out=selA[0:8, :], in0=selA[0:8, :], in1=selB[0:8, :]))
        b.emit("vector", lambda e: e.tensor_tensor_scan(
            out=cumf[0:8, :], data0=ones8[0:8, :], data1=selA[0:8, :],
            initial=0.0, op0=ALU.mult, op1=ALU.add))
        b.emit("vector", lambda e: e.tensor_sub(
            out=cumf[0:8, :], in0=cumf[0:8, :], in1=selA[0:8, :]))
        b.emit("vector", lambda e: e.tensor_mul(
            out=cumf[0:8, :], in0=cumf[0:8, :], in1=selA[0:8, :]))
        b.emit("vector", lambda e: e.tensor_add(
            out=cumf[0:8, :], in0=cumf[0:8, :], in1=selA[0:8, :]))
        b.emit("vector", lambda e: e.tensor_scalar(
            out=cumf[0:8, :], in0=cumf[0:8, :], scalar1=1.0, scalar2=None,
            op0=ALU.subtract), inc="V")
        b.wait("tensor", "V", b.cnt["V"])
        for tt in range(NTT):
            bi = pbi[0] % POOLN[0]
            pbi[0] += 1
            bank = pb[bi]
            prev = evac_after_bank.get(bi)
            if prev is not None:
                b.wait("tensor", prev[0], prev[1])
            b.emit("tensor", lambda e, tt=tt, p=bank: e.transpose(
                p[0:128, 0:8], cumf[0:8, tt*128:(tt+1)*128],
                identf[0:8, 0:8]), inc="T")
            b.wait("vector", "T", b.cnt["T"])
            tgt = b.emit("vector", lambda e, tt=tt, p=bank: e.tensor_copy(
                out=rankm_t[:, tt, :], in_=p[0:128, 0:8]), inc="V")
            evac_after_bank[bi] = ("V", tgt)
        selV = b.cnt["V"]

        b.wait("vector", "V", selV)
        for ex in range(8):
            for tt in range(NTT):
                b.emit("vector", lambda e, tt=tt, ex=ex: e.tensor_scalar(
                    out=PT[:, tt, :], in0=iota_f,
                    scalar1=rankm_t[:, tt, ex:ex+1],
                    scalar2=None, op0=ALU.is_equal), inc="V")
            ptV = b.cnt["V"]
            b.wait("tensor", "V", ptV)
            for dk in range(8):
                def evxg(e, p, dk=dk):
                    return e.tensor_copy(out=XgT[:, dk, :], in_=p)
                mm_group([(xn[:, tt, dk*128:(dk+1)*128], PT[:, tt, :])
                          for tt in range(NTT)], 128, 512, evxg, "vector")
            for tt in range(NTT):
                b.emit("vector", lambda e, tt=tt, ex=ex: e.tensor_scalar(
                    out=PT[:, tt, :], in0=PT[:, tt, :],
                    scalar1=probs_t[:, tt, ex:ex+1], scalar2=None,
                    op0=ALU.mult), inc="V")
            b.wait("tensor", "V", b.cnt["V"])
            transpose_tiles([(PT[:, tt, im*128:(im+1)*128],
                              P_g[:, im, tt*128:(tt+1)*128])
                             for tt in range(NTT) for im in range(4)])
            pgV = b.cnt["V"]
            for ic in range(2):
                hgel = {}
                w2bs = {}
                for fc in range(33):
                    if fc <= 31:
                        w1b, t1 = w1_ld(ex*64 + ic*32 + fc,
                                        W["l2_w1"][ex][:, fc*128:(fc+1)*128]
                                        .rearrange("(k p) m -> p k m", p=128),
                                        ("p (k m) -> p k m", dict(k=8)))
                        w2b, t2 = w2_ld(ex*64 + ic*32 + fc,
                                        W["l2_w2"][ex][fc*128:(fc+1)*128, :])
                        w2bs[fc] = (w2b, t2)
                        b.wait("tensor", t1[0], t1[1])
                        if fc == 0:
                            b.wait("tensor", "V", pgV)
                            for bi4 in range(4):
                                prev = evac_after_bank.get(bi4)
                                if prev is not None:
                                    b.wait("tensor", prev[0], prev[1])

                        def evh(e, p, par=fc % 2):
                            return e.activation(out=hT2[par], in_=p,
                                                func=AF.Gelu)
                        mm_group([(w1b[:, kt, :],
                                   XgT[:, kt, ic*256:(ic+1)*256])
                                  for kt in range(8)], 128, 256, None,
                                 "scalar", bank=None if False else pmisc2[fc % 2])
                        tTh = b.cnt["T"]
                        b.wait("scalar", "T", tTh)
                        hgel[fc] = b.emit("scalar", lambda e, par=fc % 2:
                                          e.activation(
                                              out=hT2[par],
                                              in_=pmisc2[par][0:128, 0:256],
                                              func=AF.Gelu), inc="S")
                        w1_done(ex*64 + ic*32 + fc, b.cnt["T"])
                    fo = fc - 1
                    if 0 <= fo <= 31:
                        b.wait("tensor", "S", hgel[fo])
                        w2b, t2 = w2bs.pop(fo)
                        b.wait("tensor", t2[0], t2[1])
                        for mt in range(8):
                            bank = pb[mt // 2]
                            coff = (mt % 2) * 256
                            b.emit("tensor", lambda e, mt=mt, fo=fo, bank=bank,
                                   coff=coff, w2b=w2b, par=fo % 2: e.matmul(
                                       bank[:, coff:coff+256],
                                       w2b[:, mt*128:(mt+1)*128], hT2[par],
                                       start=(fo == 0 and mt % 2 == 0),
                                       stop=(fo == 31 and mt % 2 == 1)),
                                   inc=("T" if (fo == 31 and mt == 7)
                                        else None))
                        w2_done(ex*64 + ic*32 + fo, b.cnt["T"])
                outT = b.cnt["T"]
                b.wait("vector", "T", outT)
                oeT = pers_t[:, 0:2048].rearrange("p (k i) -> p k i", k=8)
                for mt in range(8):
                    b.emit("vector", lambda e, mt=mt, oeT=oeT: e.tensor_copy(
                        out=oeT[:, mt, :],
                        in_=pb[mt // 2][:, (mt % 2)*256:(mt % 2)*256+256]),
                        inc="V")
                oeV = b.cnt["V"]
                b.wait("tensor", "V", oeV)
                oe = pers_t[:, 2048:4096].rearrange("p (im d8) -> p im d8",
                                                    im=2)
                transpose_tiles([(oeT[:, mt, im*128:(im+1)*128],
                                  oe[:, im, mt*128:(mt+1)*128])
                                 for mt in range(8) for im in range(2)])
                oeV2 = b.cnt["V"]
                for bi4 in range(4):
                    evac_after_bank[bi4] = ("V", oeV)
                b.wait("tensor", "V", oeV2)
                for dk in range(8):
                    for ncx in range(4):
                        bi = pbi[0] % POOLN[0]
                        pbi[0] += 1
                        bank = pb[bi]
                        prev = evac_after_bank.get(bi)
                        if prev is not None:
                            b.wait("tensor", prev[0], prev[1])
                        for im in range(2):
                            b.emit("tensor", lambda e, dk=dk, ncx=ncx, im=im,
                                   ic=ic, p=bank, oe=oe: e.matmul(
                                       p[:, 0:512],
                                       oe[:, im, dk*128:(dk+1)*128],
                                       P_g[:, ic*2+im, ncx*512:(ncx+1)*512],
                                       start=(im == 0), stop=(im == 1)),
                                   inc=("T" if im == 1 else None))
                        tT = b.cnt["T"]
                        b.wait("vector", "T", tT)
                        if ex == 0 and ic == 0:
                            tgt = b.emit("vector", lambda e, dk=dk, ncx=ncx,
                                         p=bank: e.tensor_copy(
                                             out=g2T[:, dk,
                                                     ncx*512:(ncx+1)*512],
                                             in_=p[:, 0:512]), inc="V")
                        else:
                            tgt = b.emit("vector", lambda e, dk=dk, ncx=ncx,
                                         p=bank: e.tensor_add(
                                             out=g2T[:, dk,
                                                     ncx*512:(ncx+1)*512],
                                             in0=g2T[:, dk,
                                                     ncx*512:(ncx+1)*512],
                                             in1=p[:, 0:512]), inc="V")
                        evac_after_bank[bi] = ("V", tgt)
        b.wait("scalar", "V", b.cnt["V"])
        for dk in range(8):
            b.emit("scalar", lambda e, dk=dk: e.activation(
                out=g2T[:, dk, :], in_=g2T[:, dk, :], func=AF.Silu), inc="S")
        g2S = b.cnt["S"]


        xnT = xn.rearrange("p tt d -> p (tt d)").rearrange(
            "p (k t) -> p k t", k=8)
        QK = wbuf_t[:, 0:2048]
        Kb = wbuf_t[:, 6816:8864]
        Vp2 = wbuf_t[:, 2048:3088].rearrange("p (kt w) -> p kt w", kt=16)
        wqkv_buf = wbuf_t[:, 3104:5152].rearrange("p (b2 m) -> p b2 m", b2=2)
        wtp_buf = wbuf_t[:, 5152:5664].rearrange("p (k m) -> p k m", k=8)
        stage = wbuf_t[:, 5664:6688]
        at4 = pers_t[:, 8192:8704].rearrange("p (s t) -> p s t", s=4)
        cmask2 = wbuf_t[:, 6688:6816]
        ctxh = P_g.rearrange("p i t -> p (i t)").rearrange(
            "p (tt d) -> p tt d", tt=8)
        qkv_ld, qkv_done = make_stream(wqkv_buf)

        b.wait("vector", "S", g2S)
        b.wait("vector", "WD", b.cnt["WD"])
        for tt in range(NTT):
            b.wait("vector", "T", b.cnt["T"])
            ln_apply(mcol, rstd, tt, stage)
            b.wait("tensor", "V", b.cnt["V"])
            transpose_tiles([(stage[:, dc*128:(dc+1)*128],
                              xnT[:, dc, tt*128:(tt+1)*128])
                             for dc in range(8)])
        xnTV = b.cnt["V"]
        b.emit("vector", lambda e: e.tensor_add(out=cmask2, in0=ident,
                                                in1=ustrict), inc="V")
        b.emit("vector", lambda e: e.memset(
            Vp2[:, :, 64:65].rearrange("p a c -> p (a c)"), 1.0), inc="V")
        woT = pers_t[:, 0:8192].rearrange("p (k m) -> p k m", k=8)
        for rb in range(8):
            wob, t3 = w1_ld(512 + rb, W["l2_wout"][rb*128:(rb+1)*128, :])
            b.wait("tensor", t3[0], t3[1])
            transpose_tiles([(wob[:, dc*128:(dc+1)*128],
                              woT[:, dc, rb*128:(rb+1)*128])
                             for dc in range(8)])
            w1_done(512 + rb, b.cnt["T"])
        woTV = b.cnt["V"]
        b.wait("tensor", "V", max(xnTV, woTV))
        b.wait("vector", "D0", b.cnt["D0"])
        b.wait("vector", "D1", b.cnt["D1"])

        sidx = [0]

        def proj_rows(dram_rowblk, dst_row, src0, ncols):
            wb2, t4 = qkv_ld(sidx[0], dram_rowblk, rows=64)
            sidx[0] += 1
            b.wait("tensor", t4[0], t4[1])
            transpose_tiles([(wb2[0:64, dc*128:(dc+1)*128],
                              wtp_buf[:, dc, 0:64]) for dc in range(8)],
                            idn=64)
            b.wait("tensor", "V", b.cnt["V"])
            dstb = QK if dst_row == 0 else Kb
            for ncx in range(ncols // 512):
                def evq2(e, p, ncx=ncx, dstb=dstb):
                    return e.tensor_copy(
                        out=dstb[0:64, ncx*512:(ncx+1)*512], in_=p)
                mm_group([(wtp_buf[:, kt, 0:64],
                           xnT[:, kt, src0+ncx*512:src0+(ncx+1)*512])
                          for kt in range(8)], 64, 512, evq2, "vector")
            qkv_done(sidx[0] - 1, b.cnt["T"])

        for hf in range(2):
            q0 = hf * 1024
            nkt = (hf + 1) * 8
            for h in range(16):
                proj_rows(W["l2_wqkv"][64*h:64*h+64, :], 0, q0, 1024)
                proj_rows(W["l2_wqkv"][1024+64*h:1024+64*h+64, :], 64,
                          0, nkt * 128)
                wb2, t4 = qkv_ld(sidx[0],
                                 W["l2_wqkv"][2048+64*h:2048+64*h+64, :],
                                 rows=64)
                sidx[0] += 1
                b.wait("tensor", t4[0], t4[1])
                transpose_tiles([(wb2[0:64, dc*128:(dc+1)*128],
                                  wtp_buf[:, dc, 0:64]) for dc in range(8)],
                                idn=64)
                b.wait("tensor", "V", b.cnt["V"])
                for kt in range(nkt):
                    def evv2(e, p, kt=kt):
                        return e.tensor_copy(out=Vp2[:, kt, 0:64], in_=p)
                    mm_group([(xnT[:, k2, kt*128:(kt+1)*128],
                               wtp_buf[:, k2, 0:64])
                              for k2 in range(8)], 128, 64, evv2, "vector")
                qkv_done(sidx[0] - 1, b.cnt["T"])
                projV = b.cnt["V"]
                b.wait("tensor", "V", projV)
                for qt8 in range(8):
                    qt = hf * 8 + qt8
                    bi = pbi[0] % POOLN[0]
                    pbi[0] += 1
                    cbank = pb[bi]
                    prev = evac_after_bank.get(bi)
                    if prev is not None:
                        b.wait("tensor", prev[0], prev[1])
                    ready = {}
                    for kt in range(qt + 3):
                        if kt <= qt:
                            sl4 = kt % 4
                            bi2 = pbi[0] % POOLN[0]
                            pbi[0] += 1
                            if bi2 == bi:
                                bi2 = (bi2 + 1) % POOLN[0]
                                pbi[0] += 1
                            sbank = pb[bi2]
                            prev2 = evac_after_bank.get(bi2)
                            if prev2 is not None:
                                b.wait("tensor", prev2[0], prev2[1])
                            b.emit("tensor", lambda e, kt=kt, qt8=qt8,
                                   p=sbank: e.matmul(
                                       p[:, 0:128],
                                       Kb[0:64, kt*128:(kt+1)*128],
                                       QK[0:64, qt8*128:(qt8+1)*128],
                                       start=True, stop=True), inc="T")
                            tTs = b.cnt["T"]
                            b.wait("scalar", "T", tTs)
                            stgt = b.emit("scalar", lambda e, sl4=sl4,
                                          p=sbank: e.activation(
                                              out=at4[:, sl4, :],
                                              in_=p[:, 0:128],
                                              func=AF.Exp, scale=0.125),
                                          inc="S")
                            evac_after_bank[bi2] = ("S", stgt)
                            if kt == qt:
                                b.wait("vector", "S", stgt)
                                stgt = ("V", b.emit(
                                    "vector", lambda e, sl4=sl4:
                                    e.tensor_mul(out=at4[:, sl4, :],
                                                 in0=at4[:, sl4, :],
                                                 in1=cmask2), inc="V"))
                            else:
                                stgt = ("S", stgt)
                            ready[kt] = stgt
                        kav = kt - 2
                        if 0 <= kav <= qt:
                            b.wait("tensor", ready[kav][0], ready[kav][1])
                            b.emit("tensor", lambda e, sl4=kav % 4, kav=kav,
                                   qt=qt, p=cbank: e.matmul(
                                       p[:, 0:65], at4[:, sl4, :],
                                       Vp2[:, kav, :],
                                       start=(kav == 0), stop=(kav == qt)),
                                   inc=("T" if kav == qt else None))
                    tTc = b.cnt["T"]
                    b.wait("vector", "T", tTc)
                    b.emit("vector", lambda e, p=cbank: e.reciprocal(
                        out=sums2[:, 0:1], in_=p[:, 64:65]), inc="V")
                    tgt = b.emit("vector", lambda e, p=cbank, qt8=qt8, h=h:
                                 e.tensor_scalar(
                                     out=ctxh[:, qt8, 64*h:64*h+64],
                                     in0=p[:, 0:64], scalar1=sums2[:, 0:1],
                                     scalar2=None, op0=ALU.mult), inc="V")
                    evac_after_bank[bi] = ("V", tgt)
            ctxV = b.cnt["V"]
            b.wait("tensor", "V", ctxV)
            for tt8 in range(8):
                tt = hf * 8 + tt8
                transpose_tiles([(ctxh[:, tt8, dc*128:(dc+1)*128],
                                  stage[:, dc*128:(dc+1)*128])
                                 for dc in range(8)])
                b.wait("tensor", "V", b.cnt["V"])
                stgT = stage.rearrange("p (k q) -> p k q", k=8)
                for mt in range(8):
                    bi = pbi[0] % POOLN[0]
                    pbi[0] += 1
                    bank = pb[bi]
                    prev = evac_after_bank.get(bi)
                    if prev is not None:
                        b.wait("tensor", prev[0], prev[1])
                    for kt in range(8):
                        b.emit("tensor", lambda e, kt=kt, mt=mt, p=bank:
                               e.matmul(p[:, 0:128],
                                        woT[:, kt, mt*128:(mt+1)*128],
                                        stgT[:, kt, :],
                                        start=(kt == 0), stop=(kt == 7)),
                               inc=("T" if kt == 7 else None))
                    tT = b.cnt["T"]
                    b.wait("vector", "T", tT)
                    b.emit("vector", lambda e, p=bank, mt=mt, tt=tt:
                           e.tensor_mul(out=QK[:, 0:128], in0=p[:, 0:128],
                                        in1=g2T[:, mt, tt*128:(tt+1)*128]),
                           inc="V")
                    ptgt = b.cnt["V"]
                    evac_after_bank[bi] = ("V", ptgt)
                    b.wait("tensor", "V", ptgt)
                    sslot = tp_state["slot"] % 2
                    tp_state["slot"] += 1
                    if len(tp_state["evacs"]) >= 2:
                        b.wait("tensor", "V", tp_state["evacs"][-2])
                    tT2 = b.emit("tensor", lambda e, sslot=sslot: e.transpose(
                        ptp2[sslot][:, 0:128], QK[:, 0:128], ident), inc="T")
                    b.wait("vector", "T", tT2)
                    vv = b.emit("vector", lambda e, sslot=sslot, tt=tt, mt=mt:
                                e.tensor_add(
                                    out=xres[:, tt, mt*128:(mt+1)*128],
                                    in0=xres[:, tt, mt*128:(mt+1)*128],
                                    in1=ptp2[sslot][:, 0:128]), inc="V")
                    tp_state["evacs"].append(vv)
        b.barrier()


    def layer3():
        POOLN[0] = 5
        AR.reset(); STA.reset(); WB.reset()
        stream_sem_pool[:] = ["D0", "D1", "D2", "D3", "D4", "D5", "D6", "D7"]

        bq2 = STA.alloc(8)
        bkc = STA.alloc(8)
        boc = STA.alloc(8)
        murc = STA.alloc(8)
        mukc = STA.alloc(8)
        om_r = STA.alloc(8)
        om_k = STA.alloc(8)
        sd2 = [STA.alloc(512), STA.alloc(512)]
        rc512 = STA.alloc(4)

        xnT = AR.alloc(8 * 2048, "p (k t) -> p k t", k=8)
        ctx = AR.alloc(16 * 1024, "p (tt d) -> p tt d", tt=16)
        Ka = AR.alloc(2048)
        Qa = AR.alloc(2048)
        ksq_at = AR.alloc(2048)
        Vp3 = AR.alloc(16 * 65, "p (kt w) -> p kt w", kt=16)
        stage = Ka[:, 0:1024]

        wrow_buf = WB.alloc(2 * 1024, "p (b2 m) -> p b2 m", b2=2)
        wtp3 = WB.alloc(8 * 128, "p (k m) -> p k m", k=8)
        bvrow = WB.alloc(1024)
        chA = WB.alloc(8 * 128, "p (k t) -> p k t", k=8)
        chK = WB.alloc(8 * 128, "p (k t) -> p k t", k=8)
        chR = WB.alloc(8 * 128, "p (k t) -> p k t", k=8)
        g3b = WB.alloc(128)
        s3s = WB.alloc(8 * 128, "p (k t) -> p k t", k=8)
        ctxT_s = WB.alloc(8 * 128, "p (k t) -> p k t", k=8)
        wrow_ld, wrow_done = make_stream(wrow_buf)
        woT = pers_t[:, 0:8192].rearrange("p (k m) -> p k m", k=8)

        dma("gpsimd", bq2, W["l3_bq"].rearrange("(k p) -> p k", p=128),
            slow_ok=True)
        dma("gpsimd", bkc, W["l3_bk"].rearrange("(k p) -> p k", p=128),
            slow_ok=True)
        dma("gpsimd", boc, W["l3_bo"].rearrange("(k p) -> p k", p=128),
            slow_ok=True)
        dma("gpsimd", murc, W["l3_mur"].rearrange("(k p) -> p k", p=128),
            slow_ok=True)
        dma("gpsimd", mukc, W["l3_muk"].rearrange("(k p) -> p k", p=128),
            slow_ok=True)
        dma("gpsimd", bvrow[0:1, :],
            W["l3_bv"].rearrange("(q d) -> q d", q=1))
        wWD = b.cnt["WD"]
        b.wait("vector", "WD", wWD)
        b.emit("vector", lambda e: e.tensor_scalar(
            out=om_r, in0=murc, scalar1=-1.0, scalar2=1.0, op0=ALU.mult,
            op1=ALU.add))
        b.emit("vector", lambda e: e.tensor_scalar(
            out=om_k, in0=mukc, scalar1=-1.0, scalar2=1.0, op0=ALU.mult,
            op1=ALU.add), inc="V")

        mcol, rstd = layernorm_stats(ctx[:, 0, :])
        for tt in range(NTT):
            b.wait("vector", "T", b.cnt["T"])
            ln_apply(mcol, rstd, tt, stage)
            b.wait("tensor", "V", b.cnt["V"])
            transpose_tiles([(stage[:, dc*128:(dc+1)*128],
                              xnT[:, dc, tt*128:(tt+1)*128])
                             for dc in range(8)])
        xnTV = b.cnt["V"]

        b.emit("vector", lambda e: e.memset(
            Vp3[:, :, 64:65].rearrange("p a c -> p (a c)"), 1.0), inc="V")
        b.wait("tensor", "V", b.cnt["V"])

        sidx3 = [0]

        def load_wT(dram_rowblk, dst, rows=128):
            wb2, t4 = wrow_ld(sidx3[0], dram_rowblk, rows=rows)
            sidx3[0] += 1
            b.wait("tensor", t4[0], t4[1])
            transpose_tiles([(wb2[0:rows, dc*128:(dc+1)*128],
                              dst[:, dc, 0:rows]) for dc in range(8)],
                            idn=rows)
            b.wait("tensor", "V", b.cnt["V"])
            return b.cnt["V"]

        for h in range(16):
            r0 = (h % 2) * 64
            dk = h // 2
            load_wT(W["l3_wq"][64*h:64*h+64, :], wtp3, rows=64)
            for ncx in range(4):
                def evq3(e, p, ncx=ncx, r0=r0, dk=dk):
                    return e.tensor_scalar(
                        out=Qa[0:64, ncx*512:(ncx+1)*512], in0=p,
                        scalar1=bq2[r0:r0+64, dk:dk+1], scalar2=-2.0,
                        op0=ALU.add, op1=ALU.mult)
                mm_group([(wtp3[:, kt, 0:64],
                           xnT[:, kt, ncx*512:(ncx+1)*512])
                          for kt in range(8)], 64, 512, evq3, "vector")
            wrow_done(sidx3[0] - 1, b.cnt["T"])
            qS = b.cnt["V"]
            b.wait("scalar", "V", qS)
            b.wait("scalar", "T", b.cnt["T"])
            b.emit("scalar", lambda e: e.activation(
                out=ksq_at[0:64, :], in_=Qa[0:64, :], func=AF.Square),
                inc="S")
            b.wait("tensor", "S", b.cnt["S"])
            for ncx in range(4):
                bi = pbi[0] % POOLN[0]
                pbi[0] += 1
                bank = pb[bi]
                prev = evac_after_bank.get(bi)
                if prev is not None:
                    b.wait("tensor", prev[0], prev[1])
                b.emit("tensor", lambda e, ncx=ncx, p=bank: e.matmul(
                    p[0:2, 0:512], col10[0:64, :],
                    ksq_at[0:64, ncx*512:(ncx+1)*512],
                    start=True, stop=True), inc="T")
                b.wait("vector", "T", b.cnt["T"])
                tgt = b.emit("vector", lambda e, ncx=ncx, p=bank:
                             e.tensor_scalar(
                                 out=Qa[64:66, ncx*512:(ncx+1)*512],
                                 in0=p[0:2, 0:512],
                                 scalar1=cst[64:66, 521:522],
                                 scalar2=cst[64:66, 522:523],
                                 op0=ALU.mult, op1=ALU.add), inc="V")
                evac_after_bank[bi] = ("V", tgt)
            load_wT(W["l3_wk"][64*h:64*h+64, :], wtp3, rows=64)
            for ncx in range(4):
                def evk3(e, p, ncx=ncx, r0=r0, dk=dk):
                    return e.tensor_scalar(
                        out=Ka[0:64, ncx*512:(ncx+1)*512], in0=p,
                        scalar1=bkc[r0:r0+64, dk:dk+1], scalar2=None,
                        op0=ALU.add)
                mm_group([(wtp3[:, kt, 0:64],
                           xnT[:, kt, ncx*512:(ncx+1)*512])
                          for kt in range(8)], 64, 512, evk3, "vector")
            wrow_done(sidx3[0] - 1, b.cnt["T"])
            kS = b.cnt["V"]
            b.wait("scalar", "V", kS)
            b.wait("scalar", "T", b.cnt["T"])
            b.emit("scalar", lambda e: e.activation(
                out=ksq_at[0:64, :], in_=Ka[0:64, :], func=AF.Square),
                inc="S")
            b.wait("tensor", "S", b.cnt["S"])
            for ncx in range(4):
                bi = pbi[0] % POOLN[0]
                pbi[0] += 1
                bank = pb[bi]
                prev = evac_after_bank.get(bi)
                if prev is not None:
                    b.wait("tensor", prev[0], prev[1])
                b.emit("tensor", lambda e, ncx=ncx, p=bank: e.matmul(
                    p[0:2, 0:512], col01[0:64, :],
                    ksq_at[0:64, ncx*512:(ncx+1)*512],
                    start=True, stop=True), inc="T")
                b.wait("vector", "T", b.cnt["T"])
                tgt = b.emit("vector", lambda e, ncx=ncx, p=bank:
                             e.tensor_scalar(
                                 out=Ka[64:66, ncx*512:(ncx+1)*512],
                                 in0=p[0:2, 0:512],
                                 scalar1=cst[64:66, 523:524],
                                 scalar2=cst[64:66, 524:525],
                                 op0=ALU.mult, op1=ALU.add), inc="V")
                evac_after_bank[bi] = ("V", tgt)
            load_wT(W["l3_wv"][64*h:64*h+64, :], wtp3, rows=64)
            for kt in range(16):
                def evv3(e, p, kt=kt):
                    return e.tensor_copy(out=Vp3[:, kt, 0:64], in_=p)
                kparts = [(xnT[:, k2, kt*128:(kt+1)*128], wtp3[:, k2, 0:64])
                          for k2 in range(8)]
                kparts.append((ones_row[0:1, 0:128],
                               bvrow[0:1, 64*h:64*h+64]))
                mm_group(kparts, 128, 64, evv3, "vector")
            wrow_done(sidx3[0] - 1, b.cnt["T"])
            vV3 = b.cnt["V"]
            b.wait("tensor", "V", vV3)
            b.wait("scalar", "S", b.cnt["S"])
            for qc in range(4):
                bi = pbi[0] % POOLN[0]
                pbi[0] += 1
                cbank = pb[bi]
                prev = evac_after_bank.get(bi)
                if prev is not None:
                    b.wait("tensor", prev[0], prev[1])
                ready = {}
                for kt in range(18):
                    if kt <= 15:
                        par = kt % 4
                        bi2 = pbi[0] % POOLN[0]
                        pbi[0] += 1
                        if bi2 == bi:
                            bi2 = (bi2 + 1) % POOLN[0]
                            pbi[0] += 1
                        sbank = pb[bi2]
                        prev2 = evac_after_bank.get(bi2)
                        if prev2 is not None:
                            b.wait("tensor", prev2[0], prev2[1])
                        b.emit("tensor", lambda e, kt=kt, qc=qc, p=sbank:
                               e.matmul(p[:, 0:512],
                                        Ka[0:66, kt*128:(kt+1)*128],
                                        Qa[0:66, qc*512:(qc+1)*512],
                                        start=True, stop=True), inc="T")
                        b.wait("scalar", "T", b.cnt["T"])
                        b.emit("scalar", lambda e, p=sbank, par=par % 2:
                               e.activation(out=sd2[par], in_=p[:, 0:512],
                                            func=AF.Sqrt, bias=1e-12))
                        stgt = b.emit("scalar", lambda e, par=par:
                                      e.activation(
                                          out=ksq_at[:, par*512:par*512+512],
                                          in_=sd2[par % 2], func=AF.Exp,
                                          scale=-1.0), inc="S")
                        evac_after_bank[bi2] = ("S", stgt)
                        ready[kt] = stgt
                    kav = kt - 2
                    if 0 <= kav <= 15:
                        b.wait("tensor", "S", ready[kav])
                        for qm in range(4):
                            b.emit("tensor", lambda e, kav=kav, qm=qm,
                                   par=kav % 4, p=cbank: e.matmul(
                                       p[:, qm*65:qm*65+65],
                                       ksq_at[:, par*512+qm*128:
                                              par*512+(qm+1)*128],
                                       Vp3[:, kav, :],
                                       start=(kav == 0 and qm == 0),
                                       stop=(kav == 15 and qm == 3)),
                                   inc=("T" if (kav == 15 and qm == 3)
                                        else None))
                tTc = b.cnt["T"]
                b.wait("vector", "T", tTc)
                b.emit("vector", lambda e, p=cbank: e.reciprocal(
                    out=rc512, in_=p[:, 64:260:65]), inc="V")
                for qm in range(4):
                    tgt = b.emit("vector", lambda e, p=cbank, qm=qm, qc=qc,
                                 h=h: e.tensor_scalar(
                                     out=ctx[:, qc*4+qm, 64*h:64*h+64],
                                     in0=p[:, qm*65:qm*65+64],
                                     scalar1=rc512[:, qm:qm+1], scalar2=None,
                                     op0=ALU.mult), inc="V")
                evac_after_bank[bi] = ("V", tgt)
        attV = b.cnt["V"]

        for rb in range(8):
            load_wT(W["l3_wo"][rb*128:(rb+1)*128, :],
                    woT.rearrange("p k m -> p k m")[:, :, 0:128]
                    if False else woT, rows=128) if False else None
        for rb in range(8):
            wb2, t4 = wrow_ld(sidx3[0], W["l3_wo"][rb*128:(rb+1)*128, :])
            sidx3[0] += 1
            b.wait("tensor", t4[0], t4[1])
            transpose_tiles([(wb2[:, dc*128:(dc+1)*128],
                              woT[:, dc, rb*128:(rb+1)*128])
                             for dc in range(8)])
            wrow_done(sidx3[0] - 1, b.cnt["T"])
        woTV3 = b.cnt["V"]
        b.wait("tensor", "V", max(attV, woTV3))

        for tt in range(NTT):
            transpose_tiles([(ctx[:, tt, dc*128:(dc+1)*128],
                              ctxT_s[:, dc, :]) for dc in range(8)])
            b.wait("tensor", "V", b.cnt["V"])
            for mt in range(8):
                def evs3(e, p, mt=mt):
                    return e.tensor_scalar(out=s3s[:, mt, :], in0=p,
                                           scalar1=boc[:, mt:mt+1],
                                           scalar2=None, op0=ALU.add)
                mm_group([(woT[:, kt, mt*128:(mt+1)*128], ctxT_s[:, kt, :])
                          for kt in range(8)], 128, 128, evs3, "vector")
            for dkx in range(8):
                if tt == 0:
                    b.emit("vector", lambda e, dkx=dkx: e.tensor_scalar(
                        out=chA[:, dkx, 1:128],
                        in0=xnT[:, dkx, 0:127], scalar1=om_k[:, dkx:dkx+1],
                        scalar2=None, op0=ALU.mult))
                    b.emit("vector", lambda e, dkx=dkx: e.memset(
                        chA[:, dkx, 0:1], 0.0))
                else:
                    b.emit("vector", lambda e, dkx=dkx, tt=tt: e.tensor_scalar(
                        out=chA[:, dkx, 0:128],
                        in0=xnT[:, dkx, tt*128-1:tt*128+127],
                        scalar1=om_k[:, dkx:dkx+1], scalar2=None,
                        op0=ALU.mult))
                b.emit("vector", lambda e, dkx=dkx, tt=tt:
                       e.scalar_tensor_tensor(
                           out=chA[:, dkx, 0:128],
                           in0=xnT[:, dkx, tt*128:(tt+1)*128],
                           scalar=mukc[:, dkx:dkx+1],
                           in1=chA[:, dkx, 0:128],
                           op0=ALU.mult, op1=ALU.add), inc="V")
            xkV = b.cnt["V"]
            b.wait("tensor", "V", xkV)
            for mt in range(8):
                wkb, t7 = wrow_ld(sidx3[0], W["l3_wk"][mt*128:(mt+1)*128, :])
                sidx3[0] += 1
                b.wait("tensor", t7[0], t7[1])
                transpose_tiles([(wkb[:, dc*128:(dc+1)*128],
                                  wtp3[:, dc, :]) for dc in range(8)])
                b.wait("tensor", "V", b.cnt["V"])
                def evkk(e, p, mt=mt):
                    return e.activation(out=chK[:, mt, :], in_=p,
                                        func=AF.Relu)
                mm_group([(wtp3[:, kt, :], chA[:, kt, 0:128])
                          for kt in range(8)], 128, 128, evkk, "scalar")
                wrow_done(sidx3[0] - 1, b.cnt["T"])
            b.wait("vector", "S", b.cnt["S"])
            b.emit("vector", lambda e: e.tensor_mul(
                out=chK[:, :, :], in0=chK[:, :, :], in1=chK[:, :, :]),
                inc="V")
            kaV = b.cnt["V"]
            for dkx in range(8):
                if tt == 0:
                    b.emit("vector", lambda e, dkx=dkx: e.tensor_scalar(
                        out=chA[:, dkx, 1:128],
                        in0=xnT[:, dkx, 0:127], scalar1=om_r[:, dkx:dkx+1],
                        scalar2=None, op0=ALU.mult))
                    b.emit("vector", lambda e, dkx=dkx: e.memset(
                        chA[:, dkx, 0:1], 0.0))
                else:
                    b.emit("vector", lambda e, dkx=dkx, tt=tt: e.tensor_scalar(
                        out=chA[:, dkx, 0:128],
                        in0=xnT[:, dkx, tt*128-1:tt*128+127],
                        scalar1=om_r[:, dkx:dkx+1], scalar2=None,
                        op0=ALU.mult))
                b.emit("vector", lambda e, dkx=dkx, tt=tt:
                       e.scalar_tensor_tensor(
                           out=chA[:, dkx, 0:128],
                           in0=xnT[:, dkx, tt*128:(tt+1)*128],
                           scalar=murc[:, dkx:dkx+1],
                           in1=chA[:, dkx, 0:128],
                           op0=ALU.mult, op1=ALU.add), inc="V")
            xrV = b.cnt["V"]
            b.wait("tensor", "V", xrV)
            for mt in range(8):
                wrb, t8 = wrow_ld(sidx3[0], W["l3_wr"][mt*128:(mt+1)*128, :])
                sidx3[0] += 1
                b.wait("tensor", t8[0], t8[1])
                transpose_tiles([(wrb[:, dc*128:(dc+1)*128],
                                  wtp3[:, dc, :]) for dc in range(8)])
                b.wait("tensor", "V", b.cnt["V"])
                def evrr(e, p, mt=mt):
                    return e.activation(out=chR[:, mt, :], in_=p,
                                        func=AF.Sigmoid)
                mm_group([(wtp3[:, kt, :], chA[:, kt, 0:128])
                          for kt in range(8)], 128, 128, evrr, "scalar")
                wrow_done(sidx3[0] - 1, b.cnt["T"])
            rS = b.cnt["S"]
            b.wait("tensor", "S", rS)
            b.wait("tensor", "V", kaV)
            for mt in range(8):
                wvb, t9 = wrow_ld(sidx3[0], W["l3_wv"][mt*128:(mt+1)*128, :])
                sidx3[0] += 1
                b.wait("tensor", t9[0], t9[1])
                transpose_tiles([(wvb[:, dc*128:(dc+1)*128],
                                  wtp3[:, dc, :]) for dc in range(8)])
                b.wait("tensor", "V", b.cnt["V"])
                bi = pbi[0] % POOLN[0]
                pbi[0] += 1
                bank = pb[bi]
                prev = evac_after_bank.get(bi)
                if prev is not None:
                    b.wait("tensor", prev[0], prev[1])
                for kt in range(8):
                    b.emit("tensor", lambda e, kt=kt, p=bank: e.matmul(
                        p[:, 0:128], wtp3[:, kt, :], chK[:, kt, :],
                        start=(kt == 0), stop=(kt == 7)),
                        inc=("T" if kt == 7 else None))
                wrow_done(sidx3[0] - 1, b.cnt["T"])
                tTk = b.cnt["T"]
                b.wait("vector", "T", tTk)
                b.emit("vector", lambda e, p=bank, mt=mt: e.tensor_mul(
                    out=g3b, in0=p[:, 0:128], in1=chR[:, mt, :]), inc="V")
                gv = b.cnt["V"]
                evac_after_bank[bi] = ("V", gv)
                b.wait("scalar", "V", gv)
                b.emit("scalar", lambda e: e.activation(
                    out=g3b, in_=g3b, func=AF.Silu), inc="S")
                gs = b.cnt["S"]
                b.wait("vector", "S", gs)
                b.emit("vector", lambda e, mt=mt: e.tensor_mul(
                    out=g3b, in0=g3b, in1=s3s[:, mt, :]), inc="V")
                pv = b.cnt["V"]
                b.wait("tensor", "V", pv)
                sslot = tp_state["slot"] % 2
                tp_state["slot"] += 1
                if len(tp_state["evacs"]) >= 2:
                    b.wait("tensor", "V", tp_state["evacs"][-2])
                tT2 = b.emit("tensor", lambda e, sslot=sslot: e.transpose(
                    ptp2[sslot][:, 0:128], g3b, ident), inc="T")
                b.wait("vector", "T", tT2)
                vv = b.emit("vector", lambda e, sslot=sslot, tt=tt, mt=mt:
                            e.tensor_add(
                                out=xres[:, tt, mt*128:(mt+1)*128],
                                in0=xres[:, tt, mt*128:(mt+1)*128],
                                in1=ptp2[sslot][:, 0:128]), inc="V")
                tp_state["evacs"].append(vv)
        b.barrier()

    if 0 in active:
        layer0()
    if 1 in active:
        layer1()
    if 2 in active:
        layer2()
    if 3 in active:
        layer3()

    b.barrier()
    stgt = dma("sync", y_out.rearrange("(tt p) d -> p tt d", p=128),
               xres[:, :, :], "ST")
    b.wait("sync", "ST", stgt)

    with nc.allow_low_precision(
            reason="bf16 moe accumulation within validated tolerance"), \
         nc.Block() as block:
        @block.sync
        def _(e):
            for f in b.prog["sync"]:
                f(e)

        @block.tensor
        def _(e):
            for f in b.prog["tensor"]:
                f(e)

        @block.vector
        def _(e):
            for f in b.prog["vector"]:
                f(e)

        @block.scalar
        def _(e):
            for f in b.prog["scalar"]:
                f(e)

        @block.gpsimd
        def _(e):
            for f in b.prog["gpsimd"]:
                f(e)

    es.close()
    return nc


def make_consts_i():
    return np.tile((1 << np.arange(32, dtype=np.int64)).astype(np.int32)[None, :],
                   (8, 1))


def make_consts():
    c = np.zeros((128, 640), np.float32)
    c[:, 0:512] = np.arange(512, dtype=np.float32)[None, :]
    c[:, 512] = np.arange(128, dtype=np.float32)
    c[:, 513] = 1.0
    c[:, 514] = 1e-5
    c[:, 515] = 0.0
    c[:, 516] = -1.0
    c[:, 517] = 1e-12
    c[:, 518] = 0.125
    c[:, 519] = -2.0
    c[:, 520] = -1e30
    c[64, 521] = 0.25
    c[65, 521] = 0.0
    c[64, 522] = 0.0
    c[65, 522] = 1.0
    c[64, 523] = 0.0
    c[65, 523] = 1.0
    c[64, 524] = 1.0
    c[65, 524] = 0.0
    return c


def kernel(**inputs):
    nc = build_program()
    consts = make_consts()
    x = np.ascontiguousarray(inputs["L0_attn_x"])
    in_maps = []
    for core in range(8):
        m = {"x": x[core], "consts": consts, "consts_i": make_consts_i()}
        for k, v in inputs.items():
            if k == "L0_attn_x":
                continue
            m[k] = np.ascontiguousarray(v)
        in_maps.append(m)
    res = run_bass_kernel_spmd(nc, in_maps, core_ids=list(range(8)))
    return np.stack([r["y"] for r in res.results], axis=0)


# version 08
# speedup vs baseline: 1.1842x; 1.0258x over previous
import os
import numpy as np
from contextlib import ExitStack
import concourse.bass as bass
import concourse.mybir as mybir
from concourse.bass_utils import run_bass_kernel_spmd

F32 = mybir.dt.float32
BF16 = mybir.dt.bfloat16
I32 = mybir.dt.int32
AF = mybir.ActivationFunctionType
ALU = mybir.AluOpType
AX = mybir.AxisListType

D = 1024
T = 2048
DI = 2048
DS = 16
DTR = 64
NH = 16
HD = 64
TS = 256
NSL = T // TS
NTT = T // 128

ENGS = ("sync", "tensor", "vector", "scalar", "gpsimd")
POOLN = [4]


class Builder:
    def __init__(self, nc, es):
        self.nc = nc
        self.prog = {e: [] for e in ENGS}
        self.semh = {}
        self.cnt = {}
        for s in ("bar", "T", "V", "S", "G", "WD", "XD", "ST",
                  "D0", "D1", "D2", "D3", "D4", "D5", "D6", "D7"):
            self.semh[s] = es.enter_context(nc.semaphore(s))
            self.cnt[s] = 0

    ESEM = {"tensor": "T", "vector": "V", "scalar": "S", "gpsimd": "G"}

    def emit(self, eng, fn, inc=None, n=1):
        esem = self.ESEM.get(eng)
        chain = esem is not None and inc in (None, esem)
        if chain:
            prev = self.cnt[esem]
            eh = self.semh[esem]
            if prev > 0:
                self.prog[eng].append(
                    lambda e, h=eh, v=prev: e.wait_ge(h, v))
            self.cnt[esem] += 1
            tgt = self.cnt[esem]

            def run(e, fn=fn, h=eh):
                fn(e).then_inc(h, 1)
            self.prog[eng].append(run)
            return tgt
        h = self.semh.get(inc)
        if inc is not None:
            self.cnt[inc] += n
        tgt = self.cnt.get(inc)

        def run(e, fn=fn, h=h, n=n, inc=inc):
            r = fn(e)
            if inc is not None:
                r.then_inc(h, n)
        self.prog[eng].append(run)
        return tgt

    def wait(self, eng, sem, val):
        if val <= 0:
            return
        h = self.semh[sem]
        self.prog[eng].append(lambda e, h=h, v=val: e.wait_ge(h, v))

    def barrier(self):
        self.cnt["bar"] += 5
        tgt = self.cnt["bar"]
        h = self.semh["bar"]
        for eng in ENGS:
            es_ = self.ESEM.get(eng)
            if es_ is not None and self.cnt[es_] > 0:
                eh, ev = self.semh[es_], self.cnt[es_]
                self.prog[eng].append(lambda e, h2=eh, v=ev: e.wait_ge(h2, v))
            self.prog[eng].append(lambda e, h=h: e.sem_inc(h, 1))
            self.prog[eng].append(lambda e, h=h, v=tgt: e.wait_ge(h, v))


class ColArena:
    def __init__(self, tens, cols, elem=2):
        self.t = tens
        self.cols = cols
        self.off = 0
        self.elem = elem

    def alloc(self, cols, pattern=None, **kw):
        o = self.off
        self.off += (cols + 15) // 16 * 16
        assert self.off <= self.cols, f"arena overflow {self.off}>{self.cols}"
        v = self.t[:, o:o + cols]
        if pattern:
            v = v.rearrange(pattern, **kw)
        return v

    def alloc_f32(self, cols, pattern=None, **kw):
        o = self.off
        self.off += (2 * cols + 15) // 16 * 16
        assert self.off <= self.cols, f"arena overflow {self.off}>{self.cols}"
        v = self.t[:, o:o + 2 * cols].bitcast(F32)
        if pattern:
            v = v.rearrange(pattern, **kw)
        return v

    def reset(self):
        self.off = 0


def build_program(active=(0, 1, 2, 3)):
    nc = bass.Bass("TRN2", target_bir_lowering=False)
    es = ExitStack()

    def din(name, shape, dtype=F32):
        return nc.dram_tensor(name, shape, dtype, kind="ExternalInput").ap()

    x_in = din("x", [T, D])
    W = {}
    W["l0_win"] = din("L0_attn_W_in", [D, 2 * DI])
    W["l0_convw"] = din("L0_attn_conv_w", [DI, 1, 4])
    W["l0_convb"] = din("L0_attn_conv_b", [DI])
    W["l0_wx"] = din("L0_attn_W_x", [DI, DTR + 2 * DS])
    W["l0_wdt"] = din("L0_attn_W_dt", [DTR, DI])
    W["l0_bdt"] = din("L0_attn_b_dt", [DI])
    W["l0_alog"] = din("L0_attn_A_log", [DI, DS])
    W["l0_d"] = din("L0_attn_D", [DI])
    W["l0_wout"] = din("L0_attn_W_out", [DI, D])
    W["l0_u1"] = din("L0_ffn_U1", [D, 256])
    W["l0_v1"] = din("L0_ffn_V1", [256, 4096])
    W["l0_b1"] = din("L0_ffn_b1", [4096])
    W["l0_u2"] = din("L0_ffn_U2", [4096, 256])
    W["l0_v2"] = din("L0_ffn_V2", [256, D])
    W["l0_b2"] = din("L0_ffn_b2", [D])
    for nm in ("q_l", "k_l", "v_l", "o_l", "q_g", "k_g", "v_g", "o_g"):
        W["l1_w" + nm] = din("L1_attn_W" + nm, [D, D])
    W["l1_wr"] = din("L1_ffn_Wr", [8, D])
    W["l1_we"] = din("L1_ffn_We", [8, D, D])
    W["l2_wqkv"] = din("L2_attn_w_qkv", [3 * D, D])
    W["l2_wout"] = din("L2_attn_w_out", [D, D])
    W["l2_gw"] = din("L2_ffn_gate_w", [8, D])
    W["l2_w1"] = din("L2_ffn_w1", [8, D, 4096])
    W["l2_w2"] = din("L2_ffn_w2", [8, 4096, D])
    for nm in ("q", "k", "v", "o"):
        W["l3_w" + nm] = din("L3_attn_w" + nm, [D, D])
        W["l3_b" + nm] = din("L3_attn_b" + nm, [D])
    W["l3_wr"] = din("L3_ffn_W_r", [D, D])
    W["l3_wk"] = din("L3_ffn_W_k", [D, D])
    W["l3_wv"] = din("L3_ffn_W_v", [D, D])
    W["l3_mur"] = din("L3_ffn_mu_r", [D])
    W["l3_muk"] = din("L3_ffn_mu_k", [D])
    consts = din("consts", [128, 640])
    consts_i = din("consts_i", [8, 32], I32)
    y_out = nc.dram_tensor("y", [T, D], F32, kind="ExternalOutput").ap()

    sbt = lambda name, shape, dt: es.enter_context(nc.sbuf_tensor(name, shape, dt))
    xres = sbt("xres", [128, NTT, D], F32)
    arena_t = sbt("arena", [128, 40960], BF16)
    stat_t = sbt("stat", [128, 3584], F32)
    cst = sbt("cst", [128, 640], F32)
    cstb = sbt("cstb", [128, 512], BF16)
    wbuf_t = sbt("wbuf", [128, 10240], BF16)
    pers_t = sbt("pers", [128, 8704], BF16)

    pb = [es.enter_context(nc.psum_tensor(f"pb{i}", [128, 512], F32))
          for i in range(5)]
    ptp2 = [es.enter_context(nc.psum_tensor(f"ptp{i}", [128, 512], BF16))
            for i in range(2)]
    pmisc = es.enter_context(nc.psum_tensor("pmisc", [128, 512], F32))

    b = Builder(nc, es)
    AR = ColArena(arena_t, 40960)
    STA = ColArena(stat_t, 3584 * 2)

    class F32Arena:
        def __init__(self):
            self.off = 0

        def alloc(self, cols, pattern=None, **kw):
            o = self.off
            self.off += (cols + 7) // 8 * 8
            assert self.off <= 3584, f"stat overflow {self.off}"
            v = stat_t[:, o:o + cols]
            if pattern:
                v = v.rearrange(pattern, **kw)
            return v

        def reset(self):
            self.off = 0

    STA = F32Arena()
    WB = ColArena(wbuf_t, 10240)
    PSW = ColArena(pers_t, 8704)

    def dma(queue, out_ap, in_ap, sem="WD", slow_ok=False):
        return b.emit(queue, lambda e, o=out_ap, i=in_ap, sk=slow_ok: e.dma_start(
            out=o, in_=i, allow_slow_non_contiguous=sk), inc=sem, n=16)

    ident = cstb[:, 0:128]
    ones_row = cstb[:, 128:256]
    iota_f = cst[:, 0:512]

    nc.const_aps.register(F32, 1.0, cst[:, 513:514])
    nc.const_aps.register(F32, 1e-5, cst[:, 514:515])
    nc.const_aps.register(F32, 0.0, cst[:, 515:516])
    nc.const_aps.register(F32, -1.0, cst[:, 516:517])
    nc.const_aps.register(F32, 1e-12, cst[:, 517:518])
    nc.const_aps.register(F32, 0.125, cst[:, 518:519])
    nc.const_aps.register(F32, -2.0, cst[:, 519:520])
    nc.const_aps.register(F32, -1e30, cst[:, 520:521])
    csti = sbt("csti", [8, 32], I32)
    dma("sync", csti[:, :], consts_i, "XD")
    dma("sync", cst[:, :], consts, "XD")
    dma("sync", xres[:, :, :], x_in.rearrange("(tt p) d -> p tt d", p=128), "XD")
    XD0 = b.cnt["XD"]

    ustrict = cstb[:, 256:384]
    ones_col = cstb[:, 384:385]
    col10 = cstb[:, 385:387]
    col01 = cstb[:, 387:389]
    b.emit("vector", lambda e: e.memset(cstb[:, :], 0.0), inc="V")
    b.emit("vector", lambda e: e.memset(ident, 1.0), inc="V")
    b.emit("vector", lambda e: e.memset(ones_row[0:1, :], 1.0), inc="V")
    b.emit("vector", lambda e: e.memset(ustrict, 1.0), inc="V")
    b.emit("vector", lambda e: e.memset(ones_col, 1.0), inc="V")
    b.emit("vector", lambda e: e.memset(col10[:, 0:1], 1.0), inc="V")
    b.emit("vector", lambda e: e.memset(col01[:, 1:2], 1.0), inc="V")
    vsetup = b.cnt["V"]
    b.wait("gpsimd", "V", vsetup)
    b.emit("gpsimd", lambda e: e.affine_select(
        out=ident, in_=ident, pattern=[[-1, 128]], compare_op=ALU.is_equal,
        fill=0.0, channel_multiplier=1), inc="G")
    b.emit("gpsimd", lambda e: e.affine_select(
        out=ustrict, in_=ustrict, pattern=[[1, 128]], compare_op=ALU.is_gt,
        fill=0.0, channel_multiplier=-1), inc="G")
    gsetup = b.cnt["G"]
    for eng in ("vector", "scalar", "tensor"):
        b.wait(eng, "XD", XD0)
        b.wait(eng, "G", gsetup)

    pbi = [0]
    evac_after_bank = {}

    def mm_group(kparts, out_rows, out_cols, evac_fn, evac_eng,
                 bank=None, start=True, stop=True, inc_last=True):
        if bank is None:
            bi = pbi[0] % POOLN[0]
            pbi[0] += 1
            bank = pb[bi]
            prev = evac_after_bank.get(bi)
            if prev is not None:
                b.wait("tensor", prev[0], prev[1])
        else:
            bi = None
        n = len(kparts)
        for i, (lh, rh) in enumerate(kparts):
            st_ = start and (i == 0)
            sp_ = stop and (i == n - 1)
            do_inc = inc_last and (i == n - 1)
            b.emit("tensor",
                   lambda e, lh=lh, rh=rh, st_=st_, sp_=sp_, p=bank,
                   r=out_rows, c=out_cols: e.matmul(
                       p[0:r, 0:c], lh, rh, start=st_, stop=sp_),
                   inc=("T" if do_inc else None))
        if evac_fn is None:
            return None
        tT = b.cnt["T"]
        b.wait(evac_eng, "T", tT)
        sem = {"vector": "V", "scalar": "S"}[evac_eng]
        tgt = b.emit(evac_eng,
                     lambda e, p=bank, r=out_rows, c=out_cols: evac_fn(
                         e, p[0:r, 0:c]), inc=sem)
        if bi is not None:
            evac_after_bank[bi] = (sem, tgt)
        return tgt

    tp_state = {"slot": 0, "evacs": []}

    def transpose_tiles(pairs, out_op="copy", idn=128):
        last = None
        for (srcp, dstp) in pairs:
            s = tp_state["slot"] % 2
            tp_state["slot"] += 1
            if len(tp_state["evacs"]) >= 2:
                b.wait("tensor", "V", tp_state["evacs"][-2])
            tT = b.emit("tensor", lambda e, s=s, a=srcp, idn=idn: e.transpose(
                ptp2[s][:, 0:idn],
                a, ident[0:idn, 0:idn]), inc="T")
            b.wait("vector", "T", tT)
            if out_op == "copy":
                vv = b.emit("vector", lambda e, s=s, d=dstp, idn=idn:
                            e.tensor_copy(out=d, in_=ptp2[s][:, 0:idn]),
                            inc="V")
            else:
                vv = b.emit("vector", lambda e, s=s, d=dstp, idn=idn:
                            e.tensor_add(out=d, in0=d,
                                         in1=ptp2[s][:, 0:idn]), inc="V")
            tp_state["evacs"].append(vv)
            last = vv
        return last

    def layernorm_stats(sq_scr):
        sums = STA.alloc(NTT)
        sumsq = STA.alloc(NTT)
        mcol = STA.alloc(NTT)
        rstd = STA.alloc(NTT)
        for tt in range(NTT):
            b.emit("vector", lambda e, tt=tt: e.reduce_sum(
                out=sums[:, tt:tt+1], in_=xres[:, tt, :], axis=AX.X), inc="V")
            b.emit("scalar", lambda e, tt=tt: e.activation(
                out=sq_scr, in_=xres[:, tt, :], func=AF.Square,
                accum_out=sumsq[:, tt:tt+1]), inc="S")
        b.wait("vector", "S", b.cnt["S"])
        b.emit("vector", lambda e: e.tensor_scalar(
            out=mcol, in0=sums, scalar1=1.0 / D, scalar2=None, op0=ALU.mult))
        b.emit("vector", lambda e: e.tensor_mul(out=rstd, in0=mcol, in1=mcol))
        b.emit("vector", lambda e: e.tensor_scalar(
            out=sumsq, in0=sumsq, scalar1=1.0 / D, scalar2=None, op0=ALU.mult))
        b.emit("vector", lambda e: e.tensor_sub(out=rstd, in0=sumsq, in1=rstd),
               inc="V")
        b.wait("scalar", "V", b.cnt["V"])
        b.emit("scalar", lambda e: e.activation(out=rstd, in_=rstd, func=AF.Sqrt,
                                                bias=1e-5), inc="S")
        b.wait("vector", "S", b.cnt["S"])
        b.emit("vector", lambda e: e.reciprocal(out=rstd, in_=rstd), inc="V")
        return mcol, rstd

    def ln_apply(mcol, rstd, tt, dst):
        return b.emit("vector", lambda e, tt=tt, d=dst: e.tensor_scalar(
            out=d, in0=xres[:, tt, :], scalar1=mcol[:, tt:tt+1],
            scalar2=rstd[:, tt:tt+1], op0=ALU.subtract, op1=ALU.mult), inc="V")

    stream_sem_pool = ["D0", "D1", "D2", "D3", "D4", "D5", "D6", "D7"]

    def make_stream(buf2):
        sems = (stream_sem_pool.pop(0), stream_sem_pool.pop(0))
        state = {"last": [0, 0]}

        def load(idx, dram_ap, reshape=None, rows=128):
            bufv = buf2[0:rows, idx % 2, :]
            if reshape:
                bufv = bufv.rearrange(reshape[0], **reshape[1])
            b.wait("gpsimd", "T", state["last"][idx % 2])
            return bufv, (sems[idx % 2],
                          dma("gpsimd", bufv, dram_ap, sems[idx % 2]))

        def consumed(idx, ttarget):
            state["last"][idx % 2] = ttarget
        return load, consumed

    def layer0():
        POOLN[0] = 4
        PRO = int(os.environ.get("L0_PRO", "9"))
        AR.reset(); STA.reset(); WB.reset()
        wx = PSW.alloc(16 * 96, "p (k m) -> p k m", k=16)
        wdt = PSW.alloc(16 * 128, "p (m q) -> p m q", m=16)
        u1 = PSW.alloc(8 * 256, "p (k m) -> p k m", k=8)
        v2 = PSW.alloc(2 * 1024, "p (k m) -> p k m", k=2)
        convw = PSW.alloc_f32(16 * 4, "p (k j) -> p k j", k=16)
        convb = PSW.alloc_f32(16)
        bdt = PSW.alloc_f32(16)
        a_sb = PSW.alloc_f32(16 * 16, "p (k s) -> p k s", k=16)
        d_sb = PSW.alloc_f32(16)
        b1c = PSW.alloc_f32(32)
        b2c = PSW.alloc_f32(8)

        if PRO < 1:
            return
        dma("gpsimd", wx, W["l0_wx"].rearrange("(k p) m -> p k m", p=128))
        dma("gpsimd", wdt[0:64, :, :],
            W["l0_wdt"].rearrange("c (m q) -> c m q", q=128))
        dma("gpsimd", u1, W["l0_u1"].rearrange("(k p) m -> p k m", p=128))
        dma("gpsimd", v2, W["l0_v2"].rearrange("(k p) m -> p k m", p=128))
        dma("gpsimd", convw,
            W["l0_convw"].rearrange("(k p) one j -> p (k one) j", p=128))
        dma("gpsimd", convb, W["l0_convb"].rearrange("(k p) -> p k", p=128), slow_ok=True)
        dma("gpsimd", bdt, W["l0_bdt"].rearrange("(k p) -> p k", p=128), slow_ok=True)
        dma("gpsimd", a_sb, W["l0_alog"].rearrange("(k p) s -> p k s", p=128))
        dma("gpsimd", d_sb, W["l0_d"].rearrange("(k p) -> p k", p=128), slow_ok=True)
        dma("gpsimd", b1c, W["l0_b1"].rearrange("(m p) -> p m", p=128), slow_ok=True)
        wWD = dma("gpsimd", b2c, W["l0_b2"].rearrange("(m p) -> p m", p=128), slow_ok=True)

        if PRO < 2:
            return
        b.wait("scalar", "WD", wWD)
        b.emit("scalar", lambda e: e.activation(out=a_sb[:, :, :],
                                                in_=a_sb[:, :, :], func=AF.Exp),
               inc="S")
        b.wait("vector", "S", b.cnt["S"])
        b.emit("vector", lambda e: e.tensor_scalar(
            out=a_sb[:, :, :], in0=a_sb[:, :, :], scalar1=-1.0, scalar2=None,
            op0=ALU.mult), inc="V")

        win_buf = WB.alloc(2 * 2048, "p (b2 m) -> p b2 m", b2=2)
        wout_buf = WB.alloc(2 * 2048, "p (b2 m) -> p b2 m", b2=2)
        v1_buf = WB.alloc(2 * 256, "p (b2 m) -> p b2 m", b2=2)
        u2_buf = WB.alloc(2 * 256, "p (b2 m) -> p b2 m", b2=2)
        win_ld, win_done = make_stream(win_buf)
        wout_ld, wout_done = make_stream(wout_buf)
        v1_ld, v1_done = make_stream(v1_buf)
        u2_ld, u2_done = make_stream(u2_buf)

        xn = AR.alloc(2 * 1024, "p (tt d) -> p tt d", tt=2)
        sq_scr = xn[:, 0, :]
        xnT = AR.alloc(8 * TS, "p (k t) -> p k t", k=8)
        xi = AR.alloc(16 * (TS + 3), "p (c t) -> p c t", c=16)
        gz = AR.alloc(16 * TS, "p (c t) -> p c t", c=16)
        brep = AR.alloc(16 * TS, "p (s t) -> p s t", s=16)
        crep = AR.alloc(TS * 16, "p (t s) -> p t s", t=TS)
        Hbuf = AR.alloc(TS * 16, "p (t s) -> p t s", t=TS)
        ybuf = AR.alloc(16 * TS, "p (c t) -> p c t", c=16)
        m8 = AR.alloc(TS * 8, "p (t s) -> p t s", t=TS)
        sT = AR.alloc(8 * TS, "p (k t) -> p k t", k=8)
        gT = AR.alloc(8 * TS, "p (k t) -> p k t", k=8)
        pT = AR.alloc(2 * TS, "p (k t) -> p k t", k=2)
        qbf = pT
        hT = AR.alloc(TS)
        hstate = AR.alloc(16 * 16, "p (c s) -> p c s", c=16)
        dtx = AR.alloc(TS)
        xp_bf = AR.alloc(TS)
        bcflat = AR.alloc(16 * TS)
        prod = ybuf[:, 0:8, :]
        xp_f = STA.alloc(TS)
        dt_f = STA.alloc(TS)
        da2 = [STA.alloc(TS), STA.alloc(TS), STA.alloc(TS), STA.alloc(TS)]
        cacc2 = [STA.alloc(TS), STA.alloc(TS)]

        b.emit("gpsimd", lambda e: e.memset(hstate[:, :, :], 0.0))
        b.emit("gpsimd", lambda e: e.memset(xi[:, :, 0:3], 0.0), inc="G")
        b.wait("vector", "G", b.cnt["G"])
        b.wait("tensor", "G", b.cnt["G"])

        if PRO < 3:
            return
        mcol, rstd = layernorm_stats(sq_scr)

        if PRO < 4:
            return
        PH = int(os.environ.get("L0_PHASE", "9"))
        NS2 = int(os.environ.get("L0_NSL", str(NSL)))
        for sl in range(NS2):
            tt0 = sl * TS // 128
            ntt = TS // 128
            for j in range(ntt):
                ln_apply(mcol, rstd, tt0 + j, xn[:, j, :])
            lnV = b.cnt["V"]
            b.wait("tensor", "V", lnV)
            if not os.environ.get("L0_NOTP"):
                transpose_tiles([(xn[:, j, dc*128:(dc+1)*128],
                                  xnT[:, dc, j*128:(j+1)*128])
                                 for j in range(ntt) for dc in range(8)])
            xnT_V = b.cnt["V"]

            if PH < 1:
                continue
            for mt2 in range(16):
                bufv, wtgt = win_ld(mt2, W["l0_win"][:, mt2*256:(mt2+1)*256]
                                    .rearrange("(k p) q -> p k q", p=128),
                                    ("p (k q) -> p k q", dict(k=8)))
                b.wait("tensor", wtgt[0], wtgt[1])
                if mt2 == 0:
                    b.wait("tensor", "V", xnT_V)
                for half2 in range(2):
                    mt = mt2 * 2 + half2
                    if mt < 16:
                        def ev(e, p, ct=mt):
                            return e.activation(out=xi[:, ct, 3:3+TS], in_=p,
                                                func=AF.Copy)
                    else:
                        def ev(e, p, ct=mt-16):
                            return e.activation(out=gz[:, ct, :], in_=p,
                                                func=AF.Silu)
                    mm_group([(bufv[:, kt, half2*128:(half2+1)*128],
                               xnT[:, kt, :]) for kt in range(8)],
                             128, TS, ev, "scalar")
                win_done(mt2, b.cnt["T"])
            xzS = b.cnt["S"]

            if PH < 2:
                continue
            b.wait("vector", "S", xzS)
            silu_tgts = []
            for ct in range(16):
                cacc = cacc2[ct % 2]
                if len(silu_tgts) >= 2:
                    b.wait("vector", "S", silu_tgts[-2])
                b.emit("vector", lambda e, ct=ct, cacc=cacc: e.tensor_scalar(
                    out=cacc, in0=xi[:, ct, 0:TS], scalar1=convw[:, ct, 0:1],
                    scalar2=None, op0=ALU.mult))
                for j in (1, 2, 3):
                    b.emit("vector", lambda e, ct=ct, j=j, cacc=cacc:
                           e.scalar_tensor_tensor(
                               out=cacc, in0=xi[:, ct, j:j+TS],
                               scalar=convw[:, ct, j:j+1], in1=cacc,
                               op0=ALU.mult, op1=ALU.add))
                b.emit("vector", lambda e, ct=ct: e.tensor_copy(
                    out=xi[:, ct, 0:3], in_=xi[:, ct, TS:TS+3]), inc="V")
                b.wait("scalar", "V", b.cnt["V"])
                stg = b.emit("scalar", lambda e, ct=ct, cacc=cacc: e.activation(
                    out=xi[:, ct, 3:3+TS], in_=cacc, func=AF.Silu,
                    bias=convb[:, ct:ct+1]), inc="S")
                silu_tgts.append(stg)
            convS = b.cnt["S"]

            if PH < 3:
                continue
            b.wait("tensor", "S", convS)
            b.wait("vector", "WD", b.cnt["WD"])

            def ev_xp(e, p):
                return e.tensor_copy(out=xp_bf[0:96, :], in_=p)
            xpV = mm_group([(wx[:, kt, :][:, 0:96], xi[:, kt, 3:3+TS])
                            for kt in range(16)], 96, TS, ev_xp, "vector")
            b.wait("gpsimd", "V", xpV)
            b.wait("gpsimd", "T", b.cnt["T"])
            bct = dma("gpsimd",
                      bcflat[0:1, :].rearrange("q (r t) -> q r t", r=16),
                      xp_bf[64:80, :].rearrange("r (q t) -> r q t", q=1), "WD")

            if PH < 4:
                continue
            for mt in range(2):
                def ev_pm(e, p, mt=mt):
                    return e.tensor_copy(out=pT[:, mt, :], in_=p)
                mm_group([(u1[:, kt, mt*128:(mt+1)*128], xnT[:, kt, :])
                          for kt in range(8)], 128, TS, ev_pm, "vector")
            pV = b.cnt["V"]
            for mt in range(32):
                v1b, vtgt = v1_ld(mt, W["l0_v1"][:, mt*128:(mt+1)*128]
                                  .rearrange("(k p) q -> p k q", p=128),
                                  ("p (k q) -> p k q", dict(k=2)))
                u2b, utgt = u2_ld(mt, W["l0_u2"][mt*128:(mt+1)*128, :])
                b.wait("tensor", vtgt[0], vtgt[1])
                b.wait("tensor", utgt[0], utgt[1])
                if mt == 0:
                    b.wait("tensor", "V", pV)

                def ev_h(e, p, mt=mt):
                    return e.activation(out=hT, in_=p, func=AF.Gelu_apprx_tanh,
                                        bias=b1c[:, mt:mt+1])
                mm_group([(v1b[:, kt, :], pT[:, kt, :]) for kt in range(2)],
                         128, TS, ev_h, "scalar")
                v1_done(mt, b.cnt["T"])
                hS = b.cnt["S"]
                b.wait("tensor", "S", hS)
                b.emit("tensor", lambda e, mt=mt, u2b=u2b: e.matmul(
                    pb[4][:, 0:TS], u2b[:, 0:128], hT,
                    start=(mt == 0), stop=(mt == 31)))
                b.emit("tensor", lambda e, mt=mt, u2b=u2b: e.matmul(
                    pmisc[:, 0:TS], u2b[:, 128:256], hT,
                    start=(mt == 0), stop=(mt == 31)),
                    inc=("T" if mt == 31 else None))
                u2_done(mt, b.cnt["T"])
            qT = b.cnt["T"]
            b.wait("vector", "T", qT)
            b.emit("vector", lambda e: e.tensor_copy(out=qbf[:, 0, :],
                                                     in_=pb[4][:, 0:TS]))
            b.emit("vector", lambda e: e.tensor_copy(out=qbf[:, 1, :],
                                                     in_=pmisc[:, 0:TS]), inc="V")
            qV = b.cnt["V"]
            b.wait("tensor", "V", qV)
            for mt in range(8):
                def ev_g(e, p, mt=mt):
                    return e.activation(out=gT[:, mt, :], in_=p, func=AF.Silu,
                                        bias=b2c[:, mt:mt+1])
                mm_group([(v2[:, kt, mt*128:(mt+1)*128], qbf[:, kt, :])
                          for kt in range(2)], 128, TS, ev_g, "scalar")
            gS = b.cnt["S"]

            if PH < 5:
                continue
            b.wait("tensor", "WD", b.cnt["WD"])
            for g2 in range(8):
                bi = pbi[0] % POOLN[0]
                pbi[0] += 1
                bank = pb[bi]
                prev = evac_after_bank.get(bi)
                if prev is not None:
                    b.wait("tensor", prev[0], prev[1])
                b.emit("tensor", lambda e, g2=g2, p=bank: e.matmul(
                    p[:, 0:TS], ones_row[0:1, :], bcflat[0:1, (2*g2)*TS:(2*g2+1)*TS],
                    start=True, stop=True))
                b.emit("tensor", lambda e, g2=g2, p=bank: e.matmul(
                    p[:, TS:2*TS], ones_row[0:1, :],
                    bcflat[0:1, (2*g2+1)*TS:(2*g2+2)*TS],
                    start=True, stop=True), inc="T")
                b.wait("vector", "T", b.cnt["T"])
                tgt = b.emit("vector", lambda e, g2=g2, p=bank: e.tensor_copy(
                    out=brep[:, 2*g2:2*g2+2, :].rearrange("p s t -> p (s t)"),
                    in_=p[:, 0:2*TS]), inc="V")
                evac_after_bank[bi] = ("V", tgt)
            lastBmm = b.cnt["T"]
            b.wait("gpsimd", "T", lastBmm)
            cflat_tgt = dma("gpsimd",
                            bcflat[0:1, :].rearrange("q (r t) -> q r t", r=16),
                            xp_bf[80:96, :].rearrange("r (q t) -> r q t", q=1),
                            "WD")
            b.wait("tensor", "WD", b.cnt["WD"])
            for s in range(16):
                bi = pbi[0] % POOLN[0]
                pbi[0] += 1
                bank = pb[bi]
                prev = evac_after_bank.get(bi)
                if prev is not None:
                    b.wait("tensor", prev[0], prev[1])
                b.emit("tensor", lambda e, s=s, p=bank: e.matmul(
                    p[:, 0:TS], ones_row[0:1, :], bcflat[0:1, s*TS:(s+1)*TS],
                    start=True, stop=True), inc="T")
                b.wait("vector", "T", b.cnt["T"])
                tgt = b.emit("vector", lambda e, s=s, p=bank: e.tensor_copy(
                    out=crep[:, :, s], in_=p[:, 0:TS]), inc="V")
                evac_after_bank[bi] = ("V", tgt)
            repV = b.cnt["V"]
            b.wait("scalar", "V", repV)

            if PH < 6:
                continue
            scan_tgts = []
            dtx_tgt = 0
            for ct in range(16):
                bi = pbi[0] % POOLN[0]
                pbi[0] += 1
                bank = pb[bi]
                prev = evac_after_bank.get(bi)
                if prev is not None:
                    b.wait("tensor", prev[0], prev[1])
                b.emit("tensor", lambda e, ct=ct, p=bank: e.matmul(
                    p[:, 0:TS], wdt[0:64, ct, :], xp_bf[0:64, :],
                    start=True, stop=True), inc="T")
                b.wait("scalar", "T", b.cnt["T"])
                b.wait("scalar", "V", dtx_tgt)
                b.emit("scalar", lambda e, ct=ct, p=bank: e.activation(
                    out=dt_f, in_=p[:, 0:TS], func=AF.Exp, bias=bdt[:, ct:ct+1]))
                stgt = b.emit("scalar", lambda e: e.activation(
                    out=dt_f, in_=dt_f, func=AF.Ln, bias=1.0), inc="S")
                evac_after_bank[bi] = ("S", stgt)
                b.wait("vector", "S", stgt)
                dtx_tgt = b.emit("vector", lambda e, ct=ct: e.tensor_mul(
                    out=dtx, in0=dt_f, in1=xi[:, ct, 3:3+TS]), inc="V")
                ub4 = [m8[:, :, 7], m8[:, :, 6], m8[:, :, 5], m8[:, :, 4]]
                b.wait("gpsimd", "V", dtx_tgt)
                for s in range(DS):
                    if len(scan_tgts) >= 4:
                        b.wait("scalar", "V", scan_tgts[-4])
                        b.wait("gpsimd", "V", scan_tgts[-4])
                    b.emit("scalar", lambda e, ct=ct, s=s, d=da2[s % 4]:
                           e.activation(out=d, in_=dt_f, func=AF.Exp,
                                        scale=a_sb[:, ct, s:s+1]), inc="S")
                    sS = b.cnt["S"]
                    gU = b.emit("gpsimd", lambda e, s=s, u=ub4[s % 4]:
                                e.tensor_mul(out=u, in0=dtx,
                                             in1=brep[:, s, :]), inc="G")
                    b.wait("vector", "S", sS)
                    b.wait("vector", "G", gU)
                    stg = b.emit("vector", lambda e, ct=ct, s=s, d=da2[s % 4],
                                 u=ub4[s % 4]: e.tensor_tensor_scan(
                                     out=Hbuf[:, :, s], data0=d,
                                     data1=u,
                                     initial=hstate[:, ct, s:s+1],
                                     op0=ALU.mult, op1=ALU.add), inc="V")
                    scan_tgts.append(stg)
                b.emit("vector", lambda e, ct=ct: e.tensor_copy(
                    out=hstate[:, ct, :], in_=Hbuf[:, TS-1, :]))
                b.emit("vector", lambda e: e.tensor_mul(
                    out=Hbuf[:, :, :], in0=Hbuf[:, :, :], in1=crep[:, :, :]))
                b.emit("vector", lambda e: e.tensor_add(
                    out=m8[:, :, 0:8],
                    in0=Hbuf[:, :, 0:8], in1=Hbuf[:, :, 8:16]))
                b.emit("vector", lambda e: e.tensor_add(
                    out=m8[:, :, 0:4], in0=m8[:, :, 0:4], in1=m8[:, :, 4:8]))
                b.emit("vector", lambda e: e.tensor_add(
                    out=m8[:, :, 0:2], in0=m8[:, :, 0:2], in1=m8[:, :, 2:4]))
                b.emit("vector", lambda e: e.tensor_add(
                    out=m8[:, :, 0], in0=m8[:, :, 0], in1=m8[:, :, 1]))
                b.emit("vector", lambda e, ct=ct: e.scalar_tensor_tensor(
                    out=m8[:, :, 1], in0=xi[:, ct, 3:3+TS],
                    scalar=d_sb[:, ct:ct+1], in1=m8[:, :, 0],
                    op0=ALU.mult, op1=ALU.add))
                b.emit("vector", lambda e, ct=ct: e.tensor_mul(
                    out=ybuf[:, ct, :], in0=m8[:, :, 1], in1=gz[:, ct, :]),
                    inc="V")
            scanV = b.cnt["V"]

            if PH < 7:
                continue
            b.wait("tensor", "V", scanV)
            for mt in range(8):
                wb16, wtgt = wout_ld(mt, W["l0_wout"][:, mt*128:(mt+1)*128]
                                     .rearrange("(k p) q -> p k q", p=128),
                                     ("p (k q) -> p k q", dict(k=16)))
                b.wait("tensor", wtgt[0], wtgt[1])

                def ev_s(e, p, mt=mt):
                    return e.tensor_copy(out=sT[:, mt, :], in_=p)
                mm_group([(wb16[:, kt, :], ybuf[:, kt, :]) for kt in range(16)],
                         128, TS, ev_s, "vector")
                wout_done(mt, b.cnt["T"])

            if PH < 8:
                continue
            b.wait("vector", "S", gS)
            b.emit("vector", lambda e: e.tensor_mul(
                out=prod[:, :, :], in0=sT[:, :, :], in1=gT[:, :, :]), inc="V")
            b.wait("tensor", "V", b.cnt["V"])
            transpose_tiles([(prod[:, dc, j*128:(j+1)*128],
                              xres[:, tt0 + j, dc*128:(dc+1)*128])
                             for j in range(ntt) for dc in range(8)],
                            out_op="add")
        b.barrier()


    def layer1():
        POOLN[0] = 5
        AR.reset(); STA.reset(); WB.reset()
        stream_sem_pool[:] = ["D0", "D1", "D2", "D3", "D4", "D5", "D6", "D7"]

        sq_buf = WB.alloc(2 * 1024, "p (b2 m) -> p b2 m", b2=2)
        sk_buf = WB.alloc(2 * 1024, "p (b2 m) -> p b2 m", b2=2)
        sv_buf = WB.alloc(2 * 2048, "p (b2 m) -> p b2 m", b2=2)
        so_buf = WB.alloc(2 * 1024, "p (b2 m) -> p b2 m", b2=2)
        sq_ld, sq_done = make_stream(sq_buf)
        sk_ld, sk_done = make_stream(sk_buf)
        sv_ld, sv_done = make_stream(sv_buf)
        so_ld, so_done = make_stream(so_buf)
        pass
        wWD_all = b.cnt["WD"]

        big = AR.alloc(16384)
        chunksT = AR.alloc(16384, "p (k t) -> p k t", k=8)
        Vp = AR.alloc(2 * 16 * 65, "p (tm h e) -> p tm h e", tm=2, h=16)
        ctxn = AR.alloc(2 * 1024, "p (tm d) -> p tm d", tm=2)
        at = AR.alloc(2 * 256, "p (kt t) -> p kt t", kt=2)
        summT = AR.alloc(8 * 8, "p (k c) -> p k c", k=8)
        gbuf = AR.alloc(144)
        ce_all = STA.alloc(2 * 8 * 8)
        offtot = gbuf[:, 128:144]
        wrT = AR.alloc(8 * 8, "p (k e) -> p k e", k=8)
        xn2 = big[:, 0:2048].rearrange("p (tt d) -> p tt d", tt=2)
        xnT_c = big[:, 2048:4096].rearrange("p (k t) -> p k t", k=8)
        QT = big[:, 4096:6144].rearrange("p (k t) -> p k t", k=8)
        KT = big[:, 6144:8192].rearrange("p (k t) -> p k t", k=8)
        ctxT = QT
        macc = big[:, 0:8192].rearrange("p (tt d) -> p tt d", tt=8)
        xnTh = big[:, 8192:16384].rearrange("p (k t) -> p k t", k=8)

        wrsb = ctxn[:, 1, :]
        dma("gpsimd", wrsb[0:8, :], W["l1_wr"])
        wWD_all = b.cnt["WD"]
        mcol, rstd = layernorm_stats(xn2[:, 0, :])
        recip_col = STA.alloc(1)
        lgbuf = STA.alloc(8)
        m1c = STA.alloc(1); m2c = STA.alloc(1)
        e1f = STA.alloc(8); e2f = STA.alloc(8)
        cums = STA.alloc(8); cums2 = STA.alloc(8)
        posc = STA.alloc(8); w1c = STA.alloc(1); w2c = STA.alloc(1)
        k1c = STA.alloc(1); k2c = STA.alloc(1)

        b.wait("tensor", "WD", wWD_all)
        for dk in range(8):
            sslot = tp_state["slot"] % 2
            tp_state["slot"] += 1
            if len(tp_state["evacs"]) >= 2:
                b.wait("tensor", "V", tp_state["evacs"][-2])
            tT = b.emit("tensor", lambda e, sslot=sslot, dk=dk: e.transpose(
                ptp2[sslot][0:128, 0:8], wrsb[0:8, dk*128:(dk+1)*128],
                ident[0:8, 0:8]), inc="T")
            b.wait("vector", "T", tT)
            vv = b.emit("vector", lambda e, sslot=sslot, dk=dk: e.tensor_copy(
                out=wrT[:, dk, :], in_=ptp2[sslot][0:128, 0:8]), inc="V")
            tp_state["evacs"].append(vv)
        b.emit("vector", lambda e: e.memset(offtot[0:1, :], 0.0), inc="V")
        b.emit("vector", lambda e: e.memset(
            Vp[:, :, :, 64:65].rearrange("p a b c -> p (a b c)"), 0.0), inc="V")
        b.emit("vector", lambda e: e.memset(
            Vp[:, :, :, 64:65].rearrange("p a b c -> p (a b c)"), 1.0), inc="V")

        for c in range(8):
            for j in range(2):
                ln_apply(mcol, rstd, 2*c + j, xn2[:, j, :])
            lnV = b.cnt["V"]
            b.wait("tensor", "V", lnV)
            transpose_tiles([(xn2[:, j, dc*128:(dc+1)*128],
                              xnT_c[:, dc, j*128:(j+1)*128])
                             for j in range(2) for dc in range(8)])
            xnT_V = b.cnt["V"]
            b.wait("tensor", "V", xnT_V)
            for mt in range(8):
                wqb, qt_ = sq_ld(c*8 + mt, W["l1_wq_l"][:, mt*128:(mt+1)*128]
                                 .rearrange("(k p) q -> p k q", p=128),
                                 ("p (k q) -> p k q", dict(k=8)))
                b.wait("tensor", qt_[0], qt_[1])
                def evq(e, p, mt=mt):
                    return e.tensor_copy(out=QT[:, mt, :], in_=p)
                mm_group([(wqb[:, kt, :], xnT_c[:, kt, :])
                          for kt in range(8)], 128, 256, evq, "vector")
                sq_done(c*8 + mt, b.cnt["T"])
            for mt in range(8):
                wkb, kt_ = sk_ld(c*8 + mt, W["l1_wk_l"][:, mt*128:(mt+1)*128]
                                 .rearrange("(k p) q -> p k q", p=128),
                                 ("p (k q) -> p k q", dict(k=8)))
                b.wait("tensor", kt_[0], kt_[1])
                def evk(e, p, mt=mt):
                    return e.tensor_copy(out=KT[:, mt, :], in_=p)
                mm_group([(wkb[:, kt, :], xnT_c[:, kt, :])
                          for kt in range(8)], 128, 256, evk, "vector")
                sk_done(c*8 + mt, b.cnt["T"])
            for nb in range(4):
                wvb, vt_ = sv_ld(c*4 + nb, W["l1_wv_l"][:, nb*256:(nb+1)*256]
                                 .rearrange("(k p) q -> p k q", p=128),
                                 ("p (k q) -> p k q", dict(k=8)))
                b.wait("tensor", vt_[0], vt_[1])
                for tm in range(2):
                    def evv(e, p, tm=tm, nb=nb):
                        return e.tensor_copy(
                            out=Vp[:, tm, nb*4:(nb+1)*4, 0:64],
                            in_=p.rearrange("p (h x) -> p h x", h=4))
                    mm_group([(xnT_c[:, kt, tm*128:(tm+1)*128],
                               wvb[:, kt, :])
                              for kt in range(8)], 128, 256, evv, "vector")
                sv_done(c*4 + nb, b.cnt["T"])
            projV = b.cnt["V"]
            b.wait("tensor", "V", projV)
            for h in range(16):
                r0 = (h % 2) * 64
                mt = h // 2
                for kt in range(2):
                    def eva(e, p, kt=kt):
                        return e.activation(out=at[:, kt, :], in_=p,
                                            func=AF.Exp, scale=0.125)
                    mm_group([(KT[r0:r0+64, mt, kt*128:(kt+1)*128],
                               QT[r0:r0+64, mt, 0:256])],
                             128, 256, eva, "scalar")
                aS = b.cnt["S"]
                b.wait("tensor", "S", aS)
                for qm in range(2):
                    bi = pbi[0] % POOLN[0]
                    pbi[0] += 1
                    bank = pb[bi]
                    prev = evac_after_bank.get(bi)
                    if prev is not None:
                        b.wait("tensor", prev[0], prev[1])
                    for kt in range(2):
                        b.emit("tensor", lambda e, kt=kt, qm=qm, h=h, p=bank:
                               e.matmul(p[:, 0:65], at[:, kt, qm*128:(qm+1)*128],
                                        Vp[:, kt, h, :], start=(kt == 0),
                                        stop=(kt == 1)),
                               inc=("T" if kt == 1 else None))
                    tT = b.cnt["T"]
                    b.wait("vector", "T", tT)
                    b.emit("vector", lambda e, p=bank: e.reciprocal(
                        out=recip_col, in_=p[:, 64:65]), inc="V")
                    tgt = b.emit("vector", lambda e, p=bank, qm=qm, h=h:
                                 e.tensor_scalar(
                                     out=ctxn[:, qm, 64*h:64*h+64],
                                     in0=p[:, 0:64], scalar1=recip_col,
                                     scalar2=None, op0=ALU.mult), inc="V")
                    evac_after_bank[bi] = ("V", tgt)
            ctxV = b.cnt["V"]
            b.wait("tensor", "V", ctxV)
            transpose_tiles([(ctxn[:, j, dc*128:(dc+1)*128],
                              ctxT[:, dc, j*128:(j+1)*128])
                             for j in range(2) for dc in range(8)])
            b.wait("tensor", "V", b.cnt["V"])
            for mt in range(8):
                wob, ot_ = so_ld(c*8 + mt, W["l1_wo_l"][:, mt*128:(mt+1)*128]
                                 .rearrange("(k p) q -> p k q", p=128),
                                 ("p (k q) -> p k q", dict(k=8)))
                b.wait("tensor", ot_[0], ot_[1])
                def evc(e, p, mt=mt, c=c):
                    return e.tensor_add(out=chunksT[:, mt, c*256:(c+1)*256],
                                        in0=xnT_c[:, mt, :], in1=p)
                mm_group([(wob[:, kt, :], ctxT[:, kt, :])
                          for kt in range(8)], 128, 256, evc, "vector")
                so_done(c*8 + mt, b.cnt["T"])
            b.wait("vector", "V", b.cnt["V"])
            b.emit("vector", lambda e, c=c: e.tensor_reduce(
                out=summT[:, :, c], in_=chunksT[:, :, c*256:(c+1)*256]
                .rearrange("p k (z t) -> p (k z) t", z=1),
                axis=AX.X, op=ALU.add), inc="V")
        b.emit("vector", lambda e: e.tensor_scalar(
            out=summT[:, :, :], in0=summT[:, :, :], scalar1=1.0/256,
            scalar2=None, op0=ALU.mult), inc="V")
        summV = b.cnt["V"]

        b.wait("tensor", "V", summV)
        gQ = gbuf[:, 0:64].rearrange("p (k t) -> p k t", k=8)
        gK = gbuf[:, 64:128].rearrange("p (k t) -> p k t", k=8)
        gVp = gbuf[:, 128:1168 if False else 128][:, 0:0]
        gVp = AR.alloc(16 * 65, "p (h e) -> p h e", h=16)
        gat = AR.alloc(8)
        gctxn = ctxn[:, 0, :]
        gctxT = AR.alloc(8 * 8, "p (k t) -> p k t", k=8)
        gfin = AR.alloc(8 * 8, "p (k c) -> p k c", k=8)
        b.emit("vector", lambda e: e.memset(
            gVp[:, :, 64:65].rearrange("p a c -> p (a c)"), 1.0), inc="V")
        for mt in range(8):
            wqb, qt_ = sq_ld(64 + mt, W["l1_wq_g"][:, mt*128:(mt+1)*128]
                             .rearrange("(k p) q -> p k q", p=128),
                             ("p (k q) -> p k q", dict(k=8)))
            b.wait("tensor", qt_[0], qt_[1])
            def evgq(e, p, mt=mt):
                return e.tensor_copy(out=gQ[:, mt, :], in_=p)
            mm_group([(wqb[:, kt, :], summT[:, kt, :])
                      for kt in range(8)], 128, 8, evgq, "vector")
            sq_done(64 + mt, b.cnt["T"])
        for mt in range(8):
            wkb, kt_ = sk_ld(64 + mt, W["l1_wk_g"][:, mt*128:(mt+1)*128]
                             .rearrange("(k p) q -> p k q", p=128),
                             ("p (k q) -> p k q", dict(k=8)))
            b.wait("tensor", kt_[0], kt_[1])
            def evgk(e, p, mt=mt):
                return e.tensor_copy(out=gK[:, mt, :], in_=p)
            mm_group([(wkb[:, kt, :], summT[:, kt, :])
                      for kt in range(8)], 128, 8, evgk, "vector")
            sk_done(64 + mt, b.cnt["T"])
        for nb in range(4):
            wvb, vt_ = sv_ld(32 + nb, W["l1_wv_g"][:, nb*256:(nb+1)*256]
                             .rearrange("(k p) q -> p k q", p=128),
                             ("p (k q) -> p k q", dict(k=8)))
            b.wait("tensor", vt_[0], vt_[1])
            def evgv(e, p, nb=nb):
                return e.tensor_copy(out=gVp[0:8, nb*4:(nb+1)*4, 0:64],
                                     in_=p.rearrange("p (h x) -> p h x", h=4))
            mm_group([(summT[:, kt, :], wvb[:, kt, :])
                      for kt in range(8)], 8, 256, evgv, "vector")
            sv_done(32 + nb, b.cnt["T"])
        gprojV = b.cnt["V"]
        b.wait("tensor", "V", gprojV)
        for h in range(16):
            r0 = (h % 2) * 64
            mt = h // 2
            def evga(e, p, h=h):
                return e.activation(out=gat[0:8, :], in_=p, func=AF.Exp,
                                    scale=0.125)
            mm_group([(gK[r0:r0+64, mt, :], gQ[r0:r0+64, mt, :])],
                     8, 8, evga, "scalar")
            gaS = b.cnt["S"]
            b.wait("tensor", "S", gaS)
            bi = pbi[0] % POOLN[0]
            pbi[0] += 1
            bank = pb[bi]
            prev = evac_after_bank.get(bi)
            if prev is not None:
                b.wait("tensor", prev[0], prev[1])
            b.emit("tensor", lambda e, h=h, p=bank: e.matmul(
                p[0:8, 0:65], gat[0:8, :], gVp[0:8, h, :],
                start=True, stop=True), inc="T")
            b.wait("vector", "T", b.cnt["T"])
            b.emit("vector", lambda e, p=bank: e.reciprocal(
                out=recip_col[0:8, :], in_=p[0:8, 64:65]), inc="V")
            tgt = b.emit("vector", lambda e, p=bank, h=h: e.tensor_scalar(
                out=gctxn[0:8, 64*h:64*h+64], in0=p[0:8, 0:64],
                scalar1=recip_col[0:8, :], scalar2=None, op0=ALU.mult), inc="V")
            evac_after_bank[bi] = ("V", tgt)
        b.wait("tensor", "V", b.cnt["V"])
        for dk in range(8):
            sslot = tp_state["slot"] % 2
            tp_state["slot"] += 1
            if len(tp_state["evacs"]) >= 2:
                b.wait("tensor", "V", tp_state["evacs"][-2])
            tT = b.emit("tensor", lambda e, sslot=sslot, dk=dk: e.transpose(
                ptp2[sslot][0:128, 0:8], gctxn[0:8, dk*128:(dk+1)*128],
                ident[0:8, 0:8]), inc="T")
            b.wait("vector", "T", tT)
            vv = b.emit("vector", lambda e, sslot=sslot, dk=dk: e.tensor_copy(
                out=gctxT[:, dk, :], in_=ptp2[sslot][0:128, 0:8]), inc="V")
            tp_state["evacs"].append(vv)
        b.wait("tensor", "V", b.cnt["V"])
        for mt in range(8):
            wob, ot_ = so_ld(64 + mt, W["l1_wo_g"][:, mt*128:(mt+1)*128]
                             .rearrange("(k p) q -> p k q", p=128),
                             ("p (k q) -> p k q", dict(k=8)))
            b.wait("tensor", ot_[0], ot_[1])
            def evgf(e, p, mt=mt):
                return e.tensor_add(out=gfin[:, mt, :], in0=summT[:, mt, :],
                                    in1=p)
            mm_group([(wob[:, kt, :], gctxT[:, kt, :])
                      for kt in range(8)], 128, 8, evgf, "vector")
            so_done(64 + mt, b.cnt["T"])
        b.wait("vector", "V", b.cnt["V"])
        for c in range(8):
            for mt in range(8):
                b.emit("vector", lambda e, c=c, mt=mt: e.tensor_add(
                    out=chunksT[:, mt, c*256:(c+1)*256],
                    in0=chunksT[:, mt, c*256:(c+1)*256],
                    in1=gfin[:, mt, c:c+1].to_broadcast([128, 256])), inc="V")
        s1V = b.cnt["V"]

        prodscr = ctxn
        tp_done = {0: 0, 1: 0}
        for hf in range(2):
            for tt in range(8):
                gt = hf * 8 + tt
                b.wait("vector", "T", tp_done[tt % 2])
                ln_apply(mcol, rstd, gt, ctxn[:, tt % 2, :])
                b.wait("tensor", "V", b.cnt["V"])
                transpose_tiles([(ctxn[:, tt % 2, dc*128:(dc+1)*128],
                                  xnTh[:, dc, tt*128:(tt+1)*128])
                                 for dc in range(8)])
                tp_done[tt % 2] = b.cnt["T"]
            xnhV = b.cnt["V"]
            b.wait("tensor", "V", xnhV)
            for tt in range(8):
                def evlg(e, p, tt=tt):
                    return e.tensor_copy(out=lgbuf, in_=p)
                mm_group([(xnTh[:, kt, tt*128:(tt+1)*128], wrT[:, kt, :])
                          for kt in range(8)], 128, 8, evlg, "vector")
                b.emit("vector", lambda e: e.reduce_max(
                    out=m1c, in_=lgbuf, axis=AX.X))
                b.emit("vector", lambda e: e.tensor_scalar(
                    out=e1f, in0=lgbuf, scalar1=m1c, scalar2=None,
                    op0=ALU.is_ge))
                b.emit("vector", lambda e: e.memset(cums, 0.0))
                b.emit("vector", lambda e: e.tensor_copy(out=cums[:, 1:8],
                                                         in_=e1f[:, 0:7]))
                b.emit("vector", lambda e: e.tensor_add(
                    out=cums[:, 2:8], in0=cums[:, 2:8], in1=e1f[:, 0:6]))
                b.emit("vector", lambda e: e.tensor_add(
                    out=cums[:, 3:8], in0=cums[:, 3:8], in1=e1f[:, 0:5]))
                b.emit("vector", lambda e: e.tensor_add(
                    out=cums[:, 4:8], in0=cums[:, 4:8], in1=e1f[:, 0:4]))
                b.emit("vector", lambda e: e.tensor_add(
                    out=cums[:, 5:8], in0=cums[:, 5:8], in1=e1f[:, 0:3]))
                b.emit("vector", lambda e: e.tensor_add(
                    out=cums[:, 6:8], in0=cums[:, 6:8], in1=e1f[:, 0:2]))
                b.emit("vector", lambda e: e.tensor_add(
                    out=cums[:, 7:8], in0=cums[:, 7:8], in1=e1f[:, 0:1]))
                b.emit("vector", lambda e: e.tensor_scalar(
                    out=cums, in0=cums, scalar1=0.5, scalar2=None,
                    op0=ALU.is_lt))
                b.emit("vector", lambda e: e.tensor_mul(
                    out=e1f, in0=e1f, in1=cums))
                b.emit("vector", lambda e: e.tensor_scalar(
                    out=cums2, in0=e1f, scalar1=-1e30, scalar2=None,
                    op0=ALU.mult))
                b.emit("vector", lambda e: e.tensor_add(
                    out=cums2, in0=cums2, in1=lgbuf))
                b.emit("vector", lambda e: e.reduce_max(
                    out=m2c, in_=cums2, axis=AX.X))
                b.emit("vector", lambda e: e.tensor_scalar(
                    out=e2f, in0=cums2, scalar1=m2c, scalar2=None,
                    op0=ALU.is_ge))
                b.emit("vector", lambda e: e.memset(cums, 0.0))
                b.emit("vector", lambda e: e.tensor_copy(out=cums[:, 1:8],
                                                         in_=e2f[:, 0:7]))
                b.emit("vector", lambda e: e.tensor_add(
                    out=cums[:, 2:8], in0=cums[:, 2:8], in1=e2f[:, 0:6]))
                b.emit("vector", lambda e: e.tensor_add(
                    out=cums[:, 3:8], in0=cums[:, 3:8], in1=e2f[:, 0:5]))
                b.emit("vector", lambda e: e.tensor_add(
                    out=cums[:, 4:8], in0=cums[:, 4:8], in1=e2f[:, 0:4]))
                b.emit("vector", lambda e: e.tensor_add(
                    out=cums[:, 5:8], in0=cums[:, 5:8], in1=e2f[:, 0:3]))
                b.emit("vector", lambda e: e.tensor_add(
                    out=cums[:, 6:8], in0=cums[:, 6:8], in1=e2f[:, 0:2]))
                b.emit("vector", lambda e: e.tensor_add(
                    out=cums[:, 7:8], in0=cums[:, 7:8], in1=e2f[:, 0:1]))
                b.emit("vector", lambda e: e.tensor_scalar(
                    out=cums, in0=cums, scalar1=0.5, scalar2=None,
                    op0=ALU.is_lt))
                b.emit("vector", lambda e: e.tensor_mul(
                    out=e2f, in0=e2f, in1=cums))
                b.emit("vector", lambda e: e.tensor_sub(
                    out=w1c, in0=m1c, in1=m2c), inc="V")
                b.wait("scalar", "V", b.cnt["V"])
                b.emit("scalar", lambda e: e.activation(
                    out=w1c, in_=w1c, func=AF.Sigmoid), inc="S")
                b.wait("vector", "S", b.cnt["S"])
                b.emit("vector", lambda e: e.tensor_scalar(
                    out=w2c, in0=w1c, scalar1=-1.0, scalar2=1.0,
                    op0=ALU.mult, op1=ALU.add))
                bi = pbi[0] % POOLN[0]
                pbi[0] += 1
                bank = pb[bi]
                prev = evac_after_bank.get(bi)
                if prev is not None:
                    b.wait("tensor", prev[0], prev[1])
                b.emit("vector", lambda e: e.tensor_copy(
                    out=at[:, 0, 0:8], in_=e1f))
                b.emit("vector", lambda e: e.tensor_copy(
                    out=at[:, 0, 8:16], in_=e2f), inc="V")
                b.wait("tensor", "V", b.cnt["V"])
                b.emit("tensor", lambda e, p=bank: e.matmul(
                    p[:, 0:16], ustrict, at[:, 0, 0:16],
                    start=True, stop=False))
                b.emit("tensor", lambda e, p=bank: e.matmul(
                    p[:, 0:16], ones_row[0:1, :], offtot[0:1, 0:16],
                    start=False, stop=True))
                b.emit("tensor", lambda e, p=bank: e.matmul(
                    p[0:1, 16:32], ones_col, at[:, 0, 0:16],
                    start=True, stop=True), inc="T")
                tT = b.cnt["T"]
                b.wait("vector", "T", tT)
                b.emit("vector", lambda e, p=bank: e.tensor_add(
                    out=offtot[0:1, :], in0=offtot[0:1, :], in1=p[0:1, 16:32]))
                b.emit("vector", lambda e, p=bank: e.tensor_copy(
                    out=cums2, in_=p[:, 8:16]))
                b.emit("vector", lambda e, p=bank: e.tensor_add(
                    out=posc, in0=cums2, in1=p[:, 0:8]), inc="V")
                vv = b.cnt["V"]
                evac_after_bank[bi] = ("V", vv)
                b.emit("vector", lambda e: e.tensor_mul(
                    out=cums, in0=posc, in1=e1f))
                b.emit("vector", lambda e: e.tensor_reduce(
                    out=k1c, in_=cums, axis=AX.X, op=ALU.add))
                b.emit("vector", lambda e: e.tensor_scalar(
                    out=k1c, in0=k1c, scalar1=768.5, scalar2=None,
                    op0=ALU.is_lt))
                b.emit("vector", lambda e: e.tensor_add(
                    out=cums2, in0=posc, in1=e1f))
                b.emit("vector", lambda e: e.tensor_mul(
                    out=cums2, in0=cums2, in1=e2f))
                b.emit("vector", lambda e: e.tensor_reduce(
                    out=k2c, in_=cums2, axis=AX.X, op=ALU.add))
                b.emit("vector", lambda e: e.tensor_scalar(
                    out=k2c, in0=k2c, scalar1=768.5, scalar2=None,
                    op0=ALU.is_lt))
                b.emit("vector", lambda e: e.tensor_mul(
                    out=k1c, in0=k1c, in1=w1c))
                b.emit("vector", lambda e: e.tensor_mul(
                    out=k2c, in0=k2c, in1=w2c))
                b.emit("vector", lambda e: e.tensor_scalar(
                    out=e1f, in0=e1f, scalar1=k1c, scalar2=None, op0=ALU.mult))
                b.emit("vector", lambda e: e.tensor_scalar(
                    out=e2f, in0=e2f, scalar1=k2c, scalar2=None, op0=ALU.mult))
                b.emit("vector", lambda e, hf=hf, tt=tt: e.tensor_add(
                    out=ce_all[:, (hf*8+tt)*8:(hf*8+tt)*8+8], in0=e1f,
                    in1=e2f), inc="V")
            ceV = b.cnt["V"]
            b.wait("tensor", "V", ceV)
            for ex in range(8):
                for nb4 in range(4):
                    web, wtgt = sv_ld(40 + hf*32 + ex*4 + nb4,
                                      W["l1_we"][ex][:, nb4*256:(nb4+1)*256]
                                      .rearrange("(k p) m -> p k m", p=128),
                                      ("p (k m) -> p k m", dict(k=8)))
                    b.wait("tensor", wtgt[0], wtgt[1])
                    for tt in range(8):
                        def evme(e, p, ex=ex, tt=tt, nb4=nb4, hf=hf):
                            gt = hf*8 + tt
                            dst = macc[:, tt, nb4*256:(nb4+1)*256]
                            if ex == 0:
                                return e.tensor_scalar(
                                    out=dst, in0=p,
                                    scalar1=ce_all[:, gt*8:gt*8+1],
                                    scalar2=None, op0=ALU.mult)
                            return e.scalar_tensor_tensor(
                                out=dst, in0=p,
                                scalar=ce_all[:, gt*8+ex:gt*8+ex+1],
                                in1=dst, op0=ALU.mult, op1=ALU.add)
                        mm_group([(xnTh[:, kt, tt*128:(tt+1)*128],
                                   web[:, kt, :])
                                  for kt in range(8)], 128, 256, evme,
                                 "vector")
                    sv_done(40 + hf*32 + ex*4 + nb4, b.cnt["T"])
            maccV = b.cnt["V"]
            b.wait("scalar", "V", maccV)
            for tt in range(8):
                b.emit("scalar", lambda e, tt=tt: e.activation(
                    out=macc[:, tt, :], in_=macc[:, tt, :], func=AF.Silu),
                    inc="S")
            gsS = b.cnt["S"]
            b.wait("vector", "S", gsS)
            b.wait("vector", "V", s1V)
            for tt in range(8):
                gt = hf*8 + tt
                for dc in range(8):
                    sslot = tp_state["slot"] % 2
                    tp_state["slot"] += 1
                    if len(tp_state["evacs"]) >= 2:
                        b.wait("tensor", "V", tp_state["evacs"][-2])
                    tT = b.emit("tensor", lambda e, sslot=sslot, dc=dc, gt=gt:
                                e.transpose(ptp2[sslot][:, 0:128],
                                            chunksT[:, dc, gt*128:(gt+1)*128],
                                            ident), inc="T")
                    b.wait("vector", "T", tT)
                    b.emit("vector", lambda e, sslot=sslot, tt=tt, dc=dc:
                           e.tensor_mul(out=prodscr[:, 0, 0:128],
                                        in0=ptp2[sslot][:, 0:128],
                                        in1=macc[:, tt, dc*128:(dc+1)*128]),
                           inc="V")
                    vv = b.emit("vector", lambda e, gt=gt, dc=dc: e.tensor_add(
                        out=xres[:, gt, dc*128:(dc+1)*128],
                        in0=xres[:, gt, dc*128:(dc+1)*128],
                        in1=prodscr[:, 0, 0:128]), inc="V")
                    tp_state["evacs"].append(vv)
        b.barrier()


    def layer2():
        POOLN[0] = 5
        AR.reset(); STA.reset(); WB.reset()
        stream_sem_pool[:] = ["D0", "D1", "D2", "D3", "D4", "D5", "D6", "D7"]

        identf = STA.alloc(128)
        lg = STA.alloc(8)
        negm = STA.alloc(1)
        ssum = STA.alloc(1)
        probs_t = STA.alloc(16 * 8, "p (tt e) -> p tt e", tt=16)
        rankm_t = STA.alloc(16 * 8, "p (tt e) -> p tt e", tt=16)
        cnt8 = STA.alloc(1)
        cgt8 = STA.alloc(1)
        need8 = STA.alloc(1)
        sums2 = STA.alloc(2)

        xn = AR.alloc(16 * 1024, "p (tt d) -> p tt d", tt=16)
        g2T = AR.alloc(8 * 2048, "p (k t) -> p k t", k=8)
        P_g = AR.alloc(4 * 2048, "p (i t) -> p i t", i=4)
        selA = P_g[:, 0, 0:2048]
        selB = P_g[:, 1, 0:2048]
        ones8 = P_g[:, 2, 0:2048]
        xnT_tt = P_g[:, 3, 0:1024].rearrange("p (k q) -> p k q", k=8)
        probs_e = pers_t[:, 0:4096].bitcast(F32)
        cumf = pers_t[:, 4096:8192].bitcast(F32)
        PT = pers_t[:, 0:8192].rearrange("p (tt i) -> p tt i", tt=16)
        cmask = pers_t[:, 8192:8320]
        w1_buf = WB.alloc(2 * 2048, "p (b2 m) -> p b2 m", b2=2)
        w2_buf = WB.alloc(2 * 1024, "p (b2 m) -> p b2 m", b2=2)
        XgT = WB.alloc(8 * 512, "p (k i) -> p k i", k=8)
        hT2 = [pers_t[:, 8192:8448], pers_t[:, 8448:8704]]
        pmisc2 = [pmisc[:, 0:256], ptp2[0].bitcast(F32)[:, 0:256]]
        gwsb = w1_buf[:, 0, 0:1024]
        gwT = w2_buf[:, 0, 0:64].rearrange("p (k e) -> p k e", k=8)
        w1_ld, w1_done = make_stream(w1_buf)
        w2_ld, w2_done = make_stream(w2_buf)

        mcol, rstd = layernorm_stats(xn[:, 0, :])
        for tt in range(NTT):
            ln_apply(mcol, rstd, tt, xn[:, tt, :])
        xnV = b.cnt["V"]
        b.emit("vector", lambda e: e.memset(identf, 1.0), inc="V")
        b.wait("gpsimd", "V", b.cnt["V"])
        b.emit("gpsimd", lambda e: e.affine_select(
            out=identf, in_=identf, pattern=[[-1, 128]],
            compare_op=ALU.is_equal, fill=0.0, channel_multiplier=1), inc="G")
        b.emit("vector", lambda e: e.memset(ones8[0:8, :], 1.0), inc="V")
        b.wait("vector", "G", b.cnt["G"])

        dma("gpsimd", gwsb[0:8, :], W["l2_gw"])
        b.wait("tensor", "WD", b.cnt["WD"])
        for dk in range(8):
            sslot = tp_state["slot"] % 2
            tp_state["slot"] += 1
            if len(tp_state["evacs"]) >= 2:
                b.wait("tensor", "V", tp_state["evacs"][-2])
            tT = b.emit("tensor", lambda e, sslot=sslot, dk=dk: e.transpose(
                ptp2[sslot][0:128, 0:8], gwsb[0:8, dk*128:(dk+1)*128],
                ident[0:8, 0:8]), inc="T")
            b.wait("vector", "T", tT)
            vv = b.emit("vector", lambda e, sslot=sslot, dk=dk: e.tensor_copy(
                out=gwT[:, dk, :], in_=ptp2[sslot][0:128, 0:8]), inc="V")
            tp_state["evacs"].append(vv)

        b.wait("tensor", "V", xnV)
        gw_guard = [False]
        for tt in range(NTT):
            transpose_tiles([(xn[:, tt, dc*128:(dc+1)*128], xnT_tt[:, dc, :])
                             for dc in range(8)])
            b.wait("tensor", "V", b.cnt["V"])

            def evlg(e, p):
                return e.tensor_copy(out=lg, in_=p)
            mm_group([(xnT_tt[:, kt, :], gwT[:, kt, :]) for kt in range(8)],
                     128, 8, evlg, "vector")
            b.emit("vector", lambda e: e.reduce_max(out=negm, in_=lg,
                                                    axis=AX.X))
            b.emit("vector", lambda e: e.tensor_scalar(
                out=negm, in0=negm, scalar1=-1.0, scalar2=None, op0=ALU.mult),
                inc="V")
            b.wait("scalar", "V", b.cnt["V"])
            b.emit("scalar", lambda e: e.activation(
                out=lg, in_=lg, func=AF.Exp, bias=negm), inc="S")
            b.wait("vector", "S", b.cnt["S"])
            b.emit("vector", lambda e: e.reduce_sum(out=ssum, in_=lg,
                                                    axis=AX.X))
            b.emit("vector", lambda e: e.reciprocal(out=ssum, in_=ssum))
            b.emit("vector", lambda e, tt=tt: e.tensor_scalar(
                out=probs_t[:, tt, :], in0=lg, scalar1=ssum, scalar2=None,
                op0=ALU.mult), inc="V")
            b.wait("tensor", "V", b.cnt["V"])
            bi = pbi[0] % POOLN[0]
            pbi[0] += 1
            bank = pb[bi]
            prev = evac_after_bank.get(bi)
            if prev is not None:
                b.wait("tensor", prev[0], prev[1])
            b.emit("tensor", lambda e, tt=tt, p=bank: e.transpose(
                p[0:8, 0:128], probs_t[:, tt, :], identf), inc="T")
            b.wait("vector", "T", b.cnt["T"])
            tgt = b.emit("vector", lambda e, tt=tt, p=bank: e.tensor_copy(
                out=probs_e[0:8, tt*128:(tt+1)*128], in_=p[0:8, 0:128]),
                inc="V")
            evac_after_bank[bi] = ("V", tgt)

        candi = cnt8.bitcast(I32)
        tmpi = cgt8.bitcast(I32)
        keepf = need8
        b.emit("vector", lambda e: e.memset(candi[0:8, :], 0), inc="V")
        for bit in range(30, -1, -1):
            b.emit("vector", lambda e, bit=bit: e.tensor_tensor(
                out=tmpi[0:8, :], in0=candi[0:8, :],
                in1=csti[0:8, bit:bit+1], op=ALU.bitwise_or))
            b.emit("vector", lambda e: e.tensor_scalar(
                out=selA[0:8, :], in0=probs_e[0:8, :],
                scalar1=cgt8[0:8, :], scalar2=None, op0=ALU.is_ge))
            b.emit("vector", lambda e: e.reduce_sum(
                out=keepf[0:8, :], in_=selA[0:8, :], axis=AX.X))
            b.emit("vector", lambda e: e.tensor_scalar(
                out=keepf[0:8, :], in0=keepf[0:8, :], scalar1=511.5,
                scalar2=None, op0=ALU.is_ge))
            b.emit("vector", lambda e: e.tensor_copy(
                out=tmpi[0:8, :], in_=keepf[0:8, :]))
            b.emit("vector", lambda e, bit=bit: e.tensor_tensor(
                out=tmpi[0:8, :], in0=tmpi[0:8, :],
                in1=csti[0:8, bit:bit+1], op=ALU.mult))
            b.emit("vector", lambda e: e.tensor_tensor(
                out=candi[0:8, :], in0=candi[0:8, :], in1=tmpi[0:8, :],
                op=ALU.bitwise_or), inc="V")
        thr = cnt8
        b.emit("vector", lambda e: e.tensor_scalar(
            out=selA[0:8, :], in0=probs_e[0:8, :], scalar1=thr[0:8, :],
            scalar2=None, op0=ALU.is_gt))
        b.emit("vector", lambda e: e.reduce_sum(
            out=cgt8[0:8, :], in_=selA[0:8, :], axis=AX.X))
        b.emit("vector", lambda e: e.tensor_scalar(
            out=need8[0:8, :], in0=cgt8[0:8, :], scalar1=-1.0, scalar2=511.5,
            op0=ALU.mult, op1=ALU.add))
        b.emit("vector", lambda e: e.tensor_scalar(
            out=selB[0:8, :], in0=probs_e[0:8, :], scalar1=thr[0:8, :],
            scalar2=None, op0=ALU.is_equal))
        b.emit("vector", lambda e: e.tensor_tensor_scan(
            out=cumf[0:8, :], data0=ones8[0:8, :], data1=selB[0:8, :],
            initial=0.0, op0=ALU.mult, op1=ALU.add))
        b.emit("vector", lambda e: e.tensor_sub(
            out=cumf[0:8, :], in0=cumf[0:8, :], in1=selB[0:8, :]))
        b.emit("vector", lambda e: e.tensor_scalar(
            out=cumf[0:8, :], in0=cumf[0:8, :], scalar1=need8[0:8, :],
            scalar2=None, op0=ALU.is_lt))
        b.emit("vector", lambda e: e.tensor_mul(
            out=selB[0:8, :], in0=selB[0:8, :], in1=cumf[0:8, :]))
        b.emit("vector", lambda e: e.tensor_add(
            out=selA[0:8, :], in0=selA[0:8, :], in1=selB[0:8, :]))
        b.emit("vector", lambda e: e.tensor_tensor_scan(
            out=cumf[0:8, :], data0=ones8[0:8, :], data1=selA[0:8, :],
            initial=0.0, op0=ALU.mult, op1=ALU.add))
        b.emit("vector", lambda e: e.tensor_sub(
            out=cumf[0:8, :], in0=cumf[0:8, :], in1=selA[0:8, :]))
        b.emit("vector", lambda e: e.tensor_mul(
            out=cumf[0:8, :], in0=cumf[0:8, :], in1=selA[0:8, :]))
        b.emit("vector", lambda e: e.tensor_add(
            out=cumf[0:8, :], in0=cumf[0:8, :], in1=selA[0:8, :]))
        b.emit("vector", lambda e: e.tensor_scalar(
            out=cumf[0:8, :], in0=cumf[0:8, :], scalar1=1.0, scalar2=None,
            op0=ALU.subtract), inc="V")
        b.wait("tensor", "V", b.cnt["V"])
        for tt in range(NTT):
            bi = pbi[0] % POOLN[0]
            pbi[0] += 1
            bank = pb[bi]
            prev = evac_after_bank.get(bi)
            if prev is not None:
                b.wait("tensor", prev[0], prev[1])
            b.emit("tensor", lambda e, tt=tt, p=bank: e.transpose(
                p[0:128, 0:8], cumf[0:8, tt*128:(tt+1)*128],
                identf[0:8, 0:8]), inc="T")
            b.wait("vector", "T", b.cnt["T"])
            tgt = b.emit("vector", lambda e, tt=tt, p=bank: e.tensor_copy(
                out=rankm_t[:, tt, :], in_=p[0:128, 0:8]), inc="V")
            evac_after_bank[bi] = ("V", tgt)
        selV = b.cnt["V"]

        w1_done(0, b.cnt["T"]); w1_done(1, b.cnt["T"])
        w2_done(0, b.cnt["T"]); w2_done(1, b.cnt["T"])
        b.wait("vector", "V", selV)
        for ex in range(8):
            for tt in range(NTT):
                b.emit("vector", lambda e, tt=tt, ex=ex: e.tensor_scalar(
                    out=PT[:, tt, :], in0=iota_f,
                    scalar1=rankm_t[:, tt, ex:ex+1],
                    scalar2=None, op0=ALU.is_equal), inc="V")
            ptV = b.cnt["V"]
            b.wait("tensor", "V", ptV)
            for dk in range(8):
                def evxg(e, p, dk=dk):
                    return e.tensor_copy(out=XgT[:, dk, :], in_=p)
                mm_group([(xn[:, tt, dk*128:(dk+1)*128], PT[:, tt, :])
                          for tt in range(NTT)], 128, 512, evxg, "vector")
            for tt in range(NTT):
                b.emit("vector", lambda e, tt=tt, ex=ex: e.tensor_scalar(
                    out=PT[:, tt, :], in0=PT[:, tt, :],
                    scalar1=probs_t[:, tt, ex:ex+1], scalar2=None,
                    op0=ALU.mult), inc="V")
            b.wait("tensor", "V", b.cnt["V"])
            transpose_tiles([(PT[:, tt, im*128:(im+1)*128],
                              P_g[:, im, tt*128:(tt+1)*128])
                             for tt in range(NTT) for im in range(4)])
            pgV = b.cnt["V"]
            for ic in range(2):
                hgel = {}
                w2bs = {}
                for fc in range(33):
                    if fc <= 31:
                        if fc % 2 == 0:
                            w1b2, t1 = w1_ld(ex*32 + ic*16 + fc//2,
                                             W["l2_w1"][ex][:, fc*128:(fc+2)*128]
                                             .rearrange("(k p) m -> p k m",
                                                        p=128),
                                             ("p (k m) -> p k m", dict(k=8)))
                            t1_hold = t1
                        w1b = w1b2[:, :, (fc % 2)*128:(fc % 2)*128+128]
                        t1 = t1_hold
                        w2b, t2 = w2_ld(ex*64 + ic*32 + fc,
                                        W["l2_w2"][ex][fc*128:(fc+1)*128, :])
                        w2bs[fc] = (w2b, t2)
                        b.wait("tensor", t1[0], t1[1])
                        if fc == 0:
                            b.wait("tensor", "V", pgV)
                            for bi4 in range(4):
                                prev = evac_after_bank.get(bi4)
                                if prev is not None:
                                    b.wait("tensor", prev[0], prev[1])

                        def evh(e, p, par=fc % 2):
                            return e.activation(out=hT2[par], in_=p,
                                                func=AF.Gelu)
                        mm_group([(w1b[:, kt, :],
                                   XgT[:, kt, ic*256:(ic+1)*256])
                                  for kt in range(8)], 128, 256, None,
                                 "scalar", bank=None if False else pmisc2[fc % 2])
                        tTh = b.cnt["T"]
                        b.wait("scalar", "T", tTh)
                        hgel[fc] = b.emit("scalar", lambda e, par=fc % 2:
                                          e.activation(
                                              out=hT2[par],
                                              in_=pmisc2[par][0:128, 0:256],
                                              func=AF.Gelu), inc="S")
                        if fc % 2 == 1:
                            w1_done(ex*32 + ic*16 + fc//2, b.cnt["T"])
                    fo = fc - 1
                    if 0 <= fo <= 31:
                        b.wait("tensor", "S", hgel[fo])
                        w2b, t2 = w2bs.pop(fo)
                        b.wait("tensor", t2[0], t2[1])
                        for mt in range(8):
                            bank = pb[mt // 2]
                            coff = (mt % 2) * 256
                            b.emit("tensor", lambda e, mt=mt, fo=fo, bank=bank,
                                   coff=coff, w2b=w2b, par=fo % 2: e.matmul(
                                       bank[:, coff:coff+256],
                                       w2b[:, mt*128:(mt+1)*128], hT2[par],
                                       start=(fo == 0 and mt % 2 == 0),
                                       stop=(fo == 31 and mt % 2 == 1)),
                                   inc=("T" if (fo == 31 and mt == 7)
                                        else None))
                        w2_done(ex*64 + ic*32 + fo, b.cnt["T"])
                outT = b.cnt["T"]
                b.wait("vector", "T", outT)
                oeT = pers_t[:, 0:2048].rearrange("p (k i) -> p k i", k=8)
                for mt in range(8):
                    b.emit("vector", lambda e, mt=mt, oeT=oeT: e.tensor_copy(
                        out=oeT[:, mt, :],
                        in_=pb[mt // 2][:, (mt % 2)*256:(mt % 2)*256+256]),
                        inc="V")
                oeV = b.cnt["V"]
                b.wait("tensor", "V", oeV)
                oe = pers_t[:, 2048:4096].rearrange("p (im d8) -> p im d8",
                                                    im=2)
                transpose_tiles([(oeT[:, mt, im*128:(im+1)*128],
                                  oe[:, im, mt*128:(mt+1)*128])
                                 for mt in range(8) for im in range(2)])
                oeV2 = b.cnt["V"]
                for bi4 in range(4):
                    evac_after_bank[bi4] = ("V", oeV)
                b.wait("tensor", "V", oeV2)
                for dk in range(8):
                    for ncx in range(4):
                        bi = pbi[0] % POOLN[0]
                        pbi[0] += 1
                        bank = pb[bi]
                        prev = evac_after_bank.get(bi)
                        if prev is not None:
                            b.wait("tensor", prev[0], prev[1])
                        for im in range(2):
                            b.emit("tensor", lambda e, dk=dk, ncx=ncx, im=im,
                                   ic=ic, p=bank, oe=oe: e.matmul(
                                       p[:, 0:512],
                                       oe[:, im, dk*128:(dk+1)*128],
                                       P_g[:, ic*2+im, ncx*512:(ncx+1)*512],
                                       start=(im == 0), stop=(im == 1)),
                                   inc=("T" if im == 1 else None))
                        tT = b.cnt["T"]
                        b.wait("vector", "T", tT)
                        if ex == 0 and ic == 0:
                            tgt = b.emit("vector", lambda e, dk=dk, ncx=ncx,
                                         p=bank: e.tensor_copy(
                                             out=g2T[:, dk,
                                                     ncx*512:(ncx+1)*512],
                                             in_=p[:, 0:512]), inc="V")
                        else:
                            tgt = b.emit("vector", lambda e, dk=dk, ncx=ncx,
                                         p=bank: e.tensor_add(
                                             out=g2T[:, dk,
                                                     ncx*512:(ncx+1)*512],
                                             in0=g2T[:, dk,
                                                     ncx*512:(ncx+1)*512],
                                             in1=p[:, 0:512]), inc="V")
                        evac_after_bank[bi] = ("V", tgt)
        b.wait("scalar", "V", b.cnt["V"])
        for dk in range(8):
            b.emit("scalar", lambda e, dk=dk: e.activation(
                out=g2T[:, dk, :], in_=g2T[:, dk, :], func=AF.Silu), inc="S")
        g2S = b.cnt["S"]


        xnT = xn.rearrange("p tt d -> p (tt d)").rearrange(
            "p (k t) -> p k t", k=8)
        QK = wbuf_t[:, 0:2048]
        Kb = wbuf_t[:, 2048:4096]
        Vp2 = wbuf_t[:, 6144:7184].rearrange("p (kt w) -> p kt w", kt=16)
        wtp_buf = wbuf_t[:, 8208:8720].rearrange("p (k m) -> p k m", k=8)
        stage = wbuf_t[:, 7184:8208]
        at4 = pers_t[:, 8192:8704].rearrange("p (s t) -> p s t", s=4)
        cmask2 = wbuf_t[:, 8720:8848]
        ctxh = P_g.rearrange("p i t -> p (i t)").rearrange(
            "p (tt d) -> p tt d", tt=8)
        qkv_ld, qkv_done = w2_ld, w2_done

        b.wait("vector", "S", g2S)
        b.wait("vector", "WD", b.cnt["WD"])
        for tt in range(NTT):
            b.wait("vector", "T", b.cnt["T"])
            ln_apply(mcol, rstd, tt, stage)
            b.wait("tensor", "V", b.cnt["V"])
            transpose_tiles([(stage[:, dc*128:(dc+1)*128],
                              xnT[:, dc, tt*128:(tt+1)*128])
                             for dc in range(8)])
        xnTV = b.cnt["V"]
        b.emit("vector", lambda e: e.tensor_add(out=cmask2, in0=ident,
                                                in1=ustrict), inc="V")
        b.emit("vector", lambda e: e.memset(
            Vp2[:, :, 64:65].rearrange("p a c -> p (a c)"), 1.0), inc="V")
        woT = pers_t[:, 0:8192].rearrange("p (k m) -> p k m", k=8)
        for rb in range(8):
            wob, t3 = w2_ld(512 + rb, W["l2_wout"][rb*128:(rb+1)*128, :])
            b.wait("tensor", t3[0], t3[1])
            transpose_tiles([(wob[:, dc*128:(dc+1)*128],
                              woT[:, dc, rb*128:(rb+1)*128])
                             for dc in range(8)])
            w2_done(512 + rb, b.cnt["T"])
        woTV = b.cnt["V"]
        b.wait("tensor", "V", max(xnTV, woTV))
        b.wait("vector", "D0", b.cnt["D0"])
        b.wait("vector", "D1", b.cnt["D1"])
        b.wait("vector", "D2", b.cnt["D2"])
        b.wait("vector", "D3", b.cnt["D3"])

        sidx = [1000]

        def proj_rows(dram_rowblk, dst_row, src0, ncols):
            wb2, t4 = qkv_ld(sidx[0], dram_rowblk, rows=64)
            sidx[0] += 1
            b.wait("tensor", t4[0], t4[1])
            transpose_tiles([(wb2[0:64, dc*128:(dc+1)*128],
                              wtp_buf[:, dc, 0:64]) for dc in range(8)],
                            idn=64)
            b.wait("tensor", "V", b.cnt["V"])
            dstb = QK if dst_row == 0 else Kb
            for ncx in range(ncols // 512):
                def evq2(e, p, ncx=ncx, dstb=dstb):
                    return e.tensor_copy(
                        out=dstb[0:64, ncx*512:(ncx+1)*512], in_=p)
                mm_group([(wtp_buf[:, kt, 0:64],
                           xnT[:, kt, src0+ncx*512:src0+(ncx+1)*512])
                          for kt in range(8)], 64, 512, evq2, "vector")
            qkv_done(sidx[0] - 1, b.cnt["T"])

        for hf in range(2):
            q0 = hf * 1024
            nkt = (hf + 1) * 8
            for h in range(16):
                proj_rows(W["l2_wqkv"][64*h:64*h+64, :], 0, q0, 1024)
                proj_rows(W["l2_wqkv"][1024+64*h:1024+64*h+64, :], 64,
                          0, nkt * 128)
                wb2, t4 = qkv_ld(sidx[0],
                                 W["l2_wqkv"][2048+64*h:2048+64*h+64, :],
                                 rows=64)
                sidx[0] += 1
                b.wait("tensor", t4[0], t4[1])
                transpose_tiles([(wb2[0:64, dc*128:(dc+1)*128],
                                  wtp_buf[:, dc, 0:64]) for dc in range(8)],
                                idn=64)
                b.wait("tensor", "V", b.cnt["V"])
                for kt in range(nkt):
                    def evv2(e, p, kt=kt):
                        return e.tensor_copy(out=Vp2[:, kt, 0:64], in_=p)
                    mm_group([(xnT[:, k2, kt*128:(kt+1)*128],
                               wtp_buf[:, k2, 0:64])
                              for k2 in range(8)], 128, 64, evv2, "vector")
                qkv_done(sidx[0] - 1, b.cnt["T"])
                projV = b.cnt["V"]
                b.wait("tensor", "V", projV)
                for qt8 in range(8):
                    qt = hf * 8 + qt8
                    bi = pbi[0] % POOLN[0]
                    pbi[0] += 1
                    cbank = pb[bi]
                    prev = evac_after_bank.get(bi)
                    if prev is not None:
                        b.wait("tensor", prev[0], prev[1])
                    ready = {}
                    for kt in range(qt + 3):
                        if kt <= qt:
                            sl4 = kt % 4
                            bi2 = pbi[0] % POOLN[0]
                            pbi[0] += 1
                            if bi2 == bi:
                                bi2 = (bi2 + 1) % POOLN[0]
                                pbi[0] += 1
                            sbank = pb[bi2]
                            prev2 = evac_after_bank.get(bi2)
                            if prev2 is not None:
                                b.wait("tensor", prev2[0], prev2[1])
                            b.emit("tensor", lambda e, kt=kt, qt8=qt8,
                                   p=sbank: e.matmul(
                                       p[:, 0:128],
                                       Kb[0:64, kt*128:(kt+1)*128],
                                       QK[0:64, qt8*128:(qt8+1)*128],
                                       start=True, stop=True), inc="T")
                            tTs = b.cnt["T"]
                            b.wait("scalar", "T", tTs)
                            stgt = b.emit("scalar", lambda e, sl4=sl4,
                                          p=sbank: e.activation(
                                              out=at4[:, sl4, :],
                                              in_=p[:, 0:128],
                                              func=AF.Exp, scale=0.125),
                                          inc="S")
                            evac_after_bank[bi2] = ("S", stgt)
                            if kt == qt:
                                b.wait("vector", "S", stgt)
                                stgt = ("V", b.emit(
                                    "vector", lambda e, sl4=sl4:
                                    e.tensor_mul(out=at4[:, sl4, :],
                                                 in0=at4[:, sl4, :],
                                                 in1=cmask2), inc="V"))
                            else:
                                stgt = ("S", stgt)
                            ready[kt] = stgt
                        kav = kt - 2
                        if 0 <= kav <= qt:
                            b.wait("tensor", ready[kav][0], ready[kav][1])
                            b.emit("tensor", lambda e, sl4=kav % 4, kav=kav,
                                   qt=qt, p=cbank: e.matmul(
                                       p[:, 0:65], at4[:, sl4, :],
                                       Vp2[:, kav, :],
                                       start=(kav == 0), stop=(kav == qt)),
                                   inc=("T" if kav == qt else None))
                    tTc = b.cnt["T"]
                    b.wait("vector", "T", tTc)
                    b.emit("vector", lambda e, p=cbank: e.reciprocal(
                        out=sums2[:, 0:1], in_=p[:, 64:65]), inc="V")
                    tgt = b.emit("vector", lambda e, p=cbank, qt8=qt8, h=h:
                                 e.tensor_scalar(
                                     out=ctxh[:, qt8, 64*h:64*h+64],
                                     in0=p[:, 0:64], scalar1=sums2[:, 0:1],
                                     scalar2=None, op0=ALU.mult), inc="V")
                    evac_after_bank[bi] = ("V", tgt)
            ctxV = b.cnt["V"]
            b.wait("tensor", "V", ctxV)
            for tt8 in range(8):
                tt = hf * 8 + tt8
                transpose_tiles([(ctxh[:, tt8, dc*128:(dc+1)*128],
                                  stage[:, dc*128:(dc+1)*128])
                                 for dc in range(8)])
                b.wait("tensor", "V", b.cnt["V"])
                stgT = stage.rearrange("p (k q) -> p k q", k=8)
                for mt in range(8):
                    bi = pbi[0] % POOLN[0]
                    pbi[0] += 1
                    bank = pb[bi]
                    prev = evac_after_bank.get(bi)
                    if prev is not None:
                        b.wait("tensor", prev[0], prev[1])
                    for kt in range(8):
                        b.emit("tensor", lambda e, kt=kt, mt=mt, p=bank:
                               e.matmul(p[:, 0:128],
                                        woT[:, kt, mt*128:(mt+1)*128],
                                        stgT[:, kt, :],
                                        start=(kt == 0), stop=(kt == 7)),
                               inc=("T" if kt == 7 else None))
                    tT = b.cnt["T"]
                    b.wait("vector", "T", tT)
                    b.emit("vector", lambda e, p=bank, mt=mt, tt=tt:
                           e.tensor_mul(out=QK[:, 0:128], in0=p[:, 0:128],
                                        in1=g2T[:, mt, tt*128:(tt+1)*128]),
                           inc="V")
                    ptgt = b.cnt["V"]
                    evac_after_bank[bi] = ("V", ptgt)
                    b.wait("tensor", "V", ptgt)
                    sslot = tp_state["slot"] % 2
                    tp_state["slot"] += 1
                    if len(tp_state["evacs"]) >= 2:
                        b.wait("tensor", "V", tp_state["evacs"][-2])
                    tT2 = b.emit("tensor", lambda e, sslot=sslot: e.transpose(
                        ptp2[sslot][:, 0:128], QK[:, 0:128], ident), inc="T")
                    b.wait("vector", "T", tT2)
                    vv = b.emit("vector", lambda e, sslot=sslot, tt=tt, mt=mt:
                                e.tensor_add(
                                    out=xres[:, tt, mt*128:(mt+1)*128],
                                    in0=xres[:, tt, mt*128:(mt+1)*128],
                                    in1=ptp2[sslot][:, 0:128]), inc="V")
                    tp_state["evacs"].append(vv)
        b.barrier()


    def layer3():
        POOLN[0] = 5
        AR.reset(); STA.reset(); WB.reset()
        stream_sem_pool[:] = ["D0", "D1", "D2", "D3", "D4", "D5", "D6", "D7"]

        bq2 = STA.alloc(8)
        bkc = STA.alloc(8)
        boc = STA.alloc(8)
        murc = STA.alloc(8)
        mukc = STA.alloc(8)
        om_r = STA.alloc(8)
        om_k = STA.alloc(8)
        sd2 = [STA.alloc(512), STA.alloc(512)]
        rc512 = STA.alloc(4)

        xnT = AR.alloc(8 * 2048, "p (k t) -> p k t", k=8)
        ctx = AR.alloc(16 * 1024, "p (tt d) -> p tt d", tt=16)
        Ka = AR.alloc(2048)
        Qa = AR.alloc(2048)
        ksq_at = AR.alloc(2048)
        Vp3 = AR.alloc(16 * 65, "p (kt w) -> p kt w", kt=16)
        stage = Ka[:, 0:1024]

        wrow_buf = WB.alloc(2 * 1024, "p (b2 m) -> p b2 m", b2=2)
        wtp3 = WB.alloc(8 * 128, "p (k m) -> p k m", k=8)
        bvrow = WB.alloc(1024)
        chA = WB.alloc(8 * 128, "p (k t) -> p k t", k=8)
        chK = WB.alloc(8 * 128, "p (k t) -> p k t", k=8)
        chR = WB.alloc(8 * 128, "p (k t) -> p k t", k=8)
        g3b = WB.alloc(128)
        s3s = WB.alloc(8 * 128, "p (k t) -> p k t", k=8)
        ctxT_s = WB.alloc(8 * 128, "p (k t) -> p k t", k=8)
        wrow_ld, wrow_done = make_stream(wrow_buf)
        woT = pers_t[:, 0:8192].rearrange("p (k m) -> p k m", k=8)

        dma("gpsimd", bq2, W["l3_bq"].rearrange("(k p) -> p k", p=128),
            slow_ok=True)
        dma("gpsimd", bkc, W["l3_bk"].rearrange("(k p) -> p k", p=128),
            slow_ok=True)
        dma("gpsimd", boc, W["l3_bo"].rearrange("(k p) -> p k", p=128),
            slow_ok=True)
        dma("gpsimd", murc, W["l3_mur"].rearrange("(k p) -> p k", p=128),
            slow_ok=True)
        dma("gpsimd", mukc, W["l3_muk"].rearrange("(k p) -> p k", p=128),
            slow_ok=True)
        dma("gpsimd", bvrow[0:1, :],
            W["l3_bv"].rearrange("(q d) -> q d", q=1))
        wWD = b.cnt["WD"]
        b.wait("vector", "WD", wWD)
        b.emit("vector", lambda e: e.tensor_scalar(
            out=om_r, in0=murc, scalar1=-1.0, scalar2=1.0, op0=ALU.mult,
            op1=ALU.add))
        b.emit("vector", lambda e: e.tensor_scalar(
            out=om_k, in0=mukc, scalar1=-1.0, scalar2=1.0, op0=ALU.mult,
            op1=ALU.add), inc="V")

        mcol, rstd = layernorm_stats(ctx[:, 0, :])
        for tt in range(NTT):
            b.wait("vector", "T", b.cnt["T"])
            ln_apply(mcol, rstd, tt, stage)
            b.wait("tensor", "V", b.cnt["V"])
            transpose_tiles([(stage[:, dc*128:(dc+1)*128],
                              xnT[:, dc, tt*128:(tt+1)*128])
                             for dc in range(8)])
        xnTV = b.cnt["V"]

        b.emit("vector", lambda e: e.memset(
            Vp3[:, :, 64:65].rearrange("p a c -> p (a c)"), 1.0), inc="V")
        b.wait("tensor", "V", b.cnt["V"])

        sidx3 = [0]

        def load_wT(dram_rowblk, dst, rows=128):
            wb2, t4 = wrow_ld(sidx3[0], dram_rowblk, rows=rows)
            sidx3[0] += 1
            b.wait("tensor", t4[0], t4[1])
            transpose_tiles([(wb2[0:rows, dc*128:(dc+1)*128],
                              dst[:, dc, 0:rows]) for dc in range(8)],
                            idn=rows)
            b.wait("tensor", "V", b.cnt["V"])
            return b.cnt["V"]

        for h in range(16):
            r0 = (h % 2) * 64
            dk = h // 2
            load_wT(W["l3_wq"][64*h:64*h+64, :], wtp3, rows=64)
            for ncx in range(4):
                def evq3(e, p, ncx=ncx, r0=r0, dk=dk):
                    return e.tensor_scalar(
                        out=Qa[0:64, ncx*512:(ncx+1)*512], in0=p,
                        scalar1=bq2[r0:r0+64, dk:dk+1], scalar2=-2.0,
                        op0=ALU.add, op1=ALU.mult)
                mm_group([(wtp3[:, kt, 0:64],
                           xnT[:, kt, ncx*512:(ncx+1)*512])
                          for kt in range(8)], 64, 512, evq3, "vector")
            wrow_done(sidx3[0] - 1, b.cnt["T"])
            qS = b.cnt["V"]
            b.wait("scalar", "V", qS)
            b.wait("scalar", "T", b.cnt["T"])
            b.emit("scalar", lambda e: e.activation(
                out=ksq_at[0:64, :], in_=Qa[0:64, :], func=AF.Square),
                inc="S")
            b.wait("tensor", "S", b.cnt["S"])
            for ncx in range(4):
                bi = pbi[0] % POOLN[0]
                pbi[0] += 1
                bank = pb[bi]
                prev = evac_after_bank.get(bi)
                if prev is not None:
                    b.wait("tensor", prev[0], prev[1])
                b.emit("tensor", lambda e, ncx=ncx, p=bank: e.matmul(
                    p[0:2, 0:512], col10[0:64, :],
                    ksq_at[0:64, ncx*512:(ncx+1)*512],
                    start=True, stop=True), inc="T")
                b.wait("vector", "T", b.cnt["T"])
                tgt = b.emit("vector", lambda e, ncx=ncx, p=bank:
                             e.tensor_scalar(
                                 out=Qa[64:66, ncx*512:(ncx+1)*512],
                                 in0=p[0:2, 0:512],
                                 scalar1=cst[64:66, 521:522],
                                 scalar2=cst[64:66, 522:523],
                                 op0=ALU.mult, op1=ALU.add), inc="V")
                evac_after_bank[bi] = ("V", tgt)
            load_wT(W["l3_wk"][64*h:64*h+64, :], wtp3, rows=64)
            for ncx in range(4):
                def evk3(e, p, ncx=ncx, r0=r0, dk=dk):
                    return e.tensor_scalar(
                        out=Ka[0:64, ncx*512:(ncx+1)*512], in0=p,
                        scalar1=bkc[r0:r0+64, dk:dk+1], scalar2=None,
                        op0=ALU.add)
                mm_group([(wtp3[:, kt, 0:64],
                           xnT[:, kt, ncx*512:(ncx+1)*512])
                          for kt in range(8)], 64, 512, evk3, "vector")
            wrow_done(sidx3[0] - 1, b.cnt["T"])
            kS = b.cnt["V"]
            b.wait("scalar", "V", kS)
            b.wait("scalar", "T", b.cnt["T"])
            b.emit("scalar", lambda e: e.activation(
                out=ksq_at[0:64, :], in_=Ka[0:64, :], func=AF.Square),
                inc="S")
            b.wait("tensor", "S", b.cnt["S"])
            for ncx in range(4):
                bi = pbi[0] % POOLN[0]
                pbi[0] += 1
                bank = pb[bi]
                prev = evac_after_bank.get(bi)
                if prev is not None:
                    b.wait("tensor", prev[0], prev[1])
                b.emit("tensor", lambda e, ncx=ncx, p=bank: e.matmul(
                    p[0:2, 0:512], col01[0:64, :],
                    ksq_at[0:64, ncx*512:(ncx+1)*512],
                    start=True, stop=True), inc="T")
                b.wait("vector", "T", b.cnt["T"])
                tgt = b.emit("vector", lambda e, ncx=ncx, p=bank:
                             e.tensor_scalar(
                                 out=Ka[64:66, ncx*512:(ncx+1)*512],
                                 in0=p[0:2, 0:512],
                                 scalar1=cst[64:66, 523:524],
                                 scalar2=cst[64:66, 524:525],
                                 op0=ALU.mult, op1=ALU.add), inc="V")
                evac_after_bank[bi] = ("V", tgt)
            load_wT(W["l3_wv"][64*h:64*h+64, :], wtp3, rows=64)
            for kt in range(16):
                def evv3(e, p, kt=kt):
                    return e.tensor_copy(out=Vp3[:, kt, 0:64], in_=p)
                kparts = [(xnT[:, k2, kt*128:(kt+1)*128], wtp3[:, k2, 0:64])
                          for k2 in range(8)]
                kparts.append((ones_row[0:1, 0:128],
                               bvrow[0:1, 64*h:64*h+64]))
                mm_group(kparts, 128, 64, evv3, "vector")
            wrow_done(sidx3[0] - 1, b.cnt["T"])
            vV3 = b.cnt["V"]
            b.wait("tensor", "V", vV3)
            b.wait("scalar", "S", b.cnt["S"])
            for qc in range(4):
                bi = pbi[0] % POOLN[0]
                pbi[0] += 1
                cbank = pb[bi]
                prev = evac_after_bank.get(bi)
                if prev is not None:
                    b.wait("tensor", prev[0], prev[1])
                ready = {}
                for kt in range(18):
                    if kt <= 15:
                        par = kt % 4
                        bi2 = pbi[0] % POOLN[0]
                        pbi[0] += 1
                        if bi2 == bi:
                            bi2 = (bi2 + 1) % POOLN[0]
                            pbi[0] += 1
                        sbank = pb[bi2]
                        prev2 = evac_after_bank.get(bi2)
                        if prev2 is not None:
                            b.wait("tensor", prev2[0], prev2[1])
                        b.emit("tensor", lambda e, kt=kt, qc=qc, p=sbank:
                               e.matmul(p[:, 0:512],
                                        Ka[0:66, kt*128:(kt+1)*128],
                                        Qa[0:66, qc*512:(qc+1)*512],
                                        start=True, stop=True), inc="T")
                        b.wait("scalar", "T", b.cnt["T"])
                        b.emit("scalar", lambda e, p=sbank, par=par % 2:
                               e.activation(out=sd2[par], in_=p[:, 0:512],
                                            func=AF.Sqrt, bias=1e-12))
                        stgt = b.emit("scalar", lambda e, par=par:
                                      e.activation(
                                          out=ksq_at[:, par*512:par*512+512],
                                          in_=sd2[par % 2], func=AF.Exp,
                                          scale=-1.0), inc="S")
                        evac_after_bank[bi2] = ("S", stgt)
                        ready[kt] = stgt
                    kav = kt - 2
                    if 0 <= kav <= 15:
                        b.wait("tensor", "S", ready[kav])
                        for qm in range(4):
                            b.emit("tensor", lambda e, kav=kav, qm=qm,
                                   par=kav % 4, p=cbank: e.matmul(
                                       p[:, qm*65:qm*65+65],
                                       ksq_at[:, par*512+qm*128:
                                              par*512+(qm+1)*128],
                                       Vp3[:, kav, :],
                                       start=(kav == 0 and qm == 0),
                                       stop=(kav == 15 and qm == 3)),
                                   inc=("T" if (kav == 15 and qm == 3)
                                        else None))
                tTc = b.cnt["T"]
                b.wait("vector", "T", tTc)
                b.emit("vector", lambda e, p=cbank: e.reciprocal(
                    out=rc512, in_=p[:, 64:260:65]), inc="V")
                for qm in range(4):
                    tgt = b.emit("vector", lambda e, p=cbank, qm=qm, qc=qc,
                                 h=h: e.tensor_scalar(
                                     out=ctx[:, qc*4+qm, 64*h:64*h+64],
                                     in0=p[:, qm*65:qm*65+64],
                                     scalar1=rc512[:, qm:qm+1], scalar2=None,
                                     op0=ALU.mult), inc="V")
                evac_after_bank[bi] = ("V", tgt)
        attV = b.cnt["V"]

        for rb in range(8):
            load_wT(W["l3_wo"][rb*128:(rb+1)*128, :],
                    woT.rearrange("p k m -> p k m")[:, :, 0:128]
                    if False else woT, rows=128) if False else None
        for rb in range(8):
            wb2, t4 = wrow_ld(sidx3[0], W["l3_wo"][rb*128:(rb+1)*128, :])
            sidx3[0] += 1
            b.wait("tensor", t4[0], t4[1])
            transpose_tiles([(wb2[:, dc*128:(dc+1)*128],
                              woT[:, dc, rb*128:(rb+1)*128])
                             for dc in range(8)])
            wrow_done(sidx3[0] - 1, b.cnt["T"])
        woTV3 = b.cnt["V"]
        b.wait("tensor", "V", max(attV, woTV3))

        for tt in range(NTT):
            transpose_tiles([(ctx[:, tt, dc*128:(dc+1)*128],
                              ctxT_s[:, dc, :]) for dc in range(8)])
            b.wait("tensor", "V", b.cnt["V"])
            for mt in range(8):
                def evs3(e, p, mt=mt):
                    return e.tensor_scalar(out=s3s[:, mt, :], in0=p,
                                           scalar1=boc[:, mt:mt+1],
                                           scalar2=None, op0=ALU.add)
                mm_group([(woT[:, kt, mt*128:(mt+1)*128], ctxT_s[:, kt, :])
                          for kt in range(8)], 128, 128, evs3, "vector")
            for dkx in range(8):
                if tt == 0:
                    b.emit("vector", lambda e, dkx=dkx: e.tensor_scalar(
                        out=chA[:, dkx, 1:128],
                        in0=xnT[:, dkx, 0:127], scalar1=om_k[:, dkx:dkx+1],
                        scalar2=None, op0=ALU.mult))
                    b.emit("vector", lambda e, dkx=dkx: e.memset(
                        chA[:, dkx, 0:1], 0.0))
                else:
                    b.emit("vector", lambda e, dkx=dkx, tt=tt: e.tensor_scalar(
                        out=chA[:, dkx, 0:128],
                        in0=xnT[:, dkx, tt*128-1:tt*128+127],
                        scalar1=om_k[:, dkx:dkx+1], scalar2=None,
                        op0=ALU.mult))
                b.emit("vector", lambda e, dkx=dkx, tt=tt:
                       e.scalar_tensor_tensor(
                           out=chA[:, dkx, 0:128],
                           in0=xnT[:, dkx, tt*128:(tt+1)*128],
                           scalar=mukc[:, dkx:dkx+1],
                           in1=chA[:, dkx, 0:128],
                           op0=ALU.mult, op1=ALU.add), inc="V")
            xkV = b.cnt["V"]
            b.wait("tensor", "V", xkV)
            for mt in range(8):
                wkb, t7 = wrow_ld(sidx3[0], W["l3_wk"][mt*128:(mt+1)*128, :])
                sidx3[0] += 1
                b.wait("tensor", t7[0], t7[1])
                transpose_tiles([(wkb[:, dc*128:(dc+1)*128],
                                  wtp3[:, dc, :]) for dc in range(8)])
                b.wait("tensor", "V", b.cnt["V"])
                def evkk(e, p, mt=mt):
                    return e.activation(out=chK[:, mt, :], in_=p,
                                        func=AF.Relu)
                mm_group([(wtp3[:, kt, :], chA[:, kt, 0:128])
                          for kt in range(8)], 128, 128, evkk, "scalar")
                wrow_done(sidx3[0] - 1, b.cnt["T"])
            b.wait("vector", "S", b.cnt["S"])
            b.emit("vector", lambda e: e.tensor_mul(
                out=chK[:, :, :], in0=chK[:, :, :], in1=chK[:, :, :]),
                inc="V")
            kaV = b.cnt["V"]
            for dkx in range(8):
                if tt == 0:
                    b.emit("vector", lambda e, dkx=dkx: e.tensor_scalar(
                        out=chA[:, dkx, 1:128],
                        in0=xnT[:, dkx, 0:127], scalar1=om_r[:, dkx:dkx+1],
                        scalar2=None, op0=ALU.mult))
                    b.emit("vector", lambda e, dkx=dkx: e.memset(
                        chA[:, dkx, 0:1], 0.0))
                else:
                    b.emit("vector", lambda e, dkx=dkx, tt=tt: e.tensor_scalar(
                        out=chA[:, dkx, 0:128],
                        in0=xnT[:, dkx, tt*128-1:tt*128+127],
                        scalar1=om_r[:, dkx:dkx+1], scalar2=None,
                        op0=ALU.mult))
                b.emit("vector", lambda e, dkx=dkx, tt=tt:
                       e.scalar_tensor_tensor(
                           out=chA[:, dkx, 0:128],
                           in0=xnT[:, dkx, tt*128:(tt+1)*128],
                           scalar=murc[:, dkx:dkx+1],
                           in1=chA[:, dkx, 0:128],
                           op0=ALU.mult, op1=ALU.add), inc="V")
            xrV = b.cnt["V"]
            b.wait("tensor", "V", xrV)
            for mt in range(8):
                wrb, t8 = wrow_ld(sidx3[0], W["l3_wr"][mt*128:(mt+1)*128, :])
                sidx3[0] += 1
                b.wait("tensor", t8[0], t8[1])
                transpose_tiles([(wrb[:, dc*128:(dc+1)*128],
                                  wtp3[:, dc, :]) for dc in range(8)])
                b.wait("tensor", "V", b.cnt["V"])
                def evrr(e, p, mt=mt):
                    return e.activation(out=chR[:, mt, :], in_=p,
                                        func=AF.Sigmoid)
                mm_group([(wtp3[:, kt, :], chA[:, kt, 0:128])
                          for kt in range(8)], 128, 128, evrr, "scalar")
                wrow_done(sidx3[0] - 1, b.cnt["T"])
            rS = b.cnt["S"]
            b.wait("tensor", "S", rS)
            b.wait("tensor", "V", kaV)
            for mt in range(8):
                wvb, t9 = wrow_ld(sidx3[0], W["l3_wv"][mt*128:(mt+1)*128, :])
                sidx3[0] += 1
                b.wait("tensor", t9[0], t9[1])
                transpose_tiles([(wvb[:, dc*128:(dc+1)*128],
                                  wtp3[:, dc, :]) for dc in range(8)])
                b.wait("tensor", "V", b.cnt["V"])
                bi = pbi[0] % POOLN[0]
                pbi[0] += 1
                bank = pb[bi]
                prev = evac_after_bank.get(bi)
                if prev is not None:
                    b.wait("tensor", prev[0], prev[1])
                for kt in range(8):
                    b.emit("tensor", lambda e, kt=kt, p=bank: e.matmul(
                        p[:, 0:128], wtp3[:, kt, :], chK[:, kt, :],
                        start=(kt == 0), stop=(kt == 7)),
                        inc=("T" if kt == 7 else None))
                wrow_done(sidx3[0] - 1, b.cnt["T"])
                tTk = b.cnt["T"]
                b.wait("vector", "T", tTk)
                b.emit("vector", lambda e, p=bank, mt=mt: e.tensor_mul(
                    out=g3b, in0=p[:, 0:128], in1=chR[:, mt, :]), inc="V")
                gv = b.cnt["V"]
                evac_after_bank[bi] = ("V", gv)
                b.wait("scalar", "V", gv)
                b.emit("scalar", lambda e: e.activation(
                    out=g3b, in_=g3b, func=AF.Silu), inc="S")
                gs = b.cnt["S"]
                b.wait("vector", "S", gs)
                b.emit("vector", lambda e, mt=mt: e.tensor_mul(
                    out=g3b, in0=g3b, in1=s3s[:, mt, :]), inc="V")
                pv = b.cnt["V"]
                b.wait("tensor", "V", pv)
                sslot = tp_state["slot"] % 2
                tp_state["slot"] += 1
                if len(tp_state["evacs"]) >= 2:
                    b.wait("tensor", "V", tp_state["evacs"][-2])
                tT2 = b.emit("tensor", lambda e, sslot=sslot: e.transpose(
                    ptp2[sslot][:, 0:128], g3b, ident), inc="T")
                b.wait("vector", "T", tT2)
                vv = b.emit("vector", lambda e, sslot=sslot, tt=tt, mt=mt:
                            e.tensor_add(
                                out=xres[:, tt, mt*128:(mt+1)*128],
                                in0=xres[:, tt, mt*128:(mt+1)*128],
                                in1=ptp2[sslot][:, 0:128]), inc="V")
                tp_state["evacs"].append(vv)
        b.barrier()

    if 0 in active:
        layer0()
    if 1 in active:
        layer1()
    if 2 in active:
        layer2()
    if 3 in active:
        layer3()

    b.barrier()
    stgt = dma("sync", y_out.rearrange("(tt p) d -> p tt d", p=128),
               xres[:, :, :], "ST")
    b.wait("sync", "ST", stgt)

    with nc.allow_low_precision(
            reason="bf16 moe accumulation within validated tolerance"), \
         nc.Block() as block:
        @block.sync
        def _(e):
            for f in b.prog["sync"]:
                f(e)

        @block.tensor
        def _(e):
            for f in b.prog["tensor"]:
                f(e)

        @block.vector
        def _(e):
            for f in b.prog["vector"]:
                f(e)

        @block.scalar
        def _(e):
            for f in b.prog["scalar"]:
                f(e)

        @block.gpsimd
        def _(e):
            for f in b.prog["gpsimd"]:
                f(e)

    es.close()
    return nc


def make_consts_i():
    return np.tile((1 << np.arange(32, dtype=np.int64)).astype(np.int32)[None, :],
                   (8, 1))


def make_consts():
    c = np.zeros((128, 640), np.float32)
    c[:, 0:512] = np.arange(512, dtype=np.float32)[None, :]
    c[:, 512] = np.arange(128, dtype=np.float32)
    c[:, 513] = 1.0
    c[:, 514] = 1e-5
    c[:, 515] = 0.0
    c[:, 516] = -1.0
    c[:, 517] = 1e-12
    c[:, 518] = 0.125
    c[:, 519] = -2.0
    c[:, 520] = -1e30
    c[64, 521] = 0.25
    c[65, 521] = 0.0
    c[64, 522] = 0.0
    c[65, 522] = 1.0
    c[64, 523] = 0.0
    c[65, 523] = 1.0
    c[64, 524] = 1.0
    c[65, 524] = 0.0
    return c


def kernel(**inputs):
    nc = build_program()
    consts = make_consts()
    x = np.ascontiguousarray(inputs["L0_attn_x"])
    in_maps = []
    for core in range(8):
        m = {"x": x[core], "consts": consts, "consts_i": make_consts_i()}
        for k, v in inputs.items():
            if k == "L0_attn_x":
                continue
            m[k] = np.ascontiguousarray(v)
        in_maps.append(m)
    res = run_bass_kernel_spmd(nc, in_maps, core_ids=list(range(8)))
    return np.stack([r["y"] for r in res.results], axis=0)
